# version 12
# baseline (speedup 1.0000x reference)
import numpy as np
from contextlib import ExitStack
import concourse.bass as bass
import concourse.mybir as mybir
from concourse.bass_utils import run_bass_kernel_spmd

F32 = mybir.dt.float32
BF16 = mybir.dt.bfloat16
I32 = mybir.dt.int32
AF = mybir.ActivationFunctionType
ALU = mybir.AluOpType
AX = mybir.AxisListType

T = 2048
D = 1024
SLAB = 512
NSLAB = T // SLAB
NDT = D // 128

ENGS = ("sync", "gpsimd", "scalar", "vector", "tensor")
DMA_Q = {"sync": "qsync", "gpsimd": "qgpsimd", "scalar": "qscalar"}
NLANES = 8

WARMTAB = {
    0:48,12:13,21:6,30:15,39:14,48:8,57:10,66:30,84:14,93:17,
    120:21,129:12,153:38,164:38,175:38,186:38,189:96,192:45,194:17,196:10,
    197:6,199:18,201:26,203:21,205:22,207:22,235:8,243:51,252:47,261:44,
    270:44,272:9,274:7,315:61,317:17,320:10,325:8,342:22,345:10,348:10,
    351:10,357:10,360:18,363:12,366:12,369:10,372:10,375:10,381:18,384:18,
    387:18,390:17,393:10,396:10,399:10,405:18,408:18,411:18,414:10,417:10,
    420:10,425:10,428:10,431:8,440:52,442:26,443:7,444:7,445:7,446:25,
    448:7,449:7,450:7,451:19,452:7,453:7,454:7,455:7,456:25,458:7,
    459:7,460:7,461:19,463:7,464:7,465:7,466:25,468:7,469:7,470:7,
    471:19,472:7,473:7,474:7,475:7,476:25,478:7,479:7,480:7,481:19,
    483:7,484:7,485:7,486:25,488:7,489:7,490:7,491:19,492:7,493:7,
    494:7,495:7,496:25,498:7,499:7,500:7,501:19,503:7,504:7,505:7,
    506:25,508:7,509:7,510:7,511:19,512:7,513:7,514:7,515:7,516:25,
    518:7,519:7,520:7,521:19,533:8,534:6,538:8,539:7,543:8,544:6,
    548:8,549:7,553:8,554:6,558:8,559:7,563:8,564:6,568:8,586:8,
    595:8,604:8,613:8,622:8,631:8,640:8,649:8,671:8,684:8,697:8,
    710:8,723:8,736:8,749:8,762:8,788:8,805:8,822:8,839:8,856:8,
    873:8,890:8,907:8,915:49,917:27,919:8,942:22,945:10,948:10,951:10,
    957:18,960:12,963:18,966:18,969:10,972:10,975:10,981:17,984:18,987:12,
    990:12,993:10,996:10,999:10,1005:10,1008:18,1011:18,1014:10,1017:10,1020:10,
    1025:10,1028:10,1031:8,1040:52,1042:21,1043:7,1044:7,1045:7,1046:25,1048:7,
    1049:7,1050:7,1051:19,1052:7,1053:7,1054:7,1055:7,1056:25,1058:7,1059:7,
    1060:8,1061:19,1063:7,1064:7,1065:7,1066:25,1068:7,1069:7,1070:7,1071:19,
    1072:7,1073:7,1074:7,1075:7,1076:25,1078:7,1079:7,1080:7,1081:19,1083:7,
    1084:7,1085:7,1086:25,1088:7,1089:7,1090:7,1091:19,1092:7,1093:7,1094:7,
    1095:7,1096:25,1098:7,1099:7,1100:7,1101:19,1103:7,1104:7,1105:7,1106:25,
    1108:7,1109:7,1110:7,1111:22,1113:7,1114:7,1115:7,1116:25,1118:7,1119:7,
    1120:7,1121:19,1133:8,1134:6,1138:8,1139:7,1143:8,1144:6,1148:8,1149:7,
    1153:8,1154:6,1158:8,1159:7,1163:8,1164:6,1168:8,1186:8,1195:8,1204:8,
    1213:8,1222:8,1231:8,1240:8,1249:8,1271:8,1284:8,1297:8,1310:8,1323:8,
    1336:8,1349:8,1362:8,1388:8,1405:8,1422:8,1439:8,1456:8,1473:8,1490:8,
    1507:8,1523:51,1532:47,1541:44,1550:45,1552:89,1562:14,1567:12,1574:97,1576:93,
    1578:85,1580:90,1582:87,1584:91,1586:86,1588:94,1590:87,1592:92,1594:88,1596:93,
    1598:87,1600:94,1602:88,1604:93,1608:36,1609:29,1610:23,1611:10,1612:9,1613:13,
    1614:21,1615:17,1723:10,1724:21,1725:29,1726:10,1839:10,1840:19,1841:23,1842:19,
    1955:10,1956:19,1957:19,1958:22,2071:16,2072:16,2073:18,2074:16,2187:11,2188:21,
    2189:18,2190:19,2303:24,2304:18,2305:18,2306:10,2419:14,2420:19,2421:18,2422:20,
    2533:48,2577:24,2595:10,2613:19,2622:10,2631:16,2640:12,2649:10,2658:9,2667:14,
    2676:6,2685:19,2694:9,2713:51,2724:38,2735:38,2746:38,2749:112,2752:250,2757:94,
    2816:25,2817:7,2818:7,2819:7,2820:25,2822:7,2823:7,2824:7,2825:22,2827:7,
    2828:7,2829:7,2830:25,2832:7,2833:7,2834:7,2835:22,2837:7,2838:7,2839:7,
    2840:25,2842:7,2843:7,2844:7,2845:22,2847:7,2848:7,2849:7,2850:25,2852:7,
    2853:7,2854:7,2855:22,2857:7,2858:7,2859:7,2860:25,2862:7,2863:7,2864:7,
    2865:22,2867:7,2868:7,2869:7,2870:25,2872:7,2873:7,2874:7,2875:22,2877:7,
    2878:7,2879:7,2880:25,2882:7,2883:7,2884:7,2885:22,2887:7,2888:7,2889:7,
    2890:25,2892:7,2893:7,2894:7,2895:22,2903:22,2904:8,2911:8,2920:8,2931:8,
    2942:8,2953:8,2964:8,2975:8,2986:8,3005:10,3014:8,3022:9,3028:14,3031:8,
    3050:8,3069:8,3088:8,3102:8,3107:15,3126:8,3145:8,3166:10,3181:8,3206:8,
    3216:11,3218:8,3220:6,3228:8,3233:8,3260:8,3270:11,3280:13,3287:8,3314:8,
    3341:8,3368:8,3389:6,3412:8,3426:27,3445:8,3463:6,3473:8,3480:8,3515:8,
    3550:8,3585:8,3620:8,3655:8,3665:130,3724:25,3725:7,3726:7,3727:7,3728:25,
    3730:7,3731:7,3732:7,3733:22,3735:7,3736:7,3737:7,3738:25,3740:7,3741:7,
    3742:7,3743:22,3745:7,3746:7,3747:7,3748:25,3750:7,3751:7,3752:7,3753:22,
    3755:8,3756:7,3757:7,3758:25,3760:7,3761:7,3762:7,3763:22,3765:7,3766:7,
    3767:7,3768:25,3770:7,3771:7,3772:7,3773:22,3775:7,3776:7,3777:7,3778:25,
    3780:7,3781:7,3782:7,3783:22,3785:7,3786:7,3787:7,3788:25,3790:7,3791:7,
    3792:7,3793:22,3795:7,3796:7,3797:7,3798:25,3800:7,3801:7,3802:7,3803:21,
    3811:22,3819:8,3828:8,3839:8,3850:8,3861:8,3872:8,3880:7,3883:8,3894:8,
    3913:7,3915:6,3922:8,3939:8,3949:8,3951:7,3958:8,3968:9,3970:6,3977:8,
    3987:8,3996:8,4015:8,4034:8,4044:9,4048:9,4053:8,4072:13,4074:12,4089:8,
    4101:8,4114:8,4141:8,4163:9,4168:15,4188:6,4195:8,4222:8,4242:7,4249:8,
    4273:6,4276:8,4295:15,4297:7,4320:8,4328:10,4330:7,4332:6,4353:8,4363:7,
    4367:6,4375:11,4379:7,4381:8,4388:8,4398:8,4400:8,4402:6,4406:8,4412:12,
    4423:8,4433:7,4435:7,4439:14,4447:12,4458:8,4488:10,4493:8,4528:8,4563:8,
    4573:94,4574:20,4575:19,4576:19,4577:20,4578:20,4579:24,4596:6,4652:6,4708:6,
    4764:6,4820:6,4876:6,4932:6,4988:6,5024:250,5041:38,5052:38,5063:38,5074:42,
    5077:113,5540:60,5541:181,5543:36,5545:35,5547:44,5549:37,5551:35,5553:35,5555:44,
    5592:33,5593:14,5594:7,5627:79,5635:67,5643:64,5651:64,5659:77,5667:62,5675:69,
    5683:66,5691:64,5699:81,5707:72,5715:76,5723:63,5731:73,5739:68,5747:69,5765:35,
    5766:13,5767:10,5768:19,5769:11,5770:18,5771:10,5772:20,5787:7,5789:7,5791:7,
    5801:44,5803:39,5817:7,5819:7,5821:7,5824:44,5825:41,5838:7,5840:7,5842:9,
    5845:44,5846:40,5859:7,5861:7,5863:7,5866:44,5867:39,5881:18,5896:7,5898:7,
    5900:7,5910:44,5912:43,5926:7,5928:7,5930:7,5933:44,5934:41,5947:7,5949:7,
    5951:7,5954:44,5955:42,5968:7,5970:7,5972:7,5975:44,5976:41,5990:20,6005:7,
    6007:7,6009:7,6019:44,6021:50,6035:7,6037:7,6039:7,6042:44,6043:44,6056:7,
    6058:7,6060:7,6063:44,6064:42,6084:44,6085:40,6098:11,6099:21,6114:7,6116:7,
    6118:7,6128:44,6130:42,6144:7,6146:7,6148:7,6151:44,6152:41,6165:7,6167:7,
    6169:7,6172:44,6173:44,6190:6,6193:44,6194:42,6207:10,6208:13,6227:7,6237:44,
    6239:46,6260:43,6261:43,6281:38,6282:41,6302:43,6303:44,6316:14,6317:9,6346:36,
    6348:47,6366:7,6369:41,6370:43,6385:7,6387:7,6390:44,6391:42,6411:41,6412:45,
    6455:36,6457:45,6478:38,6479:47,6499:36,6500:44,6520:36,6521:40,6535:10,6564:38,
    6566:48,6587:36,6588:41,6608:36,6609:40,6629:37,6630:42,6648:37,6649:12,6650:18,
    6651:34,6652:19,6653:13,6654:23,6655:19,6656:49,6668:12,6672:49,6683:37,6694:37,
    6705:38,6708:114,6718:37,6729:105,6740:85,6751:74,6754:207,6764:119,6765:14,6766:9,
    6767:9,6768:14,6769:15,6770:18,6771:9,6839:250,6841:22,6866:35,6868:28,6870:9,
    6871:8,6872:29,6874:18,6877:6,6880:9,6885:10,6888:26,6898:34,6908:32,6918:33,
    6925:33,6928:30,6936:9,6939:22,6946:10,6949:26,6957:11,6960:28,6967:6,6970:26,
    6976:8,6978:9,6981:21,6991:32,6997:12,6999:18,7002:22,7012:26,7019:18,7022:28,
    7029:10,7032:26,7052:15,7053:38,7056:22,7061:17,7064:33,7074:34,7081:9,7084:26,
    7091:10,7094:27,7101:25,7104:27,7110:10,7112:9,7115:21,7125:33,7133:19,7136:22,
    7146:34,7154:11,7157:28,7164:10,7167:26,7173:9,7175:9,7178:21,7188:34,7195:12,
    7198:31,7208:34,7228:17,7229:42,7232:21,7237:22,7240:30,7250:34,7260:35,7267:8,
    7270:26,7277:33,7280:26,7288:9,7291:21,7298:10,7301:28,7309:9,7312:28,7319:10,
    7322:26,7328:13,7330:20,7333:21,7343:34,7349:17,7351:9,7354:21,7361:10,7364:28,
    7371:10,7374:26,7384:33,7404:17,7405:43,7408:21,7413:10,7416:26,7423:10,7426:28,
    7436:34,7446:34,7453:27,7456:28,7462:7,7464:9,7467:21,7477:33,7483:11,7485:9,
    7488:21,7498:32,7506:20,7509:28,7519:34,7525:10,7527:9,7530:22,7540:34,7547:18,
    7550:27,7557:10,7560:26,7580:17,7581:43,7584:22,7589:18,7592:26,7602:34,7612:34,
    7622:34,7629:33,7632:26,7640:9,7643:22,7650:10,7653:26,7661:15,7664:28,7671:7,
    7674:26,7680:11,7682:10,7685:22,7695:34,7701:14,7703:11,7706:22,7713:10,7716:26,
    7723:10,7726:26,7733:10,7736:28,7756:17,7757:43,7760:21,7765:10,7768:26,7775:10,
    7778:27,7788:34,7798:34,7805:26,7808:26,7816:9,7819:28,7826:8,7829:28,7835:11,
    7837:9,7840:21,7850:34,7856:6,7858:10,7861:21,7871:34,7877:10,7879:9,7882:21,
    7892:33,7899:18,7902:27,7909:10,7912:26,7932:17,7933:41,7936:22,7941:17,7944:26,
    7954:34,7964:34,7974:34,7981:33,7984:26,7992:9,7995:22,8002:10,8005:26,8013:15,
    8016:26,8026:26,8032:9,8034:10,8037:22,8047:35,8053:13,8055:20,8058:21,8068:26,
    8075:18,8078:26,8085:10,8088:26,8108:17,8109:46,8112:27,8117:10,8120:26,8127:10,
    8130:26,8140:26,8150:34,8157:26,8160:26,8166:13,8168:19,8171:21,8181:33,8187:9,
    8189:9,8192:21,8202:34,8210:20,8213:22,8220:10,8223:30,8229:9,8231:9,8234:21,
    8244:33,8251:18,8254:26,8264:33,8284:17,8285:43,8288:21,8293:10,8296:26,8303:10,
    8306:30,8316:34,8326:34,8336:29,8342:19,8344:9,8347:21,8354:10,8357:30,8365:9,
    8368:22,8375:10,8378:26,8386:15,8389:28,8396:7,8399:26,8405:19,8407:9,8410:21,
    8417:10,8420:30,8427:15,8430:34,8440:34,8460:15,8461:40,8464:22,8469:17,8472:26,
    8482:33,8492:34,8502:34,8509:33,8512:26,8520:9,8523:22,8530:10,8533:26,8541:10,
    8544:28,8551:6,8554:26,8560:11,8562:11,8565:26,8575:33,8581:15,8583:9,8586:22,
    8593:10,8596:26,8603:10,8606:27,8613:10,8616:29,8636:17,8637:43,8640:21,8645:10,
    8648:26,8655:10,8658:27,8668:34,8678:34,8685:26,8688:27,8696:9,8699:28,8706:9,
    8709:26,8715:13,8717:19,8720:21,8730:34,8736:8,8738:11,8741:28,8751:34,8757:12,
    8759:19,8762:21,8772:34,8779:18,8782:26,8789:10,8792:26,8812:17,8813:43,8816:22,
    8821:10,8824:27,8834:34,8844:35,8851:8,8854:26,8861:33,8864:26,8872:9,8875:21,
    8882:10,8885:29,8893:9,8896:28,8903:10,8906:26,8914:17,8917:28,8927:27,8933:18,
    8935:9,8938:22,8945:10,8948:29,8955:10,8958:26,8968:33,8988:15,8989:41,8992:24,
    8997:10,9000:26,9007:10,9010:26,9020:34,9030:34,9037:26,9040:26,9048:9,9051:28,
    9058:7,9061:26,9067:13,9069:17,9072:21,9082:32,9088:7,9090:11,9093:28,9103:34,
    9109:10,9111:9,9114:21,9124:34,9131:18,9134:27,9141:10,9144:26,9164:15,9165:39,
    9168:21,9173:18,9176:26,9186:34,9196:34,9206:26,9213:33,9216:26,9224:9,9227:21,
    9234:10,9237:26,9245:9,9248:26,9258:26,9264:13,9266:19,9269:22,9279:34,9285:15,
    9287:11,9290:21,9297:10,9300:26,9307:10,9310:26,9320:26,9340:17,9341:43,9344:21,
    9349:10,9352:26,9359:10,9362:26,9372:34,9382:33,9389:25,9392:27,9400:15,9403:28,
    9413:26,9419:10,9421:9,9424:21,9434:32,9442:20,9445:22,9455:34,9461:9,9463:9,
    9466:21,9476:34,9483:18,9486:26,9496:33,9516:17,9517:43,9520:21,9525:10,9528:26,
    9535:10,9538:28,9548:34,9558:34,9565:8,9568:26,9575:10,9578:26,9588:34,9598:34,
    9605:6,9608:26,9615:10,9618:26,9628:33,9638:34,9648:34,9655:10,9658:26,9665:10,
    9668:27,9686:64,9688:30,9696:73,9697:250,9699:14,9701:10,9703:9,9705:11,9707:17,
    9709:10,9711:12,9713:12,9715:9,9723:7,9755:33,9776:134,9787:109,9795:107,9803:103,
    9811:110,9819:108,9827:105,9835:104,9843:110,9851:109,9859:104,9867:112,9875:112,9883:107,
    9891:105,9899:111,9910:31,9911:13,9912:9,10018:8,10744:20,10745:19,10746:9,10875:11,
    10891:9,10907:7,10923:11,10939:8,10955:8,10971:7,10982:23,
}


class Prog:

    def __init__(self, nc):
        self.nc = nc
        self.es = ExitStack()
        self.stream = {e: [] for e in ENGS}
        self.sem = {}
        self.cnt = {}
        for k in ENGS:
            self.sem[k] = self.es.enter_context(nc.semaphore("sem_" + k))
            self.cnt[k] = 0
        for q in DMA_Q.values():
            for ln in range(NLANES):
                k = f"{q}{ln}"
                self.sem[k] = self.es.enter_context(nc.semaphore("sem_" + k))
                self.cnt[k] = 0
        self.lane_idx = {q: 0 for q in DMA_Q.values()}
        self.waited = {e: {} for e in ENGS}
        self.wait_idx = 0
        self.nodummy = False
        self.warm_dummy = None

    def _wait(self, eng, tok):
        if tok is None:
            return
        k, v = tok
        if v <= 0 or v <= self.waited[eng].get(k, 0):
            return
        if eng == "tensor":
            idx = self.wait_idx
            self.wait_idx += 1
            n = WARMTAB.get(idx, 0)
            if n and not self.nodummy and self.warm_dummy is not None:
                dst, src = self.warm_dummy
                for _ in range(n):
                    self.stream[eng].append(
                        ("op", "transpose",
                         (dst[0:128, 0:128], src[:], src[:]), {},
                         None, 0))
        self.waited[eng][k] = v
        self.stream[eng].append(("wait", self.sem[k], v))

    def emit(self, eng, opname, args, kwargs=None, waits=(), inc=True,
             inc_amt=1, sem_key=None):
        for t in waits:
            self._wait(eng, t)
        key = sem_key or eng
        if inc:
            self.cnt[key] += inc_amt
            v = self.cnt[key]
            self.stream[eng].append(
                ("op", opname, args, kwargs or {}, self.sem[key], inc_amt))
            return (key, v)
        self.stream[eng].append(("op", opname, args, kwargs or {}, None, 0))
        return None

    def mm(self, out, lhsT, rhs, start, stop, waits=(), inc=False):
        return self.emit("tensor", "matmul", (out, lhsT, rhs),
                         {"start": start, "stop": stop}, waits, inc)

    def tp(self, out, in_, ident, waits=(), inc=True):
        return self.emit("tensor", "transpose", (out, in_, ident), {},
                         waits, inc)

    def act(self, out, in_, func, waits=(), bias=0.0, scale=1.0,
            accum_out=None, inc=True):
        kw = {"bias": bias, "scale": scale}
        if accum_out is not None:
            kw["accum_out"] = accum_out
        return self.emit("scalar", "activation", (out, in_, func), kw,
                         waits, inc)

    def vec(self, opname, *args, waits=(), inc=True, **kwargs):
        return self.emit("vector", opname, args, kwargs, waits, inc)

    def gp(self, opname, *args, waits=(), inc=True, **kwargs):
        return self.emit("gpsimd", opname, args, kwargs, waits, inc)

    def dma(self, q, out, in_, waits=(), **kw):
        qk = DMA_Q[q]
        ln = self.lane_idx[qk]
        self.lane_idx[qk] = (ln + 1) % NLANES
        lane = f"{qk}{ln}"
        prev = (lane, self.cnt[lane])
        return self.emit(q, "dma_start", (), {"out": out, "in_": in_, **kw},
                         list(waits) + [prev], inc=True, inc_amt=16,
                         sem_key=lane)

    def now(self, eng):
        return (eng, self.cnt[eng])

    def barrier(self):
        toks = [(k, self.cnt[k]) for k in self.cnt]
        for e in ENGS:
            for t in toks:
                self._wait(e, t)

    def finalize(self, final_waits=()):
        for t in final_waits:
            self._wait("sync", t)
        with self.nc.Block() as block:
            for e in ENGS:
                stream = self.stream[e]

                def run(E, stream=stream):
                    for rec in stream:
                        if rec[0] == "wait":
                            E.wait_ge(rec[1], rec[2])
                        else:
                            _, opname, args, kwargs, sem, amt = rec
                            r = getattr(E, opname)(*args, **kwargs)
                            if sem is not None:
                                r.then_inc(sem, amt)
                getattr(block, e)(run)
        self.es.close()


class PsumPool:

    def __init__(self, es, nc, name, n, shape=(128, 512), dtype=F32,
                 tiles=None):
        if tiles is None:
            tiles = [es.enter_context(
                nc.psum_tensor(f"{name}{i}", list(shape), dtype))
                for i in range(n)]
        self.tiles = tiles
        self.last_use = [None] * len(tiles)
        self.i = 0

    def acquire(self):
        i = self.i
        self.i = (self.i + 1) % len(self.tiles)
        return self.tiles[i], self.last_use[i], i

    def set_last_use(self, idx, tok):
        self.last_use[idx] = tok


def cdiv(a, b):
    return (a + b - 1) // b


class LB:

    def __init__(self, name):
        self.nc = bass.Bass("TRN2", target_bir_lowering=False, name=name)
        self.P = Prog(self.nc)
        self.es = self.P.es
        self._scopes = [self.es]
        self._uid = 0
        self.out_dma_toks = []
        self.pg = PsumPool(self.es, self.nc, "pg", 7, (128, 512))
        self.psbf = PsumPool(self.es, self.nc, "psbf", 1, (128, 512), BF16)

    def push_scope(self):
        es = ExitStack()
        self._scopes.append(es)
        return es

    def pop_scope(self):
        es = self._scopes.pop()
        es.close()
        self.P.barrier()

    def finalize(self):
        while len(self._scopes) > 1:
            self._scopes.pop().close()
        self.P.finalize(self.out_dma_toks)

    def sb(self, name, shape, dtype):
        self._uid += 1
        return self._scopes[-1].enter_context(
            self.nc.sbuf_tensor(f"{name}_{self._uid}", list(shape), dtype))

    def dram_in(self, name, shape, dtype=F32):
        if not hasattr(self, "_din"):
            self._din = {}
        if name not in self._din:
            self._din[name] = self.nc.dram_tensor(
                name, list(shape), dtype, kind="ExternalInput").ap()
        return self._din[name]

    def dram_out(self, name, shape, dtype=F32):
        return self.nc.dram_tensor(name, list(shape), dtype,
                                   kind="ExternalOutput").ap()

    def dram_tmp(self, name, shape, dtype=F32):
        return self.nc.dram_tensor(name, list(shape), dtype,
                                   kind="Internal").ap()


    def load_w(self, name, shape, out_dtype=BF16, q="gpsimd", col0=None,
               col1=None, rowsplit=128, declared=None, row0=0, row1=None):
        ap = declared if declared is not None else self.dram_in(
            name, shape, F32)
        rows, cols = shape
        apx = ap
        if col0 is not None:
            apx = apx[:, col0:col1]
            cols = col1 - col0
        if row1 is not None:
            apx = apx[row0:row1, :]
            rows = row1 - row0
        ntile = cdiv(rows, rowsplit)
        tiles, toks = [], []
        queue = q if out_dtype != F32 else "sync"
        for i in range(ntile):
            r0, r1 = i * rowsplit, min(rows, (i + 1) * rowsplit)
            t = self.sb(f"w_{name}", (r1 - r0, cols), out_dtype)
            toks.append(self.P.dma(queue, t[:], apx[r0:r1, :]))
            tiles.append(t)
        return tiles, toks

    def load_row(self, name, n, out_dtype=F32, src=None):
        ap = src if src is not None else self.dram_in(name, (n,), F32)
        t = self.sb(f"r_{name}", (1, n), out_dtype)
        q = "gpsimd" if out_dtype != F32 else "sync"
        tok = self.P.dma(q, t[:], ap[None, :])
        return t, tok

    def load_col(self, name, n, out_dtype=F32, src=None):
        ap = src if src is not None else self.dram_in(name, (n,), F32)
        ntile = cdiv(n, 128)
        tiles, toks = [], []
        q = "gpsimd" if out_dtype != F32 else "sync"
        for i in range(ntile):
            r0, r1 = i * 128, min(n, (i + 1) * 128)
            t = self.sb(f"c_{name}", (r1 - r0, 1), out_dtype)
            toks.append(self.P.dma(q, t[:], ap[r0:r1, None]))
            tiles.append(t)
        return tiles, toks

    def consts(self):
        P = self.P
        ident = self.dram_in("c_ident", (128, 128), F32)
        self.identf = self.sb("identf", (128, 128), F32)
        self.identb = self.sb("identb", (128, 128), BF16)
        self.tok_identf = P.dma("sync", self.identf[:], ident)
        self.tok_identb = P.dma("gpsimd", self.identb[:], ident)
        self.onescol_f = self.sb("onescol_f", (128, 1), F32)
        self.onesrow_f = self.sb("onesrow_f", (1, 128), F32)
        self.onesrow_b = self.sb("onesrow_b", (1, 128), BF16)
        self.onescol_b = self.sb("onescol_b", (128, 1), BF16)
        self.tok_onescol = P.vec("memset", self.onescol_f[:], 1.0)
        self.tok_onesf = P.vec("memset", self.onesrow_f[:], 1.0)
        self.tok_ones = P.vec("memset", self.onesrow_b[:], 1.0)
        self.tok_onescol_b = P.vec("memset", self.onescol_b[:], 1.0)
        self.c_eps = self.sb("c_eps", (1, 1), F32)
        self.tok_eps = P.vec("memset", self.c_eps[:], 1e-5)
        self.c_one = self.sb("c_one", (1, 1), F32)
        self.tok_one = P.vec("memset", self.c_one[:], 1.0)
        self.warm_src = self.sb("warm_src", (128, 128), BF16)
        P.vec("memset", self.warm_src[:], 1.0)
        P.warm_dummy = (self.psbf.tiles[0], self.warm_src)


    def load_x_tokmajor(self, ap, xT):
        P = self.P
        tmps = [self.sb("ldx_tm", (128, D), F32) for _ in range(2)]
        cons = [None, None]
        x_toks = [None] * NDT
        for tt in range(T // 128):
            buf = tmps[tt % 2]
            ld = P.dma("sync", buf[:], ap[tt * 128:(tt + 1) * 128, :],
                       waits=[cons[tt % 2]])
            last_mt = None
            for i in range(NDT):
                ps, war, idx = self.pg.acquire()
                mt = P.tp(ps[:, 0:128], buf[:, i * 128:(i + 1) * 128],
                          self.identf[:],
                          waits=[ld, self.tok_identf, war], inc=True)
                ct = P.vec("tensor_copy", xT[i][:, tt * 128:(tt + 1) * 128],
                           ps[:, 0:128], waits=[mt])
                self.pg.set_last_use(idx, ct)
                x_toks[i] = ct
                last_mt = mt
            cons[tt % 2] = last_mt
        return x_toks

    def load_x_fm(self, ap, xT):
        return [self.P.dma("sync", xT[i][:], ap[i * 128:(i + 1) * 128, :])
                for i in range(len(xT))]

    def store_x_fm(self, xT, x_toks, ap):
        for i in range(NDT):
            tok = self.P.dma("sync", ap[i * 128:(i + 1) * 128, :], xT[i][:],
                             waits=[x_toks[i]])
            self.out_dma_toks.append(tok)


    def proj(self, W, Wtoks, X, Xtoks, out_rows, evac, t_width=T,
             slab=SLAB, w_col0=0, extra_waits=(), pool=None):
        P = self.P
        pool = pool or self.pg
        nkc = len(W)
        noc = cdiv(out_rows, 128)
        for s in range(cdiv(t_width, slab)):
            s0, s1 = s * slab, min(t_width, (s + 1) * slab)
            for oc in range(noc):
                o0, o1 = oc * 128, min(out_rows, (oc + 1) * 128)
                ps, war, idx = pool.acquire()
                mm_tok = None
                for kc in range(nkc):
                    first, last = kc == 0, kc == nkc - 1
                    waits = [Wtoks[kc], Xtoks[kc]] + list(extra_waits)
                    if first:
                        waits.append(war)
                    mm_tok = P.mm(ps[0:o1 - o0, 0:s1 - s0],
                                  W[kc][:, w_col0 + o0:w_col0 + o1],
                                  X[kc][:, s0:s1], first, last,
                                  waits=waits, inc=last)
                use = evac(oc, s0, s1, ps, mm_tok)
                pool.set_last_use(idx, use)


    def norm_fm(self, X, Xtoks, kind, writeback, eps=1e-5):
        P = self.P
        sqb = [self.sb("nrm_sq", (128, SLAB), F32)]
        sq_prev = [None]
        out_toks = [None] * NDT
        r_mu = self.sb("r_mu", (1, SLAB), F32)
        r_var = self.sb("r_var", (1, SLAB), F32)
        r_scale = self.sb("r_scale", (1, SLAB), F32)
        r_shift = self.sb("r_shift", (1, SLAB), F32)
        prev_bt = None
        for s in range(NSLAB):
            s0, s1 = s * SLAB, (s + 1) * SLAB
            psA, warA, idxA = self.pg.acquire()
            psB, warB, idxB = self.pg.acquire()
            mm_tok = None
            for i in range(NDT):
                b = sqb[0]
                sq_tok = P.act(b[:], X[i][:, s0:s1], AF.Square,
                               waits=[Xtoks[i], sq_prev[0]])
                sq_prev[0] = sq_tok
                first, last = i == 0, i == NDT - 1
                xdt = X[i][:].dtype
                ocol = self.onescol_f if xdt == F32 else self.onescol_b
                ocol_tok = (self.tok_onescol if xdt == F32
                            else self.tok_onescol_b)
                if kind == "ln":
                    P.mm(psA[0:1, 0:SLAB], ocol[:], X[i][:, s0:s1],
                         first, last,
                         waits=[Xtoks[i], ocol_tok,
                                warA if first else None])
                mm_tok = P.mm(psB[0:1, 0:SLAB], self.onescol_f[:],
                              b[:], first, last,
                              waits=[sq_tok, self.tok_onescol,
                                     warB if first else None],
                              inc=True)
                sq_prev[0] = mm_tok
            if kind == "ln":
                tmu = P.act(r_mu[:], psA[0:1, 0:SLAB], AF.Copy,
                            scale=1.0 / D, waits=[mm_tok, prev_bt])
                tmsq = P.act(r_var[:], psB[0:1, 0:SLAB], AF.Copy,
                             scale=1.0 / D, waits=[mm_tok, tmu])
                tmm = P.vec("scalar_tensor_tensor", r_scale[:], r_mu[:], -1.0,
                            r_mu[:], op0=ALU.mult, op1=ALU.mult, waits=[tmu])
                tv = P.vec("tensor_add", r_var[:], r_var[:], r_scale[:],
                           waits=[tmsq, tmm])
                tl = P.act(r_var[:], r_var[:], AF.Ln, bias=self.c_eps[:],
                           waits=[tv, self.tok_eps])
                ts = P.act(r_scale[:], r_var[:], AF.Exp, scale=-0.5,
                           waits=[tl])
                tsh = P.vec("scalar_tensor_tensor", r_shift[:], r_mu[:], -1.0,
                            r_scale[:], op0=ALU.mult, op1=ALU.mult,
                            waits=[tmu, ts])
                row_toks = [ts, tsh]
            else:
                tl = P.act(r_var[:], psB[0:1, 0:SLAB], AF.Ln,
                           scale=1.0 / D, waits=[mm_tok, prev_bt])
                tq = P.act(r_mu[:], r_var[:], AF.Exp, scale=0.5, waits=[tl])
                ta = P.vec("tensor_scalar_add", r_mu[:], r_mu[:], eps,
                           waits=[tq])
                tr = P.vec("reciprocal", r_scale[:], r_mu[:], waits=[ta])
                row_toks = [tr]
            bt0 = P.mm(psA[:, 0:SLAB], self.onesrow_f[:], r_scale[:],
                       True, True,
                       waits=row_toks + [self.tok_onesf], inc=True)
            bt = bt0
            if kind == "ln":
                bt = P.mm(psB[:, 0:SLAB], self.onesrow_f[:],
                          r_shift[:], True, True,
                          waits=row_toks + [bt0], inc=True)
            prev_bt = bt
            last_use = None
            for i in range(NDT):
                tok = writeback(i, s0, s1, psA, psB, bt)
                out_toks[i] = tok
                last_use = tok
            self.pg.set_last_use(idxA, last_use)
            self.pg.set_last_use(idxB, last_use)
        return out_toks



def emit_l0(lb, x0, xnf):
    P = lb.P
    yblk_b = lb.sb("yblk", (8, D), BF16)

    lb.push_scope()
    xT = [lb.sb("xT", (128, T), F32) for _ in range(NDT)]
    x_toks = lb.load_x_tokmajor(x0, xT)
    ln_tmp = [lb.sb("ln_tmp", (128, SLAB), F32) for _ in range(2)]
    ln_prev = [None, None]

    def ln_writeback(i, s0, s1, psA, psB, bt):
        tmp = ln_tmp[i % 2]
        ta = P.vec("tensor_mul", tmp[:], xT[i][:, s0:s1], psA[:, 0:SLAB],
                   waits=[x_toks[i], bt, ln_prev[i % 2]])
        tb = P.vec("tensor_add", xnf[i][:, s0:s1], tmp[:],
                   psB[:, 0:SLAB], waits=[ta])
        ln_prev[i % 2] = tb
        return tb

    xn_toks = lb.norm_fm(xT, x_toks, "ln", ln_writeback)
    lb.pop_scope()

    NBLK = T // 256
    ain = float(np.exp(0.125) / (256 * np.exp(0.125) + 1792))
    aout = float(1.0 / (256 * np.exp(0.125) + 1792))

    lb.push_scope()
    wv, wv_tok = lb.load_w("l0_wv", (D, D))
    wout, wout_tok = lb.load_w("l0_wout", (D, D))
    bv_col, bv_tok = lb.load_col("l0_bv", D)
    bout_b, bout_btok = lb.load_row("l0_bout", D, out_dtype=BF16)
    ab_b = [lb.sb("ab", (128, NBLK), BF16) for _ in range(NDT)]
    ab_toks = []
    for i in range(NDT):
        sxn = lb.sb("sxn", (128, NBLK), F32)
        stot = lb.sb("stot", (128, 1), F32)
        x3 = xnf[i].rearrange("p (b c) -> p b c", b=NBLK)
        t1 = P.vec("reduce_sum", sxn[:, :, None], x3, axis=AX.X,
                   waits=[xn_toks[i]])
        t2 = P.vec("reduce_sum", stot[:, :, None], sxn[:, None, :],
                   axis=AX.X, waits=[t1])
        t3 = P.vec("tensor_scalar_mul", sxn[:], sxn[:], ain - aout,
                   waits=[t1, t2])
        t4 = P.vec("tensor_scalar_mul", stot[:], stot[:], aout, waits=[t2])
        t5 = P.vec("tensor_scalar", ab_b[i][:], sxn[:], stot[:], None,
                   op0=ALU.add, waits=[t3, t4])
        ab_toks.append(t5)
    r_b = [lb.sb("r0a", (128, NBLK), BF16) for _ in range(NDT)]
    r_toks = [None] * NDT

    def evac_r(oc, s0, s1, ps, mm_tok):
        tok = P.vec("tensor_scalar", r_b[oc][:, s0:s1], ps[0:128, 0:NBLK],
                    bv_col[oc][:], None, op0=ALU.add,
                    waits=[mm_tok, bv_tok[oc]])
        r_toks[oc] = tok
        return tok

    lb.proj(wv, wv_tok, ab_b, ab_toks, D, evac_r, t_width=NBLK, slab=NBLK)

    yb_tok = None
    for half in range(2):
        c0, c1 = half * 512, (half + 1) * 512
        psy, war_y, psy_i = lb.pg.acquire()
        ytok = None
        for kc in range(NDT):
            P.mm(psy[0:NBLK, 0:512], r_b[kc][:], wout[kc][:, c0:c1],
                 kc == 0, False,
                 waits=[r_toks[kc], wout_tok[kc],
                        war_y if kc == 0 else None])
        ytok = P.mm(psy[0:NBLK, 0:512], lb.onesrow_b[0:1, 0:NBLK],
                    bout_b[0:1, c0:c1], False, True,
                    waits=[bout_btok, lb.tok_ones], inc=True)
        yb_tok = P.vec("tensor_copy", yblk_b[0:NBLK, c0:c1],
                       psy[0:NBLK, 0:512], waits=[ytok, yb_tok])
        lb.pg.set_last_use(psy_i, yb_tok)
    lb.pop_scope()

    cur_toks = list(xn_toks)
    cur_toks = emit_l0_mla(lb, xnf, cur_toks, stop=None)
    cur_toks = emit_l0_moe(lb, xnf, cur_toks)

    lb.push_scope()
    eblk_t, tok_eblk_l = lb.load_w("c_eblk", (8, T), BF16, rowsplit=8)
    eblk, tok_eblk = eblk_t[0], tok_eblk_l[0]
    for oc in range(NDT):
        for s in range(NSLAB):
            s0, s1 = s * SLAB, (s + 1) * SLAB
            ps, war, idx = lb.pg.acquire()
            et = P.mm(ps[:], yblk_b[0:NBLK, oc * 128:(oc + 1) * 128],
                      eblk[0:NBLK, s0:s1], True, True,
                      waits=[yb_tok, tok_eblk, war], inc=True)
            at = P.vec("tensor_add", xnf[oc][:, s0:s1], xnf[oc][:, s0:s1],
                       ps[:], waits=[et, cur_toks[oc]])
            lb.pg.set_last_use(idx, at)
            cur_toks[oc] = at
    xin_buf = [lb.sb("xin", (128, D), F32) for _ in range(2)]
    xin_tok = [None, None]
    for tt in range(T // 128):
        buf = xin_buf[tt % 2]
        ld = P.dma("sync", buf[:], x0[tt * 128:(tt + 1) * 128, :],
                   waits=[xin_tok[tt % 2]])
        last = None
        for oc in range(NDT):
            ps, war, idx = lb.pg.acquire()
            mt = P.tp(ps[:, 0:128], buf[:, oc * 128:(oc + 1) * 128],
                      lb.identf[:], waits=[ld, lb.tok_identf, war], inc=True)
            at = P.vec("tensor_add", xnf[oc][:, tt * 128:(tt + 1) * 128],
                       xnf[oc][:, tt * 128:(tt + 1) * 128], ps[:, 0:128],
                       waits=[mt, cur_toks[oc]])
            lb.pg.set_last_use(idx, at)
            cur_toks[oc] = at
            last = at
        xin_tok[tt % 2] = last
    lb.pop_scope()
    return cur_toks


def emit_l0_mla(lb, xnf, cur_toks, stop=None):
    P = lb.P
    NH, LAT = 16, 256
    cur_toks = list(cur_toks)

    lb.push_scope()
    cmask_t, cmask_tokl = lb.load_w("c_cmask01", (128, 4 * SLAB), BF16)
    cmask, tok_cmask = cmask_t[0], cmask_tokl[0]

    wln1_col, wln1_tok = lb.load_col("l0_wln1", D)
    xr_b = [lb.sb("xr", (128, T), BF16) for _ in range(NDT)]
    rms_tmp = [lb.sb("rms_tmp", (128, SLAB), F32)]
    rms_prev = [None]

    def rms_wb(i, s0, s1, psA, psB, bt):
        tmp = rms_tmp[0]
        ta = P.vec("tensor_mul", tmp[:], xnf[i][:, s0:s1], psA[:, 0:SLAB],
                   waits=[cur_toks[i], bt, rms_prev[0]])
        tb = P.vec("tensor_scalar", xr_b[i][:, s0:s1], tmp[:],
                   wln1_col[i][:], None, op0=ALU.mult,
                   waits=[ta, wln1_tok[i]])
        rms_prev[0] = tb
        return tb

    xr_toks = lb.norm_fm(xnf, cur_toks, "rms", rms_wb)
    if stop == "rms":
        lb.pop_scope()
        return cur_toks

    kvl_b = [lb.sb("kvl", (128, T), BF16) for _ in range(2)]
    ql_b = [lb.sb("ql", (128, T), BF16) for _ in range(2)]
    kvl_toks = [None] * 2
    ql_toks = [None] * 2
    lb.push_scope()
    wkvd, wkvd_tok = lb.load_w("l0_wkvd", (D, LAT))
    wqd, wqd_tok = lb.load_w("l0_wqd", (D, LAT))

    def mk_evac(dst, dtoks):
        def evac(oc, s0, s1, ps, mm_tok):
            tok = P.vec("tensor_copy", dst[oc][:, s0:s1],
                        ps[0:128, 0:s1 - s0], waits=[mm_tok])
            dtoks[oc] = tok
            return tok
        return evac

    lb.proj(wkvd, wkvd_tok, xr_b, xr_toks, LAT, mk_evac(kvl_b, kvl_toks))
    lb.proj(wqd, wqd_tok, xr_b, xr_toks, LAT, mk_evac(ql_b, ql_toks))
    lb.pop_scope()
    if stop == "latents":
        lb.pop_scope()
        return cur_toks

    for hg in range(2):
        lb.push_scope()
        KK = [lb.sb("KK", (128, T), BF16) for _ in range(4)]
        QQ = [lb.sb("QQ", (128, T), BF16) for _ in range(4)]
        V = [lb.sb("V", (128, 520), BF16) for _ in range(T // 128)]
        KK_toks = [None] * 4
        QQ_toks = [None] * 4
        V_toks = [None] * (T // 128)
        co = hg * 256
        lb.push_scope()
        rop_tmp = [lb.sb("rop_tmp", (64, SLAB), F32) for _ in range(2)]
        cos1_t, cos1_tok = lb.load_w("c_cos", (32, T), BF16, rowsplit=32)
        sinn1_t, sinn1_tok = lb.load_w("c_sinn", (32, T), BF16, rowsplit=32)
        cos1, sinn1 = cos1_t[0], sinn1_t[0]
        tok_cos1, tok_sinn1 = cos1_tok[0], sinn1_tok[0]
        wku, wku_tok = lb.load_w("l0_wku", (LAT, 512), col0=co, col1=co + 256)
        wqu, wqu_tok = lb.load_w("l0_wqu", (LAT, 512), col0=co, col1=co + 256)
        wrq, wrq_tok = lb.load_w("l0_wropeq", (LAT, 512), col0=co,
                                 col1=co + 256)
        wrqp, wrqp_tok = lb.load_w("l0_wropeqp", (LAT, 512), col0=co,
                                   col1=co + 256)
        wrk, wrk_tok = lb.load_w("l0_wropek", (D, 512), col0=co,
                                 col1=co + 256)
        wrkp, wrkp_tok = lb.load_w("l0_wropekp", (D, 512), col0=co,
                                   col1=co + 256)
        wvu, wvu_tok = lb.load_w("l0_wvu", (LAT, D), col0=hg * 512,
                                 col1=hg * 512 + 512)

        rop_war = [None]

        def emit_pair(dst, toks_arr, g4, wk, wk_tok, xin, xin_toks,
                      wr1, wr1_tok, wr2, wr2_tok, rin, rin_toks):
            last = rop_war[0]
            for s in range(NSLAB):
                s0, s1 = s * SLAB, (s + 1) * SLAB
                ps, war, idx = lb.pg.acquire()
                mmk = None
                for kc in range(len(wk)):
                    mmk = P.mm(ps[0:64, 0:SLAB],
                               wk[kc][:, g4 * 64:(g4 + 1) * 64],
                               xin[kc][:, s0:s1], kc == 0,
                               kc == len(wk) - 1,
                               waits=[wk_tok[kc], xin_toks[kc],
                                      war if kc == 0 else None],
                               inc=(kc == len(wk) - 1))
                c1 = P.vec("tensor_copy", dst[g4][0:32, s0:s1],
                           ps[0:32, 0:SLAB], waits=[mmk, last])
                c2 = P.vec("tensor_copy", dst[g4][64:96, s0:s1],
                           ps[32:64, 0:SLAB], waits=[mmk])
                lb.pg.set_last_use(idx, c2)
                ps1, war1, idx1 = lb.pg.acquire()
                mm1 = None
                for kc in range(len(wr1)):
                    mm1 = P.mm(ps1[0:64, 0:SLAB],
                               wr1[kc][:, g4 * 64:(g4 + 1) * 64],
                               rin[kc][:, s0:s1], kc == 0,
                               kc == len(wr1) - 1,
                               waits=[wr1_tok[kc], rin_toks[kc],
                                      war1 if kc == 0 else None],
                               inc=(kc == len(wr1) - 1))
                ps2, war2, idx2 = lb.pg.acquire()
                mm2 = None
                for kc in range(len(wr2)):
                    mm2 = P.mm(ps2[0:64, 0:SLAB],
                               wr2[kc][:, g4 * 64:(g4 + 1) * 64],
                               rin[kc][:, s0:s1], kc == 0,
                               kc == len(wr2) - 1,
                               waits=[wr2_tok[kc], rin_toks[kc],
                                      war2 if kc == 0 else None],
                               inc=(kc == len(wr2) - 1))
                tm1, tm2 = rop_tmp[0], rop_tmp[1]
                v1a = P.vec("tensor_mul", tm1[0:32, :], ps1[0:32, 0:SLAB],
                            cos1[0:32, s0:s1], waits=[mm1, tok_cos1, last])
                v1 = P.vec("tensor_mul", tm1[32:64, :], ps1[32:64, 0:SLAB],
                           cos1[0:32, s0:s1], waits=[mm1, v1a])
                lb.pg.set_last_use(idx1, v1)
                v2a = P.vec("tensor_mul", tm2[0:32, :], ps2[0:32, 0:SLAB],
                            sinn1[0:32, s0:s1], waits=[mm2, tok_sinn1, last])
                v2 = P.vec("tensor_mul", tm2[32:64, :], ps2[32:64, 0:SLAB],
                           sinn1[0:32, s0:s1], waits=[mm2, v2a])
                lb.pg.set_last_use(idx2, v2)
                a1 = P.vec("tensor_add", dst[g4][32:64, s0:s1], tm1[0:32, :],
                           tm2[0:32, :], waits=[v1a, v2a, v1, v2])
                a2 = P.vec("tensor_add", dst[g4][96:128, s0:s1],
                           tm1[32:64, :], tm2[32:64, :], waits=[v1, v2])
                last = a2
            rop_war[0] = a2
            toks_arr[g4] = last

        for g4 in range(4):
            emit_pair(KK, KK_toks, g4, wku, wku_tok, kvl_b, kvl_toks,
                      wrk, wrk_tok, wrkp, wrkp_tok, xr_b, xr_toks)
            emit_pair(QQ, QQ_toks, g4, wqu, wqu_tok, ql_b, ql_toks,
                      wrq, wrq_tok, wrqp, wrqp_tok, ql_b, ql_toks)
        if stop == "kkqq":
            lb.pop_scope()
            lb.pop_scope()
            lb.pop_scope()
            return cur_toks

        for tt in range(T // 128):
            ms = P.vec("memset",
                       V[tt].rearrange("p (h c) -> p h c", c=65)[:, :, 64:65],
                       1.0, waits=[V_toks[tt]])
            ps, war, idx = lb.pg.acquire()
            mmv = None
            for kc in range(2):
                mmv = P.mm(ps[0:128, 0:SLAB],
                           kvl_b[kc][:, tt * 128:(tt + 1) * 128],
                           wvu[kc][:], kc == 0, kc == 1,
                           waits=[kvl_toks[kc], wvu_tok[kc],
                                  war if kc == 0 else None], inc=(kc == 1))
            dview = V[tt].rearrange("p (h c) -> p h c", c=65)[:, 0:8, 0:64]
            sview = ps[0:128, 0:SLAB].rearrange("p (h c) -> p h c", c=64)
            cv = P.vec("tensor_copy", dview, sview, waits=[mmv, ms])
            lb.pg.set_last_use(idx, cv)
            V_toks[tt] = cv
        lb.pop_scope()
        if stop == "v":
            lb.pop_scope()
            lb.pop_scope()
            return cur_toks
        wo4, wo4_tok = lb.load_w("l0_wo", (D, D), row0=hg * 512,
                                 row1=hg * 512 + 512)

        negU = lb.sb("negU", (128, 8), F32)
        negU_toks = [None] * 8
        lb.push_scope()
        sqt = lb.sb("sqt", (128, T), BF16)
        m12 = [lb.sb("mkq", (1, 1), F32) for _ in range(2)]
        mtmp = lb.sb("mtmp", (1, 1), F32)
        uval = lb.sb("uval", (1, 1), F32)
        sq_war = {0: None, 64: None}
        for h in range(8):
            g4, hb = h // 2, (h % 2) * 64
            toks_nrm = []
            for j, (MAT, MT) in enumerate(((KK, KK_toks), (QQ, QQ_toks))):
                sq_tok = P.act(sqt[hb:hb + 64, :], MAT[g4][hb:hb + 64, :],
                               AF.Square,
                               waits=[MT[g4], sq_war[hb]] + toks_nrm)
                mj = P.vec("memset", m12[j][:], 0.0, waits=toks_nrm)
                ps, war, idx = lb.pg.acquire()
                for s in range(NSLAB):
                    mmn = P.mm(ps[0:1, 0:SLAB],
                               lb.onescol_b[hb:hb + 64, :],
                               sqt[hb:hb + 64, s * SLAB:(s + 1) * SLAB],
                               True, True,
                               waits=[sq_tok, lb.tok_onescol_b,
                                      war if s == 0 else None,
                                      toks_nrm[-1] if toks_nrm else None],
                               inc=True)
                    c1 = P.vec("reduce_max", mtmp[:, :, None],
                               ps[0:1, None, 0:SLAB], axis=AX.X,
                               waits=[mmn, mj] + toks_nrm[-1:])
                    c2 = P.vec("tensor_max", m12[j][:],
                               m12[j][:], mtmp[:], waits=[c1])
                    toks_nrm.append(c2)
                lb.pg.set_last_use(idx, toks_nrm[-1])
                sq_war[hb] = toks_nrm[-1]
            t2 = P.vec("tensor_mul", m12[0][:], m12[0][:], m12[1][:],
                       waits=toks_nrm)
            t3 = P.act(uval[:], m12[0][:], AF.Ln, waits=[t2])
            t4 = P.act(uval[:], uval[:], AF.Exp, scale=0.5, waits=[t3])
            t5 = P.vec("tensor_scalar_mul", uval[:], uval[:], -0.125,
                       waits=[t4])
            ps, war, idx = lb.pg.acquire()
            t6 = P.mm(ps[0:128, 0:1], lb.onesrow_f[:], uval[:], True, True,
                      waits=[t5, lb.tok_onesf, war], inc=True)
            t7 = P.vec("tensor_copy", negU[:, h:h + 1], ps[0:128, 0:1],
                       waits=[t6])
            lb.pg.set_last_use(idx, t7)
            negU_toks[h] = t7
        lb.pop_scope()

        if stop == "u":
            lb.pop_scope()
            lb.pop_scope()
            return cur_toks
        P.barrier()
        poolS = PsumPool(None, None, "sS", 0,
                         tiles=lb.pg.tiles[0:3] + lb.pg.tiles[6:7])
        poolPV = PsumPool(None, None, "sPV", 0, tiles=lb.pg.tiles[3:5])
        poolN = PsumPool(None, None, "sN", 0, tiles=lb.pg.tiles[5:6])
        NPB = 4
        Pb = [lb.sb("Pb", (128, SLAB), BF16) for _ in range(NPB)]
        pb_war = [None] * NPB
        yn4 = [lb.sb("yn4", (128, SLAB), BF16) for _ in range(4)]
        yn_toks = [None] * 8
        yn_war = [None] * 8
        bcn = [lb.sb("bcn", (64, SLAB), F32) for _ in range(2)]
        bcn_war = [None, None]
        rrow = [lb.sb("rrow", (1, SLAB), F32) for _ in range(2)]
        rrow_war = [None, None]
        MOFF = {0: 0, 128: 1, 256: 2, 384: 3}
        qs_max = {"attn1": 1, "attnq1": 1, "attnq2": 2, "attnh8": 1,
                  "attnwo1": 1}.get(stop, NSLAB)
        h_max = {"attn1": 1, "attnwo1": 1}.get(stop, 8)
        do_wo = stop not in ("attn1", "attnh8")
        for qs in range(qs_max):
            q0, q1 = qs * SLAB, (qs + 1) * SLAB
            for h in range(h_max):
                g4, hb = h // 2, (h % 2) * 64
                pspv, warpv, ipv = poolPV.acquire()
                nkc = 4 * (qs + 1)
                pvmm = None
                pend = None
                for kc in range(nkc):
                    psS, warS, iS = poolS.acquire()
                    smm = P.mm(psS[0:128, 0:SLAB],
                               KK[g4][hb:hb + 64, kc * 128:(kc + 1) * 128],
                               QQ[g4][hb:hb + 64, q0:q1], True, True,
                               waits=[KK_toks[g4], QQ_toks[g4], warS],
                               inc=True)
                    pb = Pb[kc % NPB]
                    e = P.act(pb[:], psS[0:128, 0:SLAB], AF.Exp, scale=0.125,
                              bias=negU[:, h:h + 1],
                              waits=[smm, negU_toks[h], pb_war[kc % NPB]])
                    poolS.set_last_use(iS, e)
                    ptok = e
                    if kc >= qs * 4:
                        mi = MOFF[kc * 128 - qs * 512]
                        ptok = P.vec("tensor_mul", pb[:], pb[:],
                                     cmask[:, mi * 512:(mi + 1) * 512],
                                     waits=[e, tok_cmask])
                    if pend is not None:
                        pkc, pptok = pend
                        pvmm = P.mm(pspv[0:65, 0:SLAB],
                                    V[pkc][:, h * 65:(h + 1) * 65],
                                    Pb[pkc % NPB][:], pkc == 0, False,
                                    waits=[V_toks[pkc], pptok,
                                           warpv if pkc == 0 else None],
                                    inc=True)
                        pb_war[pkc % NPB] = pvmm
                    pend = (kc, ptok)
                pkc, pptok = pend
                pvmm = P.mm(pspv[0:65, 0:SLAB],
                            V[pkc][:, h * 65:(h + 1) * 65],
                            Pb[pkc % NPB][:], pkc == 0, True,
                            waits=[V_toks[pkc], pptok,
                                   warpv if pkc == 0 else None],
                            inc=True)
                pb_war[pkc % NPB] = pvmm
                rr = rrow[h % 2]
                tl = P.act(rr[:], pspv[64:65, 0:SLAB], AF.Ln,
                           waits=[pvmm, rrow_war[h % 2]])
                t1 = P.act(rr[:], rr[:], AF.Exp, scale=-1.0, waits=[tl])
                psn, warn, inn = poolN.acquire()
                t2 = P.mm(psn[0:64, 0:SLAB], lb.onesrow_f[0:1, 0:64], rr[:],
                          True, True, waits=[t1, warn, lb.tok_onesf],
                          inc=True)
                rrow_war[h % 2] = t2
                bc = bcn[h % 2]
                t3 = P.vec("tensor_copy", bc[:], psn[0:64, 0:SLAB],
                           waits=[t2, bcn_war[h % 2]])
                poolN.set_last_use(inn, t3)
                t4 = P.vec("tensor_mul", yn4[g4][hb:hb + 64, :],
                           pspv[0:64, 0:SLAB], bc[:],
                           waits=[t3, yn_war[h]])
                poolPV.set_last_use(ipv, t4)
                bcn_war[h % 2] = t4
                yn_toks[h] = t4
            last_wo = None
            for oc in range(NDT if do_wo else 0):
                psO, warO, iO = poolS.acquire()
                mmo = None
                for g4 in range(4):
                    mmo = P.mm(psO[0:128, 0:SLAB],
                               wo4[g4][:, oc * 128:(oc + 1) * 128],
                               yn4[g4][:], g4 == 0, g4 == 3,
                               waits=[wo4_tok[g4], yn_toks[2 * g4],
                                      yn_toks[2 * g4 + 1],
                                      warO if g4 == 0 else None],
                               inc=(g4 == 3))
                at = P.vec("tensor_add", xnf[oc][:, q0:q1],
                           xnf[oc][:, q0:q1], psO[0:128, 0:SLAB],
                           waits=[mmo, cur_toks[oc]])
                poolS.set_last_use(iO, at)
                cur_toks[oc] = at
                last_wo = at
            for h in range(8):
                yn_war[h] = last_wo
        P.barrier()
        lb.pop_scope()
    lb.pop_scope()
    return cur_toks


def emit_l0_moe(lb, xnf, cur_toks):
    P = lb.P
    E = 7
    cur_toks = list(cur_toks)
    lb.push_scope()
    xr = [lb.sb("xr2", (128, T), BF16) for _ in range(NDT)]
    wrow = [lb.sb("wrow", (1, T), BF16) for _ in range(E)]
    wrow_toks = [None] * E
    lb.push_scope()
    wln2_col, wln2_tok = lb.load_col("l0_wln2", D)
    rms_tmp = [lb.sb("rms2_tmp", (128, SLAB), F32) for _ in range(2)]
    rms_prev = [None, None]

    def rms_wb(i, s0, s1, psA, psB, bt):
        tmp = rms_tmp[i % 2]
        ta = P.vec("tensor_mul", tmp[:], xnf[i][:, s0:s1], psA[:, 0:SLAB],
                   waits=[cur_toks[i], bt, rms_prev[i % 2]])
        tb = P.vec("tensor_scalar", xr[i][:, s0:s1], tmp[:],
                   wln2_col[i][:], None, op0=ALU.mult,
                   waits=[ta, wln2_tok[i]])
        rms_prev[i % 2] = tb
        return tb

    xr_toks = lb.norm_fm(xnf, cur_toks, "rms", rms_wb)
    lb.pop_scope()

    lb.push_scope()
    wr_t, wr_tok = lb.load_w("l0_wrouter", (D, E))
    rb_col, rb_tok = lb.load_col("l0_rbias", E)
    probs_T = lb.sb("probsT", (E, T), F32)
    probs_toks = []
    for s in range(NSLAB):
        s0, s1 = s * SLAB, (s + 1) * SLAB
        ps, war, idx = lb.pg.acquire()
        mmr = None
        for kc in range(NDT):
            mmr = P.mm(ps[0:E, 0:SLAB], wr_t[kc][:], xr[kc][:, s0:s1],
                       kc == 0, kc == NDT - 1,
                       waits=[wr_tok[kc], xr_toks[kc],
                              war if kc == 0 else None],
                       inc=(kc == NDT - 1))
        sg = P.act(probs_T[0:E, s0:s1], ps[0:E, 0:SLAB], AF.Sigmoid,
                   bias=rb_col[0][0:E, :], waits=[mmr, rb_tok[0]])
        lb.pg.set_last_use(idx, sg)
        probs_toks.append(sg)
    ptile = lb.sb("ptile", (128, E), F32)
    junk = lb.sb("junk", (128, E), F32)
    accgt = lb.sb("accgt", (128, 1), F32)
    acceq = lb.sb("acceq", (128, 1), F32)
    zcol = lb.sb("zcol", (128, 1), F32)
    ztok = P.vec("memset", zcol[:], 0.0)
    sel = lb.sb("sel", (128, E), F32)
    den = lb.sb("den", (128, 1), F32)
    for tt in range(T // 128):
        t0 = tt * 128
        ps, war, idx = lb.pg.acquire()
        mt = P.tp(ps[0:128, 0:E], probs_T[0:E, t0:t0 + 128],
                  lb.identf[0:E, 0:E],
                  waits=probs_toks + [lb.tok_identf, war], inc=True)
        cp = P.vec("tensor_copy", ptile[:], ps[0:128, 0:E], waits=[mt])
        lb.pg.set_last_use(idx, cp)
        prev = cp
        sel_toks = []
        for e in range(E):
            t1 = P.vec("tensor_scalar", junk[:], ptile[:],
                       ptile[:, e:e + 1], 0.0, op0=ALU.is_gt,
                       op1=ALU.add, accum_out=accgt[:], waits=[cp, prev])
            if e > 0:
                t2 = P.vec("tensor_scalar", junk[:, 0:e], ptile[:, 0:e],
                           ptile[:, e:e + 1], 0.0, op0=ALU.is_equal,
                           op1=ALU.add, accum_out=acceq[:], waits=[t1])
                t3 = P.vec("tensor_add", accgt[:], accgt[:], acceq[:],
                           waits=[t2])
            else:
                t3 = t1
            t4 = P.vec("tensor_single_scalar", sel[:, e:e + 1], accgt[:],
                       2.0, op=ALU.is_lt, waits=[t3])
            sel_toks.append(t4)
            prev = t4
        t5 = P.vec("tensor_mul", sel[:], sel[:], ptile[:], waits=sel_toks)
        t6 = P.vec("reduce_sum", den[:, :, None], sel[:, None, :],
                   axis=AX.X, waits=[t5])
        t7 = P.vec("reciprocal", den[:], den[:], waits=[t6])
        t8 = P.vec("tensor_scalar", sel[:], sel[:], den[:], None,
                   op0=ALU.mult, waits=[t7])
        for e in range(E):
            ps2, war2, idx2 = lb.pg.acquire()
            mt2 = P.tp(ps2[0:1, 0:128], sel[:, e:e + 1], lb.identf[:],
                       waits=[t8, lb.tok_identf, war2], inc=True)
            cv = P.vec("tensor_copy", wrow[e][0:1, t0:t0 + 128],
                       ps2[0:1, 0:128], waits=[mt2, wrow_toks[e]])
            lb.pg.set_last_use(idx2, cv)
            wrow_toks[e] = cv

    lb.pop_scope()

    wgs_ap = lb.dram_in("l0_wgs", (D, D))
    wus_ap = lb.dram_in("l0_wus", (D, D))
    wds_ap = lb.dram_in("l0_wds", (D, D))
    wgr_ap = lb.dram_in("l0_wgr", (E * D, D))
    wur_ap = lb.dram_in("l0_wur", (E * D, D))
    wdr_ap = lb.dram_in("l0_wdr", (E * D, D))
    gbuf = [lb.sb("gw", (128, D), BF16) for _ in range(NDT)]
    ubuf = [lb.sb("uw", (128, D), BF16) for _ in range(NDT)]
    dbuf = [lb.sb("dw", (128, D), BF16) for _ in range(NDT)]
    hmid = [lb.sb("hmid", (128, SLAB), BF16) for _ in range(NDT)]
    sgb = lb.sb("sgb", (128, SLAB), F32)
    hmt = lb.sb("hmt", (128, SLAB), F32)
    P.barrier()
    poolGU = PsumPool(None, None, "mGU", 0, tiles=lb.pg.tiles[0:4])
    poolW = PsumPool(None, None, "mW", 0, tiles=lb.pg.tiles[4:5])
    poolD = PsumPool(None, None, "mD", 0, tiles=lb.pg.tiles[5:7])

    def _srcs(ei, kc):
        e = ei - 1
        if e < 0:
            return (wgs_ap[kc * 128:(kc + 1) * 128, :],
                    wus_ap[kc * 128:(kc + 1) * 128, :],
                    wds_ap[kc * 128:(kc + 1) * 128, :])
        r0 = e * D + kc * 128
        return (wgr_ap[r0:r0 + 128, :], wur_ap[r0:r0 + 128, :],
                wdr_ap[r0:r0 + 128, :])

    def issue_gu(ei, wait):
        gt, ut = [], []
        for kc in range(NDT):
            gsrc, usrc, _ = _srcs(ei, kc)
            gt.append(P.dma("gpsimd", gbuf[kc][:], gsrc, waits=[wait]))
            ut.append(P.dma("gpsimd", ubuf[kc][:], usrc, waits=[wait]))
        return gt, ut

    def issue_d(ei, wait):
        return [P.dma("gpsimd", dbuf[kc][:], _srcs(ei, kc)[2],
                      waits=[wait]) for kc in range(NDT)]

    gu_next = issue_gu(0, None)
    d_next = issue_d(0, None)
    for ei in range(E + 1):
        e = ei - 1
        g_toks, u_toks = gu_next
        d_toks = d_next
        for s in range(NSLAB):
            s0, s1 = s * SLAB, (s + 1) * SLAB
            if e >= 0:
                psw, warw, idxw = poolW.acquire()
                bw = P.mm(psw[0:128, 0:SLAB], lb.onesrow_b[:],
                          wrow[e][0:1, s0:s1], True, True,
                          waits=[wrow_toks[e], lb.tok_ones, warw], inc=True)
            hm_toks = []
            for ic in range(NDT):
                psg, warg, idxg = poolGU.acquire()
                mmg = None
                for kc in range(NDT):
                    mmg = P.mm(psg[0:128, 0:SLAB],
                               gbuf[kc][:, ic * 128:(ic + 1) * 128],
                               xr[kc][:, s0:s1], kc == 0, kc == NDT - 1,
                               waits=[g_toks[kc], xr_toks[kc],
                                      warg if kc == 0 else None],
                               inc=(kc == NDT - 1))
                psu, waru, idxu = poolGU.acquire()
                mmu = None
                for kc in range(NDT):
                    mmu = P.mm(psu[0:128, 0:SLAB],
                               ubuf[kc][:, ic * 128:(ic + 1) * 128],
                               xr[kc][:, s0:s1], kc == 0, kc == NDT - 1,
                               waits=[u_toks[kc], xr_toks[kc],
                                      waru if kc == 0 else None],
                               inc=(kc == NDT - 1))
                st = P.act(sgb[:], psg[0:128, 0:SLAB], AF.Silu,
                           waits=[mmg, hm_toks[-1] if hm_toks else None])
                poolGU.set_last_use(idxg, st)
                if e >= 0:
                    mt = P.vec("tensor_mul", hmt[:], sgb[:],
                               psu[0:128, 0:SLAB], waits=[st, mmu])
                    poolGU.set_last_use(idxu, mt)
                    ht = P.vec("tensor_mul", hmid[ic][:], hmt[:],
                               psw[0:128, 0:SLAB], waits=[mt, bw])
                else:
                    ht = P.vec("tensor_mul", hmid[ic][:], sgb[:],
                               psu[0:128, 0:SLAB], waits=[st, mmu])
                    poolGU.set_last_use(idxu, ht)
                hm_toks.append(ht)
            if e >= 0:
                poolW.set_last_use(idxw, hm_toks[-1])
            if s == NSLAB - 1 and ei < E:
                gu_next = issue_gu(ei + 1, mmu)
            for oc in range(NDT):
                psd, ward, idxd = poolD.acquire()
                mmd = None
                for ic in range(NDT):
                    mmd = P.mm(psd[0:128, 0:SLAB],
                               dbuf[ic][:, oc * 128:(oc + 1) * 128],
                               hmid[ic][:], ic == 0, ic == NDT - 1,
                               waits=[d_toks[ic], hm_toks[ic],
                                      ward if ic == 0 else None],
                               inc=(ic == NDT - 1))
                at = P.vec("tensor_add", xnf[oc][:, s0:s1],
                           xnf[oc][:, s0:s1], psd[0:128, 0:SLAB],
                           waits=[mmd, cur_toks[oc]])
                poolD.set_last_use(idxd, at)
                cur_toks[oc] = at
        if ei < E:
            d_next = issue_d(ei + 1, P.now("tensor"))
    P.barrier()
    lb.pop_scope()
    return cur_toks




def _rope_tables():
    HH = 32
    inv = 1.0 / (10000.0 ** (np.arange(0, HH, 2, dtype=np.float64) / HH))
    t = np.arange(T, dtype=np.float64)
    freqs = np.outer(t, inv)
    emb = np.concatenate([freqs, freqs], axis=1)
    cos = np.cos(emb).astype(np.float32).T
    sin = np.sin(emb).astype(np.float32).T
    sinn = sin.copy()
    sinn[0:16] = -sinn[0:16]
    return np.ascontiguousarray(cos), np.ascontiguousarray(sinn)


def _consts_common():
    c = {}
    c["c_ident"] = np.eye(128, dtype=np.float32)
    k = np.arange(128)[:, None]
    q = np.arange(SLAB)[None, :]
    cm = [(q >= k + o).astype(np.float32) for o in (0, 128, 256, 384)]
    c["c_cmask01"] = np.concatenate(cm, axis=1)
    tt = np.arange(T)
    c["c_eblk"] = (tt[None, :] // 256 == np.arange(8)[:, None]).astype(
        np.float32)
    return c


def _perm_rope_cols(W):
    W = W.reshape(W.shape[0], 16, 32)
    Wp = np.concatenate([W[:, :, 16:32], W[:, :, 0:16]], axis=2)
    return np.ascontiguousarray(Wp.reshape(W.shape[0], 512))


def prep_l0(inputs):
    f = lambda a: np.ascontiguousarray(np.asarray(a, dtype=np.float32))
    d = dict(_consts_common())
    cos1, sinn1 = _rope_tables()
    d["c_cos"] = cos1
    d["c_sinn"] = sinn1
    d["l0_wv"] = f(inputs["L0_attn_W_qkv"][:, 2048:3072])
    d["l0_bv"] = f(inputs["L0_attn_b_qkv"][2048:3072])
    d["l0_wout"] = f(inputs["L0_attn_W_out"])
    d["l0_bout"] = f(inputs["L0_attn_b_out"])
    d["l0_wln1"] = f(inputs["L0_ffn_w_ln1"])
    d["l0_wln2"] = f(inputs["L0_ffn_w_ln2"])
    d["l0_wkvd"] = f(inputs["L0_ffn_W_kvd"])
    d["l0_wqd"] = f(inputs["L0_ffn_W_qd"])
    d["l0_wku"] = f(inputs["L0_ffn_W_ku"])
    d["l0_wqu"] = f(inputs["L0_ffn_W_qu"])
    d["l0_wvu"] = f(inputs["L0_ffn_W_vu"])
    d["l0_wropeq"] = f(inputs["L0_ffn_W_ropeq"])
    d["l0_wropeqp"] = _perm_rope_cols(f(inputs["L0_ffn_W_ropeq"]))
    d["l0_wropek"] = f(inputs["L0_ffn_W_ropek"])
    d["l0_wropekp"] = _perm_rope_cols(f(inputs["L0_ffn_W_ropek"]))
    d["l0_wo"] = f(inputs["L0_ffn_W_o"])
    d["l0_wgs"] = f(inputs["L0_ffn_Wg_s"])
    d["l0_wus"] = f(inputs["L0_ffn_Wu_s"])
    d["l0_wds"] = f(inputs["L0_ffn_Wd_s"])
    d["l0_wgr"] = f(inputs["L0_ffn_Wg_r"]).reshape(7 * 1024, 1024)
    d["l0_wur"] = f(inputs["L0_ffn_Wu_r"]).reshape(7 * 1024, 1024)
    d["l0_wdr"] = f(inputs["L0_ffn_Wd_r"]).reshape(7 * 1024, 1024)
    d["l0_wrouter"] = f(inputs["L0_ffn_W_router"])
    d["l0_rbias"] = f(inputs["L0_ffn_routing_bias"])
    return d


def _want_inputs(nc):
    want = set()
    for a in nc.m.functions[0].allocations:
        if isinstance(a, mybir.MemoryLocationSet) and a.kind == "ExternalInput":
            want.add(a.memorylocations[0].name)
    want.discard("partition_id")
    return want


_BUILD_CACHE = {}


def _get_nc(name, builder):
    if name not in _BUILD_CACHE:
        nc = builder()
        _BUILD_CACHE[name] = (nc, _want_inputs(nc))
    return _BUILD_CACHE[name]


def build_all():
    lb = LB("mk")
    P = lb.P
    x0 = lb.dram_in("x0", (T, D), F32)
    xout = lb.dram_out("xout", (T, D), F32)
    x3spill = lb.dram_tmp("x3spill", (D, T), F32)
    lb.consts()

    lb.push_scope()
    stream = [lb.sb("stream", (128, T), F32) for _ in range(NDT)]
    lb.push_scope()
    cur = emit_l0(lb, x0, stream)
    lb.pop_scope()
    lb.push_scope()
    cur = emit_l1(lb, stream, cur)
    lb.pop_scope()
    lb.push_scope()
    cur = emit_l2(lb, stream, cur)
    lb.pop_scope()
    for i in range(NDT):
        lb.P.dma("sync", x3spill[i * 128:(i + 1) * 128, :], stream[i][:],
                 waits=[cur[i]])
    lb.pop_scope()

    lb.push_scope()
    emit_l3(lb, x3spill, xout)
    lb.pop_scope()
    lb.finalize()
    return lb.nc


def kernel(**inputs):
    NC = 8
    x = np.ascontiguousarray(np.asarray(inputs["L0_attn_x"],
                                        dtype=np.float32))
    nc, want = _get_nc("all", build_all)
    base = {}
    base.update(prep_l0(inputs))
    base.update(prep_l1(inputs))
    base.update(prep_l3(inputs))
    maps = []
    for c in range(NC):
        m = dict(base)
        m.update(prep_l2(inputs, core=c))
        m = {k: v for k, v in m.items() if k in want}
        m["x0"] = np.ascontiguousarray(x[c])
        assert set(m) == want, want - set(m)
        maps.append(m)
    r = run_bass_kernel_spmd(nc, maps, core_ids=list(range(NC)))
    out = np.stack([r.results[c]["xout"] for c in range(NC)], axis=0)
    return np.ascontiguousarray(out.astype(np.float32))



def emit_l1(lb, acc, acc_toks):
    P = lb.P
    xspill = lb.dram_tmp("l1_xspill", (D, T))
    cumpad_d = lb.dram_tmp("l1_cumpad", (16, 2 * T), BF16)
    jm_t, jm_tok_l = lb.load_w("c_jmat", (128, 128), BF16)
    jmat, jm_tok = jm_t[0], jm_tok_l[0]

    acc_toks = list(acc_toks)
    xnb = [lb.sb("xnb", (128, T), BF16) for _ in range(NDT)]

    lb.push_scope()
    ln_tmp = lb.sb("ln_tmp", (128, SLAB), F32)
    ln_prev = [None]

    def ln_wb(i, s0, s1, psA, psB, bt):
        ta = P.vec("tensor_mul", ln_tmp[:], acc[i][:, s0:s1],
                   psA[:, 0:SLAB], waits=[acc_toks[i], bt, ln_prev[0]])
        tb = P.vec("tensor_add", xnb[i][:, s0:s1], ln_tmp[:],
                   psB[:, 0:SLAB], waits=[ta])
        ln_prev[0] = tb
        return tb

    xn_toks = lb.norm_fm(acc, acc_toks, "ln", ln_wb)
    lb.pop_scope()

    bmax_row = lb.sb("bmax_row", (1, 16), F32)
    lb.push_scope()
    rpe_t, rpe_tokl = lb.load_w("l1_rpe", (16, T), F32, rowsplit=16)
    cum0, cum1 = rpe_t[0], lb.sb("cum1", (16, T), F32)
    ct = P.vec("tensor_scalar_mul", cum0[:], cum0[:], 8.0,
               waits=[rpe_tokl[0]])
    src, dst = cum0, cum1
    k = 1
    while k < T:
        t1 = P.vec("tensor_copy", dst[:, 0:k], src[:, 0:k], waits=[ct])
        ct = P.vec("tensor_add", dst[:, k:T], src[:, k:T], src[:, 0:T - k],
                   waits=[t1])
        src, dst = dst, src
        k *= 2
    cum = src
    cumpad = lb.sb("cumpad", (16, 2 * T), BF16)
    t1 = P.vec("memset", cumpad[:], -240000.0)
    t2 = P.vec("tensor_copy", cumpad[:, T - 1:2 * T - 1], cum[:],
               waits=[t1, ct])
    cp_tok = P.dma("sync", cumpad_d, cumpad[:], waits=[t2])
    bmax_col = lb.sb("bmax_col", (16, 1), F32)
    t3 = P.vec("reduce_max", bmax_col[:, :, None], cum[:, None, :],
               axis=AX.X, waits=[ct])
    ps, war, idx = lb.pg.acquire()
    t4 = P.tp(ps[0:1, 0:16], bmax_col[:], lb.identf[0:16, 0:16],
              waits=[t3, lb.tok_identf, war], inc=True)
    t5 = P.vec("tensor_copy", bmax_row[:], ps[0:1, 0:16], waits=[t4])
    lb.pg.set_last_use(idx, t5)
    bmax_tok = t5
    lb.pop_scope()

    cur_toks = list(acc_toks)

    for hg in range(2):
        lb.push_scope()
        KT = [lb.sb("KT", (64, 2 * T), BF16) for _ in range(4)]
        QT = [lb.sb("QT", (64, 2 * T), BF16) for _ in range(4)]
        V = [lb.sb("V", (128, 520), BF16) for _ in range(T // 128)]
        KT_toks = [None] * 4
        QT_toks = [None] * 4
        V_toks = [None] * (T // 128)
        lb.push_scope()
        wq, wq_tok = lb.load_w("l1_wq", (D, D), col0=hg * 512,
                               col1=hg * 512 + 512)
        wk, wk_tok = lb.load_w("l1_wk", (D, D), col0=hg * 512,
                               col1=hg * 512 + 512)
        wv, wv_tok = lb.load_w("l1_wv", (D, D), col0=hg * 512,
                               col1=hg * 512 + 512)

        def evac_kq(dst, dtoks):
            def evac(oc, s0, s1, ps, mm_tok):
                c1 = P.vec("tensor_copy", dst[oc][0:64, s0:s1],
                           ps[0:64, 0:s1 - s0], waits=[mm_tok])
                c2 = P.vec("tensor_copy", dst[oc][0:64, T + s0:T + s1],
                           ps[64:128, 0:s1 - s0], waits=[mm_tok])
                dtoks[oc] = c2
                return c2
            return evac

        lb.proj(wk, wk_tok, xnb, xn_toks, 512, evac_kq(KT, KT_toks))
        lb.proj(wq, wq_tok, xnb, xn_toks, 512, evac_kq(QT, QT_toks))
        for tt in range(T // 128):
            ms = P.vec("memset",
                       V[tt].rearrange("p (h c) -> p h c", c=65)[:, :, 64:65],
                       1.0, waits=[V_toks[tt]])
            ps, war, idx = lb.pg.acquire()
            mmv = None
            for kc in range(NDT):
                mmv = P.mm(ps[0:128, 0:SLAB],
                           xnb[kc][:, tt * 128:(tt + 1) * 128],
                           wv[kc][:], kc == 0, kc == NDT - 1,
                           waits=[xn_toks[kc], wv_tok[kc],
                                  war if kc == 0 else None],
                           inc=(kc == NDT - 1))
            dview = V[tt].rearrange("p (h c) -> p h c", c=65)[:, 0:8, 0:64]
            sview = ps[0:128, 0:SLAB].rearrange("p (h c) -> p h c", c=64)
            cv = P.vec("tensor_copy", dview, sview, waits=[mmv, ms])
            lb.pg.set_last_use(idx, cv)
            V_toks[tt] = cv
        lb.pop_scope()

        wo4, wo4_tok = lb.load_w("l1_wout", (D, D), row0=hg * 512,
                                 row1=hg * 512 + 512)
        negU = lb.sb("negU", (128, 8), F32)
        negU_toks = [None] * 8
        lb.push_scope()
        sqt = lb.sb("sqt", (64, 2 * T), BF16)
        m12 = [lb.sb("mkq", (1, 1), F32) for _ in range(2)]
        mtmp = lb.sb("mtmp", (1, 1), F32)
        uval = lb.sb("uval", (1, 1), F32)
        sq_war = [None]
        for h in range(8):
            g4, off = h // 2, (h % 2) * T
            toks_nrm = []
            for j, (MAT, MT) in enumerate(((KT, KT_toks), (QT, QT_toks))):
                sq_tok = P.act(sqt[0:64, 0:T], MAT[g4][0:64, off:off + T],
                               AF.Square,
                               waits=[MT[g4], sq_war[0]] + toks_nrm)
                mj = P.vec("memset", m12[j][:], 0.0, waits=toks_nrm)
                ps, war, idx = lb.pg.acquire()
                for s in range(NSLAB):
                    mmn = P.mm(ps[0:1, 0:SLAB], lb.onescol_b[0:64, :],
                               sqt[0:64, s * SLAB:(s + 1) * SLAB],
                               True, True,
                               waits=[sq_tok, lb.tok_onescol_b,
                                      war if s == 0 else None,
                                      toks_nrm[-1] if toks_nrm else None],
                               inc=True)
                    c1 = P.vec("reduce_max", mtmp[:, :, None],
                               ps[0:1, None, 0:SLAB], axis=AX.X,
                               waits=[mmn, mj] + toks_nrm[-1:])
                    c2 = P.vec("tensor_max", m12[j][:], m12[j][:], mtmp[:],
                               waits=[c1])
                    toks_nrm.append(c2)
                lb.pg.set_last_use(idx, toks_nrm[-1])
                sq_war[0] = toks_nrm[-1]
            t2 = P.vec("tensor_mul", m12[0][:], m12[0][:], m12[1][:],
                       waits=toks_nrm)
            t3 = P.act(uval[:], m12[0][:], AF.Ln, waits=[t2])
            t4 = P.act(uval[:], uval[:], AF.Exp, scale=0.5, waits=[t3])
            t4b = P.vec("tensor_add", uval[:], uval[:],
                        bmax_row[0:1, hg * 8 + h:hg * 8 + h + 1],
                        waits=[t4, bmax_tok])
            t5 = P.vec("tensor_scalar_mul", uval[:], uval[:], -0.125,
                       waits=[t4b])
            ps, war, idx = lb.pg.acquire()
            t6 = P.mm(ps[0:128, 0:1], lb.onesrow_f[:], uval[:], True, True,
                      waits=[t5, lb.tok_onesf, war], inc=True)
            t7 = P.vec("tensor_copy", negU[:, h:h + 1], ps[0:128, 0:1],
                       waits=[t6])
            lb.pg.set_last_use(idx, t7)
            negU_toks[h] = t7
        lb.pop_scope()

        P.barrier()
        poolS = PsumPool(None, None, "sS", 0,
                         tiles=lb.pg.tiles[0:3] + lb.pg.tiles[6:7])
        poolPV = PsumPool(None, None, "sPV", 0, tiles=lb.pg.tiles[3:5])
        poolN = PsumPool(None, None, "sN", 0, tiles=lb.pg.tiles[5:6])
        NPB = 4
        Pb = [lb.sb("Pb", (128, SLAB), BF16) for _ in range(NPB)]
        pb_war = [None] * NPB
        Hb = [lb.sb("Hb", (128, SLAB), BF16) for _ in range(4)]
        hb_war = [None] * 4
        hb_i = [0]
        yn4 = [lb.sb("yn4", (128, SLAB), BF16) for _ in range(4)]
        yn_toks = [None] * 8
        yn_war = [None] * 8
        bcn = [lb.sb("bcn", (64, SLAB), F32) for _ in range(2)]
        bcn_war = [None, None]
        rrow = [lb.sb("rrow", (1, SLAB), F32) for _ in range(2)]
        rrow_war = [None, None]
        for qs in range(NSLAB):
            q0, q1 = qs * SLAB, (qs + 1) * SLAB
            for h in range(8):
                g4, off = h // 2, (h % 2) * T
                gh = hg * 8 + h
                pspv, warpv, ipv = poolPV.acquire()
                nkc = 4 * (qs + 1)
                pvmm = None
                pend = None
                for kc in range(nkc):
                    hbuf = Hb[hb_i[0] % 4]
                    base = q0 - (kc * 128) - 127 + T - 1
                    hsrc = bass.AP(cumpad_d.tensor, gh * 2 * T + base,
                                   [[1, 128], [1, SLAB]])
                    hb_tok = P.dma("sync" if kc % 2 == 0 else "gpsimd",
                                   hbuf[:], hsrc,
                                   waits=[cp_tok, hb_war[hb_i[0] % 4]])
                    psS, warS, iS = poolS.acquire()
                    smm = P.mm(psS[0:128, 0:SLAB],
                               KT[g4][0:64, off + kc * 128:
                                      off + (kc + 1) * 128],
                               QT[g4][0:64, off + q0:off + q1], True, False,
                               waits=[KT_toks[g4], QT_toks[g4], warS])
                    smm = P.mm(psS[0:128, 0:SLAB], jmat[:], hbuf[:],
                               False, True,
                               waits=[jm_tok, hb_tok], inc=True)
                    pb = Pb[kc % NPB]
                    e = P.act(pb[:], psS[0:128, 0:SLAB], AF.Exp,
                              scale=0.125, bias=negU[:, h:h + 1],
                              waits=[smm, negU_toks[h], pb_war[kc % NPB]])
                    poolS.set_last_use(iS, e)
                    hb_war[hb_i[0] % 4] = e
                    hb_i[0] += 1
                    if pend is not None:
                        pkc, pptok = pend
                        pvmm = P.mm(pspv[0:65, 0:SLAB],
                                    V[pkc][:, h * 65:(h + 1) * 65],
                                    Pb[pkc % NPB][:], pkc == 0, False,
                                    waits=[V_toks[pkc], pptok,
                                           warpv if pkc == 0 else None],
                                    inc=True)
                        pb_war[pkc % NPB] = pvmm
                    pend = (kc, e)
                pkc, pptok = pend
                pvmm = P.mm(pspv[0:65, 0:SLAB],
                            V[pkc][:, h * 65:(h + 1) * 65],
                            Pb[pkc % NPB][:], pkc == 0, True,
                            waits=[V_toks[pkc], pptok,
                                   warpv if pkc == 0 else None],
                            inc=True)
                pb_war[pkc % NPB] = pvmm
                rr = rrow[h % 2]
                tl = P.act(rr[:], pspv[64:65, 0:SLAB], AF.Ln,
                           waits=[pvmm, rrow_war[h % 2]])
                t1 = P.act(rr[:], rr[:], AF.Exp, scale=-1.0, waits=[tl])
                psn, warn, inn = poolN.acquire()
                t2 = P.mm(psn[0:64, 0:SLAB], lb.onesrow_f[0:1, 0:64], rr[:],
                          True, True, waits=[t1, warn, lb.tok_onesf],
                          inc=True)
                rrow_war[h % 2] = t2
                bc = bcn[h % 2]
                t3 = P.vec("tensor_copy", bc[:], psn[0:64, 0:SLAB],
                           waits=[t2, bcn_war[h % 2]])
                poolN.set_last_use(inn, t3)
                hb2 = (h % 2) * 64
                t4 = P.vec("tensor_mul", yn4[g4][hb2:hb2 + 64, :],
                           pspv[0:64, 0:SLAB], bc[:],
                           waits=[t3, yn_war[h]])
                poolPV.set_last_use(ipv, t4)
                bcn_war[h % 2] = t4
                yn_toks[h] = t4
            last_wo = None
            for oc in range(NDT):
                psO, warO, iO = poolS.acquire()
                mmo = None
                for g4 in range(4):
                    mmo = P.mm(psO[0:128, 0:SLAB],
                               wo4[g4][:, oc * 128:(oc + 1) * 128],
                               yn4[g4][:], g4 == 0, g4 == 3,
                               waits=[wo4_tok[g4], yn_toks[2 * g4],
                                      yn_toks[2 * g4 + 1],
                                      warO if g4 == 0 else None],
                               inc=(g4 == 3))
                at = P.vec("tensor_add", acc[oc][:, q0:q1],
                           acc[oc][:, q0:q1], psO[0:128, 0:SLAB],
                           waits=[mmo, cur_toks[oc]])
                poolS.set_last_use(iO, at)
                cur_toks[oc] = at
                last_wo = at
            for h in range(8):
                yn_war[h] = last_wo
        P.barrier()
        lb.pop_scope()

    spill_toks = []
    for i in range(NDT):
        spill_toks.append(P.dma("sync", xspill[i * 128:(i + 1) * 128, :],
                                acc[i][:], waits=[cur_toks[i]]))

    lb.push_scope()
    ysb = [lb.sb("ysb", (128, T), F32) for _ in range(NDT)]
    ysb_toks = [None] * NDT
    lb.push_scope()
    wfc_ap = lb.dram_in("l1_wfcT", (D, 4096))
    wpr_ap = lb.dram_in("l1_wprojT", (4096, D))
    wfcb = [lb.sb("wfcb", (128, SLAB), BF16) for _ in range(NDT)]
    wprb = [lb.sb("wprb", (128, D), BF16) for _ in range(4)]
    hrelu = lb.sb("hrelu", (128, SLAB), F32)
    hfc = [lb.sb("hfc", (128, T), BF16) for _ in range(4)]
    hl_tok = None
    first_y = [[True] * NSLAB for _ in range(NDT)]

    def issue_wfc(fg, wait):
        return [P.dma(
            "gpsimd", wfcb[kc][:],
            wfc_ap[kc * 128:(kc + 1) * 128,
                   fg * SLAB:(fg + 1) * SLAB], waits=wait)
            for kc in range(NDT)]

    def issue_wpr(fg, wait):
        return [P.dma("gpsimd", wprb[fl][:],
                      wpr_ap[fg * SLAB + fl * 128:fg * SLAB + fl * 128 + 128,
                             :], waits=wait) for fl in range(4)]

    wfc_next = issue_wfc(0, [])
    wpr_next = issue_wpr(0, [])
    for fg in range(8):
        wfc_toks = wfc_next
        wpr_toks = wpr_next
        hfc_toks = []
        for fl in range(4):
            for s in range(NSLAB):
                s0, s1 = s * SLAB, (s + 1) * SLAB
                ps, war, idx = lb.pg.acquire()
                mmh = None
                for kc in range(NDT):
                    mmh = P.mm(ps[0:128, 0:SLAB],
                               wfcb[kc][:, fl * 128:(fl + 1) * 128],
                               xnb[kc][:, s0:s1], kc == 0, kc == NDT - 1,
                               waits=[wfc_toks[kc], xn_toks[kc],
                                      war if kc == 0 else None],
                               inc=(kc == NDT - 1))
                rt = P.act(hrelu[:], ps[0:128, 0:SLAB], AF.Relu,
                           waits=[mmh, hl_tok])
                st = P.act(hfc[fl][:, s0:s1], hrelu[:], AF.Square,
                           waits=[rt])
                lb.pg.set_last_use(idx, st)
                hl_tok = st
            hfc_toks.append(hl_tok)
        if fg < 7:
            wfc_next = issue_wfc(fg + 1, [mmh])
        for oc in range(NDT):
            for s in range(NSLAB):
                s0, s1 = s * SLAB, (s + 1) * SLAB
                ps, war, idx = lb.pg.acquire()
                mmy = None
                for fl in range(4):
                    mmy = P.mm(ps[0:128, 0:SLAB],
                               wprb[fl][:, oc * 128:(oc + 1) * 128],
                               hfc[fl][:, s0:s1], fl == 0, fl == 3,
                               waits=[wpr_toks[fl], hfc_toks[fl],
                                      war if fl == 0 else None],
                               inc=(fl == 3))
                if first_y[oc][s]:
                    yt = P.vec("tensor_copy", ysb[oc][:, s0:s1],
                               ps[0:128, 0:SLAB], waits=[mmy, ysb_toks[oc]])
                    first_y[oc][s] = False
                else:
                    yt = P.vec("tensor_add", ysb[oc][:, s0:s1],
                               ysb[oc][:, s0:s1], ps[0:128, 0:SLAB],
                               waits=[mmy, ysb_toks[oc]])
                lb.pg.set_last_use(idx, yt)
                ysb_toks[oc] = yt
        if fg < 7:
            wpr_next = issue_wpr(fg + 1, [P.now("tensor")])
    lb.pop_scope()
    mem = [lb.sb("mem", (128, SLAB), F32) for _ in range(2)]
    spk = [lb.sb("spk", (128, SLAB), F32) for _ in range(2)]
    ssum = [lb.sb("ssum", (128, SLAB), F32) for _ in range(2)]
    arel = [lb.sb("arel", (128, SLAB), F32) for _ in range(2)]
    arel_tok = [None, None]
    prev_out = [None, None]
    bi = 0
    for oc in range(NDT):
        for s in range(NSLAB):
            s0, s1 = s * SLAB, (s + 1) * SLAB
            par = bi % 2
            EV = P.vec
            me, sp, su = mem[par], spk[par], ssum[par]
            ab = arel[par]
            ld = P.dma("sync", ab[:], xspill[oc * 128:(oc + 1) * 128, s0:s1],
                       waits=[spill_toks[oc], arel_tok[par]])
            y = ysb[oc]
            t1 = EV("tensor_single_scalar", su[:], y[:, s0:s1], 1.0,
                    op=ALU.is_gt, waits=[ysb_toks[oc], prev_out[par]])
            t2 = EV("tensor_copy", sp[:], su[:], waits=[t1])
            t3 = EV("tensor_copy", me[:], y[:, s0:s1],
                    waits=[ysb_toks[oc], prev_out[par]])
            for step in range(4):
                t4 = EV("scalar_tensor_tensor", me[:], me[:], 0.9,
                        y[:, s0:s1], op0=ALU.mult, op1=ALU.add,
                        waits=[t3, t2])
                t5 = EV("tensor_sub", me[:], me[:], sp[:], waits=[t4])
                t6 = EV("tensor_single_scalar", sp[:], me[:], 1.0,
                        op=ALU.is_gt, waits=[t5])
                t7 = EV("tensor_add", su[:], su[:], sp[:],
                        waits=[t6])
                t3 = t7
                t2 = t7
            t8 = EV("scalar_tensor_tensor", acc[oc][:, s0:s1], su[:],
                    0.2, ab[:], op0=ALU.mult, op1=ALU.add,
                    waits=[t3, ld, spill_toks[oc], cur_toks[oc]])
            arel_tok[par] = t8
            prev_out[par] = t8
            cur_toks[oc] = t8
            bi += 1
    lb.pop_scope()
    return cur_toks


def prep_l1(inputs):
    f = lambda a: np.ascontiguousarray(np.asarray(a, dtype=np.float32))
    d = {}
    d["c_ident"] = np.eye(128, dtype=np.float32)
    d["c_jmat"] = np.eye(128, dtype=np.float32)[::-1].copy()
    qkv = f(inputs["L1_attn_qkv"])
    d["l1_wq"] = f(qkv[:, 0].reshape(D, D))
    d["l1_wk"] = f(qkv[:, 1].reshape(D, D))
    d["l1_wv"] = f(qkv[:, 2].reshape(D, D))
    d["l1_wout"] = f(inputs["L1_attn_out_w"]).reshape(D, D)
    d["l1_rpe"] = f(inputs["L1_attn_rpe_bias"])[:, :T]
    d["l1_wfcT"] = f(np.asarray(inputs["L1_ffn_W_fc"]).T)
    d["l1_wprojT"] = f(np.asarray(inputs["L1_ffn_W_proj"]).T)
    return d



def emit_l2(lb, acc, acc_toks):
    P = lb.P
    rmT_d = lb.dram_tmp("l2_rmT", (D, T), BF16)
    acc_toks = list(acc_toks)

    xnb = [lb.sb("xnb_l2", (128, T), BF16) for _ in range(NDT)]

    lb.push_scope()
    ln_tmp = lb.sb("ln_tmp", (128, SLAB), F32)
    ln_prev = [None]
    xn_toks = [None] * NDT

    def ln_wb(i, s0, s1, psA, psB, bt):
        ta = P.vec("tensor_mul", ln_tmp[:], acc[i][:, s0:s1],
                   psA[:, 0:SLAB], waits=[acc_toks[i], bt, ln_prev[0]])
        tb = P.vec("tensor_add", xnb[i][:, s0:s1], ln_tmp[:],
                   psB[:, 0:SLAB], waits=[ta])
        ln_prev[0] = tb
        return tb

    xn_toks = lb.norm_fm(acc, acc_toks, "ln", ln_wb)
    lb.pop_scope()

    lb.push_scope()
    lb.P.nodummy = True
    triu_t, triu_tokl = lb.load_w("c_triu", (128, 128), BF16)
    triu, triu_tok = triu_t[0], triu_tokl[0]
    ew1 = lb.sb("ew1", (1, D), F32)
    e127 = lb.sb("e127", (1, D), F32)
    e128 = lb.sb("e128", (1, D), F32)
    EWneg = lb.sb("EWneg", (128, D), BF16)
    EWu = lb.sb("EWu", (128, D), BF16)
    lb.push_scope()
    iota_r, iota_tok = lb.load_row("c_iota128", 128)
    iom1_r, iom1_tok = lb.load_row("c_iotam1", 128)
    td_r, td_tok = lb.load_row("l2_td", D)
    tf_r, tf_tok = lb.load_row("l2_tf", D)
    wrow = lb.sb("wrow_l2", (1, D), F32)
    t1 = P.act(wrow[:], td_r[:], AF.Exp, waits=[td_tok])
    t1 = P.vec("tensor_scalar_mul", wrow[:], wrow[:], -1.0, waits=[t1])
    t1b = P.act(ew1[:], wrow[:], AF.Exp, waits=[t1])
    t2 = P.vec("tensor_scalar_mul", e127[:], wrow[:], 127.0, waits=[t1])
    t2 = P.act(e127[:], e127[:], AF.Exp, waits=[t2])
    t3 = P.vec("tensor_scalar_mul", e128[:], wrow[:], 128.0, waits=[t1])
    t3 = P.act(e128[:], e128[:], AF.Exp, waits=[t3])
    ew_toks = []
    for half in range(2):
        c0, c1_ = half * 512, (half + 1) * 512
        ps, war, idx = lb.pg.acquire()
        mm1 = P.mm(ps[0:128, 0:SLAB], iota_r[0:1, :], wrow[0:1, c0:c1_],
                   True, True, waits=[iota_tok, t1, war], inc=True)
        a1 = P.act(EWneg[:, c0:c1_], ps[0:128, 0:SLAB], AF.Exp, scale=-1.0,
                   waits=[mm1])
        ps2, war2, idx2 = lb.pg.acquire()
        P.mm(ps2[0:128, 0:SLAB], iom1_r[0:1, :], wrow[0:1, c0:c1_],
             True, False, waits=[iom1_tok, t1, war2])
        mm2 = P.mm(ps2[0:128, 0:SLAB], lb.onesrow_f[:], tf_r[0:1, c0:c1_],
                   False, True, waits=[tf_tok, lb.tok_onesf], inc=True)
        a2 = P.act(EWu[:, c0:c1_], ps2[0:128, 0:SLAB], AF.Exp, waits=[mm2])
        lb.pg.set_last_use(idx, a1)
        lb.pg.set_last_use(idx2, a2)
        ew_toks += [a1, a2]
    lb.pop_scope()
    aA = lb.sb("aA", (1, D), F32)
    aB = lb.sb("aB", (1, D), F32)
    cwA = lb.sb("cwA", (1, D), BF16)
    cwB = lb.sb("cwB", (1, D), BF16)
    sumA = lb.sb("sumA", (1, D), F32)
    sumB = lb.sb("sumB", (1, D), F32)
    cA_tok = P.vec("memset", aA[:], 0.0)
    cB_tok = P.vec("memset", aB[:], 0.0)
    zA = lb.sb("zA", (128, D), BF16)
    zB = lb.sb("zB", (128, D), BF16)
    numb = lb.sb("numb", (128, SLAB), F32)
    denb = lb.sb("denb", (128, SLAB), F32)
    rm_c = lb.sb("rm_c", (128, D), BF16)
    rm_sp = [lb.sb("rm_sp", (128, 128), BF16) for _ in range(2)]
    z_war = [None, None]
    prev_nd = None
    rm_war = [None]
    st_war = [None, None]
    CHP = 4

    def phase_open(ch0):
        WPH = CHP * 128
        t0_, t1_ = ch0 * 128, ch0 * 128 + WPH
        lb.push_scope()
        ekL = [lb.sb("ek", (128, D), BF16) for _ in range(CHP)]
        ekvL = [lb.sb("ekv", (128, D), BF16) for _ in range(CHP)]
        ekL_t = [None] * CHP
        ekvL_t = [None] * CHP

        def mix_and_proj(tm_name, w_name, out_tiles, out_toks, act):
            lb.push_scope()
            tm_col, tm_tok = lb.load_col(tm_name, D)
            wT, wT_tok = lb.load_w(w_name, (D, D))
            xm = [lb.sb("xm", (128, WPH), BF16) for _ in range(NDT)]
            xm_toks = []
            for i in range(NDT):
                if t0_ == 0:
                    d1 = P.vec("tensor_sub", xm[i][:, 1:WPH],
                               xnb[i][:, 1:WPH], xnb[i][:, 0:WPH - 1],
                               waits=[xn_toks[i]])
                    m1 = P.vec("scalar_tensor_tensor", xm[i][:, 1:WPH],
                               xm[i][:, 1:WPH], tm_col[i][:],
                               xnb[i][:, 0:WPH - 1],
                               op0=ALU.mult, op1=ALU.add,
                               waits=[d1, tm_tok[i]])
                    m2 = P.vec("tensor_scalar", xm[i][:, 0:1],
                               xnb[i][:, 0:1], tm_col[i][:], None,
                               op0=ALU.mult, waits=[m1])
                else:
                    d1 = P.vec("tensor_sub", xm[i][:],
                               xnb[i][:, t0_:t1_],
                               xnb[i][:, t0_ - 1:t1_ - 1],
                               waits=[xn_toks[i]])
                    m2 = P.vec("scalar_tensor_tensor", xm[i][:],
                               xm[i][:], tm_col[i][:],
                               xnb[i][:, t0_ - 1:t1_ - 1],
                               op0=ALU.mult, op1=ALU.add,
                               waits=[d1, tm_tok[i]])
                xm_toks.append(m2)
            for lc in range(CHP):
                for half in range(2):
                    ps, war, idx = lb.pg.acquire()
                    mmv = None
                    for kc in range(NDT):
                        mmv = P.mm(ps[0:128, 0:SLAB],
                                   xm[kc][:, lc * 128:(lc + 1) * 128],
                                   wT[kc][:, half * 512:(half + 1) * 512],
                                   kc == 0, kc == NDT - 1,
                                   waits=[xm_toks[kc], wT_tok[kc],
                                          war if kc == 0 else None],
                                   inc=(kc == NDT - 1))
                    dst = out_tiles[lc][:, half * 512:(half + 1) * 512]
                    if act == "exp":
                        ev = P.act(dst, ps[0:128, 0:SLAB], AF.Exp,
                                   waits=[mmv, out_toks[lc]])
                    else:
                        ev = P.vec("tensor_copy", dst, ps[0:128, 0:SLAB],
                                   waits=[mmv, out_toks[lc]])
                    lb.pg.set_last_use(idx, ev)
                    out_toks[lc] = ev
            lb.pop_scope()

        mix_and_proj("l2_tmk", "l2_wkT", ekL, ekL_t, "exp")
        mix_and_proj("l2_tmv", "l2_wvT", ekvL, ekvL_t, "copy")
        for lc in range(CHP):
            ekvL_t[lc] = P.vec("tensor_mul", ekvL[lc][:], ekL[lc][:],
                               ekvL[lc][:], waits=[ekL_t[lc], ekvL_t[lc]])
        P.barrier()
        for k in range(len(lb.pg.tiles)):
            lb.pg.last_use[k] = None
        pT = PsumPool(None, None, "wT", 0, tiles=lb.pg.tiles[0:4])
        pR = PsumPool(None, None, "wR", 0, tiles=lb.pg.tiles[4:6])
        return ({ch0 + lc: ekL[lc] for lc in range(CHP)},
                {ch0 + lc: ekvL[lc] for lc in range(CHP)},
                {ch0 + lc: ekL_t[lc] for lc in range(CHP)},
                {ch0 + lc: ekvL_t[lc] for lc in range(CHP)}, pT, pR)

    for ch in range(T // 128):
        if ch % CHP == 0:
            ek, ekv, ek_toks, ekv_toks, poolT, poolR = phase_open(ch)
        cw1 = P.vec("scalar_tensor_tensor", cwA[:], aA[:], 1.0, ew1[:],
                    op0=ALU.mult, op1=ALU.mult, waits=[cA_tok, t1b])
        cw2 = P.vec("scalar_tensor_tensor", cwB[:], aB[:], 1.0, ew1[:],
                    op0=ALU.mult, op1=ALU.mult, waits=[cB_tok, t1b])
        zt1 = P.vec("tensor_mul", zA[:], ekv[ch][:], EWneg[:],
                    waits=[ekv_toks[ch], z_war[0]] + ew_toks)
        zt2 = P.vec("tensor_mul", zB[:], ek[ch][:], EWneg[:],
                    waits=[ek_toks[ch], z_war[1]] + ew_toks)
        sum_toks = []
        nd_last = None
        for half in range(2):
            c0, c1_ = half * 512, (half + 1) * 512
            psA_, warA, idxA = poolT.acquire()
            P.mm(psA_[0:128, 0:SLAB], triu[:], zA[:, c0:c1_], True, False,
                 waits=[triu_tok, zt1, warA])
            mmA = P.mm(psA_[0:128, 0:SLAB], lb.onesrow_b[:],
                       cwA[0:1, c0:c1_], False, True,
                       waits=[cw1, lb.tok_ones], inc=True)
            psB_, warB, idxB = poolT.acquire()
            P.mm(psB_[0:128, 0:SLAB], triu[:], zB[:, c0:c1_], True, False,
                 waits=[triu_tok, zt2, warB])
            mmB = P.mm(psB_[0:128, 0:SLAB], lb.onesrow_b[:],
                       cwB[0:1, c0:c1_], False, True,
                       waits=[cw2, lb.tok_ones], inc=True)
            n1 = P.vec("tensor_mul", numb[:], psA_[0:128, 0:SLAB],
                       EWu[:, c0:c1_], waits=[mmA, prev_nd])
            n2 = P.vec("tensor_add", numb[:], numb[:], ekv[ch][:, c0:c1_],
                       waits=[n1])
            d1 = P.vec("tensor_mul", denb[:], psB_[0:128, 0:SLAB],
                       EWu[:, c0:c1_], waits=[mmB, n2])
            d2 = P.vec("tensor_add", denb[:], denb[:], ek[ch][:, c0:c1_],
                       waits=[d1])
            d3 = P.vec("reciprocal", denb[:], denb[:], waits=[d2])
            y1 = P.vec("tensor_mul", numb[:], numb[:], denb[:], waits=[d3])
            y2 = P.vec("tensor_copy", rm_c[:, c0:c1_], numb[:],
                       waits=[y1, rm_war[0]])
            prev_nd = y2
            nd_last = y2
            poolT.set_last_use(idxA, n1)
            poolT.set_last_use(idxB, d1)
            psS_, warS, idxS = poolR.acquire()
            sm1 = P.mm(psS_[0:1, 0:SLAB], lb.onescol_b[:], zA[:, c0:c1_],
                       True, True, waits=[zt1, lb.tok_onescol_b, warS],
                       inc=True)
            sv1 = P.vec("tensor_copy", sumA[0:1, c0:c1_], psS_[0:1, 0:SLAB],
                        waits=[sm1])
            psS2_, warS2, idxS2 = poolR.acquire()
            sm2 = P.mm(psS2_[0:1, 0:SLAB], lb.onescol_b[:], zB[:, c0:c1_],
                       True, True, waits=[zt2, lb.tok_onescol_b, warS2],
                       inc=True)
            sv2 = P.vec("tensor_copy", sumB[0:1, c0:c1_], psS2_[0:1, 0:SLAB],
                        waits=[sm2])
            poolR.set_last_use(idxS, sv1)
            poolR.set_last_use(idxS2, sv2)
            sum_toks += [sv1, sv2]
        z_war[0] = sum_toks[-2]
        z_war[1] = sum_toks[-1]
        tp_last = None
        for i in range(NDT):
            pst, wart, idxt = lb.psbf.acquire()
            mt = P.tp(pst[0:128, 0:128], rm_c[:, i * 128:(i + 1) * 128],
                      lb.identb[:],
                      waits=[nd_last, lb.tok_identb, wart], inc=True)
            bi2 = (ch * NDT + i) % 2
            sbf = rm_sp[bi2]
            cv = P.vec("tensor_copy", sbf[:], pst[0:128, 0:128],
                       waits=[mt, st_war[bi2]])
            lb.psbf.set_last_use(idxt, cv)
            st = P.dma("sync",
                       bass.AP(rmT_d.tensor, (i * 128) * T + ch * 128,
                               [[T, 128], [1, 128]]),
                       sbf[:], waits=[cv])
            st_war[bi2] = st
            tp_last = cv
            lb.out_dma_toks.append(st)
        rm_war[0] = tp_last
        u1 = P.vec("tensor_mul", sumA[:], sumA[:], e127[:],
                   waits=sum_toks + [t2])
        u2 = P.vec("scalar_tensor_tensor", aA[:], aA[:], 1.0, e128[:],
                   op0=ALU.mult, op1=ALU.mult, waits=[t3, cw1, u1])
        cA_tok = P.vec("tensor_add", aA[:], aA[:], sumA[:], waits=[u2])
        u3 = P.vec("tensor_mul", sumB[:], sumB[:], e127[:],
                   waits=sum_toks + [t2])
        u4 = P.vec("scalar_tensor_tensor", aB[:], aB[:], 1.0, e128[:],
                   op0=ALU.mult, op1=ALU.mult, waits=[cw2, u3])
        cB_tok = P.vec("tensor_add", aB[:], aB[:], sumB[:], waits=[u4])
        if ch % CHP == CHP - 1:
            lb.pop_scope()
    lb.pop_scope()
    lb.P.nodummy = False

    lb.push_scope()
    rmT = [lb.sb("rmTs", (128, T), BF16) for _ in range(NDT)]
    rmT_toks = lb.load_x_fm(rmT_d, rmT)
    lb.push_scope()
    tmr_col, tmr_tok = lb.load_col("l2_tmr", D)
    wrT, wrT_tok = lb.load_w("l2_wrT", (D, D))
    xm_r = [lb.sb("xmr", (128, T), BF16) for _ in range(NDT)]
    dbuf_r = lb.sb("dmixr", (128, T), F32)
    xm_toks = []
    dprev = None
    for i in range(NDT):
        d1 = P.vec("tensor_sub", dbuf_r[:, 1:T], xnb[i][:, 1:T],
                   xnb[i][:, 0:T - 1], waits=[xn_toks[i], dprev])
        m1 = P.vec("scalar_tensor_tensor", xm_r[i][:, 1:T], dbuf_r[:, 1:T],
                   tmr_col[i][:], xnb[i][:, 0:T - 1],
                   op0=ALU.mult, op1=ALU.add, waits=[d1, tmr_tok[i]])
        m2 = P.vec("tensor_scalar", xm_r[i][:, 0:1], xnb[i][:, 0:1],
                   tmr_col[i][:], None, op0=ALU.mult, waits=[m1])
        xm_toks.append(m2)
        dprev = m1
    sgtmp = lb.sb("sgtmp", (128, SLAB), BF16)
    sg_war = [None]

    def evac_gate(oc, s0, s1, ps, mm_tok):
        sg = P.act(sgtmp[:], ps[0:128, 0:s1 - s0], AF.Sigmoid,
                   waits=[mm_tok, sg_war[0]])
        tok = P.vec("tensor_mul", rmT[oc][:, s0:s1], rmT[oc][:, s0:s1],
                    sgtmp[:], waits=[sg, rmT_toks[oc]])
        sg_war[0] = tok
        rmT_toks[oc] = tok
        return tok

    lb.proj(wrT, wrT_tok, xm_r, xm_toks, D, evac_gate)
    lb.pop_scope()

    lb.push_scope()
    woT, woT_tok = lb.load_w("l2_woT", (D, D))

    def evac_out(oc, s0, s1, ps, mm_tok):
        tok = P.vec("tensor_add", acc[oc][:, s0:s1], acc[oc][:, s0:s1],
                    ps[0:128, 0:s1 - s0], waits=[mm_tok, acc_toks[oc]])
        acc_toks[oc] = tok
        return tok

    lb.proj(woT, woT_tok, rmT, rmT_toks, D, evac_out)
    lb.pop_scope()
    lb.pop_scope()

    emit_l2_moe(lb, xnb, xn_toks, acc, acc_toks)
    return acc_toks


def emit_l2_moe(lb, xnb, xn_toks, acc, acc_toks):
    P = lb.P
    E = 8
    lb.push_scope()
    ridx_ap = lb.dram_in("l2_ridx", (T, 2), I32)
    rw_ap = lb.dram_in("l2_rw", (T, E))
    mwrow = [lb.sb("mwrow", (1, T), BF16) for _ in range(E)]
    mwrow_toks = [None] * E
    lb.push_scope()
    idx_i = lb.sb("idx_i", (128, 2), I32)
    idx_f = lb.sb("idx_f", (128, 2), F32)
    rw_t = lb.sb("rw_t", (128, E), F32)
    junk2 = lb.sb("junk2", (128, 2), F32)
    cnt = lb.sb("cnt", (128, 1), F32)
    mw_t = lb.sb("mw_t", (128, E), F32)
    for tt in range(T // 128):
        t0 = tt * 128
        ld1 = P.dma("sync", idx_i[:], ridx_ap[t0:t0 + 128, :],
                    waits=[P.now("vector")])
        ld2 = P.dma("sync", rw_t[:], rw_ap[t0:t0 + 128, :],
                    waits=[P.now("vector")])
        cv = P.vec("tensor_copy", idx_f[:], idx_i[:], waits=[ld1])
        prev = cv
        for e in range(E):
            t1 = P.vec("tensor_scalar", junk2[:], idx_f[:], float(e), 0.0,
                       op0=ALU.is_equal, op1=ALU.add, accum_out=cnt[:],
                       waits=[prev])
            t2 = P.vec("tensor_mul", mw_t[:, e:e + 1], rw_t[:, e:e + 1],
                       cnt[:], waits=[t1, ld2])
            prev = t2
        for e in range(E):
            ps2, war2, idx2 = lb.pg.acquire()
            mt2 = P.tp(ps2[0:1, 0:128], mw_t[:, e:e + 1], lb.identf[:],
                       waits=[prev, lb.tok_identf, war2], inc=True)
            cv2 = P.vec("tensor_copy", mwrow[e][0:1, t0:t0 + 128],
                        ps2[0:1, 0:128], waits=[mt2, mwrow_toks[e]])
            lb.pg.set_last_use(idx2, cv2)
            mwrow_toks[e] = cv2
            prev = cv2
    lb.pop_scope()

    gate_ap = lb.dram_in("l2_gate", (E * D, D))
    up_ap = lb.dram_in("l2_up", (E * D, D))
    down_ap = lb.dram_in("l2_down", (E * D, D))
    gb_ap = lb.dram_in("l2_gb", (E, D))
    ub_ap = lb.dram_in("l2_ub", (E, D))
    db_ap = lb.dram_in("l2_db", (E, D))
    gbuf = [lb.sb("gw2", (128, D), BF16) for _ in range(NDT)]
    ubuf = [lb.sb("uw2", (128, D), BF16) for _ in range(NDT)]
    dbuf = [lb.sb("dw2", (128, D), BF16) for _ in range(NDT)]
    gb_c = [lb.sb("gb_c", (128, 1), F32) for _ in range(NDT)]
    ub_c = [lb.sb("ub_c", (128, 1), F32) for _ in range(NDT)]
    db_r = lb.sb("db_r", (1, D), BF16)
    hmid = [lb.sb("hmid2", (128, SLAB), BF16) for _ in range(NDT)]
    g1 = [lb.sb("g1", (128, SLAB), F32) for _ in range(2)]
    s1b = [lb.sb("s1b", (128, SLAB), F32) for _ in range(2)]
    u1 = [lb.sb("u1", (128, SLAB), F32) for _ in range(2)]
    P.barrier()
    poolGU = PsumPool(None, None, "gGU", 0, tiles=lb.pg.tiles[0:4])
    poolW = PsumPool(None, None, "gW", 0, tiles=lb.pg.tiles[4:5])
    poolD = PsumPool(None, None, "gD", 0, tiles=lb.pg.tiles[5:7])

    def issue_gub(e, wait):
        gt, ut, bt = [], [], []
        for kc in range(NDT):
            r0 = e * D + kc * 128
            gt.append(P.dma("gpsimd", gbuf[kc][:],
                            gate_ap[r0:r0 + 128, :], waits=wait))
            ut.append(P.dma("gpsimd", ubuf[kc][:],
                            up_ap[r0:r0 + 128, :], waits=wait))
            bt.append(P.dma(
                "sync", gb_c[kc][:],
                bass.AP(gb_ap.tensor, e * D + kc * 128, [[1, 128], [1, 1]]),
                waits=wait))
            bt.append(P.dma(
                "sync", ub_c[kc][:],
                bass.AP(ub_ap.tensor, e * D + kc * 128, [[1, 128], [1, 1]]),
                waits=wait))
        return gt, ut, bt

    def issue_dn(e, wait):
        dt = [P.dma("gpsimd", dbuf[kc][:],
                    down_ap[e * D + kc * 128:e * D + (kc + 1) * 128, :],
                    waits=wait) for kc in range(NDT)]
        return dt, P.dma("gpsimd", db_r[:], db_ap[e:e + 1, :], waits=wait)

    gub_next = issue_gub(0, [])
    dn_next = issue_dn(0, [])
    chain_war = [None, None]
    for e in range(E):
        g_toks, u_toks, b_toks = gub_next
        d_toks, dbr_tok = dn_next
        for s in range(NSLAB):
            s0, s1_ = s * SLAB, (s + 1) * SLAB
            psw, warw, idxw = poolW.acquire()
            bw = P.mm(psw[0:128, 0:SLAB], lb.onesrow_b[:],
                      mwrow[e][0:1, s0:s1_], True, True,
                      waits=[mwrow_toks[e], lb.tok_ones, warw], inc=True)
            hm_toks = []
            for ic in range(NDT):
                psg, warg, idxg = poolGU.acquire()
                mmg = None
                for kc in range(NDT):
                    mmg = P.mm(psg[0:128, 0:SLAB],
                               gbuf[kc][:, ic * 128:(ic + 1) * 128],
                               xnb[kc][:, s0:s1_], kc == 0, kc == NDT - 1,
                               waits=[g_toks[kc], xn_toks[kc],
                                      warg if kc == 0 else None],
                               inc=(kc == NDT - 1))
                psu, waru, idxu = poolGU.acquire()
                mmu = None
                for kc in range(NDT):
                    mmu = P.mm(psu[0:128, 0:SLAB],
                               ubuf[kc][:, ic * 128:(ic + 1) * 128],
                               xnb[kc][:, s0:s1_], kc == 0, kc == NDT - 1,
                               waits=[u_toks[kc], xn_toks[kc],
                                      waru if kc == 0 else None],
                               inc=(kc == NDT - 1))
                pr = ic % 2
                war_ch = chain_war[pr]
                w1 = P.vec("tensor_scalar", g1[pr][:], psg[0:128, 0:SLAB],
                           gb_c[ic][:], 7.0, op0=ALU.add, op1=ALU.min,
                           waits=[mmg, b_toks[2 * ic], war_ch])
                poolGU.set_last_use(idxg, w1)
                w2 = P.act(s1b[pr][:], g1[pr][:], AF.Sigmoid, scale=1.702,
                           waits=[w1, war_ch])
                w3 = P.gp("tensor_mul", g1[pr][:], g1[pr][:], s1b[pr][:],
                          waits=[w2, war_ch])
                w4 = P.vec("tensor_scalar", u1[pr][:], psu[0:128, 0:SLAB],
                           ub_c[ic][:], -7.0, op0=ALU.add, op1=ALU.max,
                           waits=[mmu, b_toks[2 * ic + 1], war_ch])
                poolGU.set_last_use(idxu, w4)
                w5 = P.gp("tensor_scalar", u1[pr][:], u1[pr][:], 7.0, 1.0,
                          op0=ALU.min, op1=ALU.add, waits=[w4])
                w6 = P.gp("tensor_mul", u1[pr][:], u1[pr][:], g1[pr][:],
                          waits=[w5, w3])
                w7 = P.vec("tensor_mul", hmid[ic][:], u1[pr][:],
                           psw[0:128, 0:SLAB], waits=[w6, bw])
                hm_toks.append(w7)
                chain_war[pr] = w7
            poolW.set_last_use(idxw, hm_toks[-1])
            if s == NSLAB - 1 and e + 1 < E:
                gub_next = issue_gub(e + 1, [mmu, w4])
            for oc in range(NDT):
                psd, ward, idxd = poolD.acquire()
                mmd = None
                for ic in range(NDT):
                    mmd = P.mm(psd[0:128, 0:SLAB],
                               dbuf[ic][:, oc * 128:(oc + 1) * 128],
                               hmid[ic][:], ic == 0, False,
                               waits=[d_toks[ic], hm_toks[ic],
                                      ward if ic == 0 else None])
                mmd = P.mm(psd[0:128, 0:SLAB],
                           db_r[0:1, oc * 128:(oc + 1) * 128],
                           mwrow[e][0:1, s0:s1_], False, True,
                           waits=[dbr_tok, mwrow_toks[e]], inc=True)
                at = P.vec("tensor_add", acc[oc][:, s0:s1_],
                           acc[oc][:, s0:s1_], psd[0:128, 0:SLAB],
                           waits=[mmd, acc_toks[oc]])
                poolD.set_last_use(idxd, at)
                acc_toks[oc] = at
        if e + 1 < E:
            dn_next = issue_dn(e + 1, [P.now("tensor")])
    P.barrier()
    lb.pop_scope()


def prep_l2(inputs, core):
    f = lambda a: np.ascontiguousarray(np.asarray(a, dtype=np.float32))
    d = {}
    d["c_ident"] = np.eye(128, dtype=np.float32)
    s_ = np.arange(128)
    d["c_triu"] = (s_[:, None] < s_[None, :]).astype(np.float32)
    d["c_iota128"] = np.arange(128, dtype=np.float32)
    d["c_iotam1"] = np.arange(-1, 127, dtype=np.float32)
    d["l2_tmk"] = f(inputs["L2_attn_time_mix_k"]).reshape(D)
    d["l2_tmv"] = f(inputs["L2_attn_time_mix_v"]).reshape(D)
    d["l2_tmr"] = f(inputs["L2_attn_time_mix_r"]).reshape(D)
    d["l2_wkT"] = f(np.asarray(inputs["L2_attn_Wk"]).T)
    d["l2_wvT"] = f(np.asarray(inputs["L2_attn_Wv"]).T)
    d["l2_wrT"] = f(np.asarray(inputs["L2_attn_Wr"]).T)
    d["l2_woT"] = f(np.asarray(inputs["L2_attn_Wo"]).T)
    d["l2_td"] = f(inputs["L2_attn_time_decay"])
    d["l2_tf"] = f(inputs["L2_attn_time_first"])
    gup = np.asarray(inputs["L2_ffn_gate_up_proj"], dtype=np.float32)
    d["l2_gate"] = np.ascontiguousarray(gup[:, :, 0::2]).reshape(8 * D, D)
    d["l2_up"] = np.ascontiguousarray(gup[:, :, 1::2]).reshape(8 * D, D)
    gub = np.asarray(inputs["L2_ffn_gate_up_proj_bias"], dtype=np.float32)
    d["l2_gb"] = np.ascontiguousarray(gub[:, 0::2])
    d["l2_ub"] = np.ascontiguousarray(gub[:, 1::2])
    d["l2_down"] = f(inputs["L2_ffn_down_proj"]).reshape(8 * D, D)
    d["l2_db"] = f(inputs["L2_ffn_down_proj_bias"])
    if core == 0:
        d["l2_ridx"] = np.ascontiguousarray(
            np.asarray(inputs["L2_ffn_router_indices"], dtype=np.int32))
        d["l2_rw"] = f(inputs["L2_ffn_routing_weights"])
    else:
        d["l2_ridx"] = np.zeros((T, 2), np.int32)
        d["l2_rw"] = np.zeros((T, 8), np.float32)
    return d



DS = 16
DI = 2048


def emit_l3(lb, x3, xout):
    P = lb.P
    final_d = lb.dram_tmp("l3_final", (DI, T), BF16)
    ones512 = lb.sb("ones512", (1, SLAB), BF16)
    tok_o512 = P.vec("memset", ones512[:], 1.0)
    onecol = lb.sb("onecol", (128, 1), F32)
    tok_onecol = P.vec("memset", onecol[:], 1.0)

    xnb = [lb.sb("xnb", (128, T), BF16) for _ in range(NDT)]

    lb.push_scope()
    xbuf = [lb.sb("xb_l3", (128, T), F32) for _ in range(NDT)]
    x_toks = lb.load_x_fm(x3, xbuf)
    ln_tmp = lb.sb("ln_tmp", (128, SLAB), F32)
    ln_prev = [None]

    def ln_wb(i, s0, s1, psA, psB, bt):
        ta = P.vec("tensor_mul", ln_tmp[:], xbuf[i][:, s0:s1],
                   psA[:, 0:SLAB], waits=[x_toks[i], bt, ln_prev[0]])
        tb = P.vec("tensor_add", xnb[i][:, s0:s1], ln_tmp[:],
                   psB[:, 0:SLAB], waits=[ta])
        ln_prev[0] = tb
        return tb

    xn_toks = lb.norm_fm(xbuf, x_toks, "ln", ln_wb)
    lb.pop_scope()

    lb.push_scope()
    nw_col, nw_tok = lb.load_col("l3_normw", D)
    nb_col, nb_tok = lb.load_col("l3_normb", D)
    xn2b = [lb.sb("xn2b", (128, T), BF16) for _ in range(NDT)]
    lb.push_scope()
    l2tmp = lb.sb("l2tmp", (128, SLAB), F32)
    l2prev = [None]

    def ln2_wb(i, s0, s1, psA, psB, bt):
        ta = P.vec("tensor_mul", l2tmp[:], xnb[i][:, s0:s1],
                   psA[:, 0:SLAB], waits=[xn_toks[i], bt, l2prev[0]])
        tb = P.vec("tensor_add", l2tmp[:], l2tmp[:], psB[:, 0:SLAB],
                   waits=[ta])
        tc = P.vec("tensor_scalar", l2tmp[:], l2tmp[:], nw_col[i][:], None,
                   op0=ALU.mult, waits=[tb, nw_tok[i]])
        td = P.vec("tensor_scalar", xn2b[i][:, s0:s1], l2tmp[:],
                   nb_col[i][:], None, op0=ALU.add, waits=[tc, nb_tok[i]])
        l2prev[0] = td
        return td

    xn2_toks = lb.norm_fm(xnb, xn_toks, "ln", ln2_wb)
    lb.pop_scope()

    xp0 = lb.sb("xp0", (1, T), BF16)
    B_b = lb.sb("B_b", (128, T), BF16)
    C_b = lb.sb("C_b", (128, T), BF16)
    dtw_r, dtw_tok = lb.load_row("l3_dtw", DI, out_dtype=BF16)
    dtb_r, dtb_tok = lb.load_row("l3_dtb", DI, out_dtype=BF16)
    acolT_t, acolT_tok = lb.load_w("l3_acolT", (128, 256), out_dtype=F32)
    acolT, acolT_tok = acolT_t[0], acolT_tok[0]
    dsk_t, dsk_tok = lb.load_col("l3_dsk", DI)
    bca_t, bca_tok = lb.load_w("c_bcall", (128, 16 * 128), BF16)
    BcAll, bca_tok = bca_t[0], bca_tok[0]
    sel_t, sel_tok = lb.load_w("c_selall", (128, 16 * 128), BF16)
    SelAll, sel_tok = sel_t[0], sel_tok[0]
    xc_d = lb.dram_tmp("l3_xc", (DI, T), BF16)

    lb.push_scope()
    xi = [lb.sb("xi", (128, T), BF16) for _ in range(DI // 128)]
    xi_toks = [None] * (DI // 128)
    lb.push_scope()
    inpT_a, inpT_tok = lb.load_w("l3_inpT", (D, 2 * DI), col0=0, col1=DI)

    def evac_xi(oc, s0, s1, ps, mm_tok):
        tok = P.vec("tensor_copy", xi[oc][:, s0:s1], ps[0:128, 0:s1 - s0],
                    waits=[mm_tok])
        xi_toks[oc] = tok
        return tok

    lb.proj(inpT_a, inpT_tok, xn2b, xn2_toks, DI, evac_xi)
    lb.pop_scope()

    cw_t, cw_tok = lb.load_w("l3_convw", (DI, 4), out_dtype=F32)
    cb_t, cb_tok = lb.load_col("l3_convb", DI)
    xpB_t, xpB_tok = lb.load_w("l3_xpB", (DI, DS))
    xpC_t, xpC_tok = lb.load_w("l3_xpC", (DI, DS))
    xpd_t, xpd_tok = lb.load_w("l3_xpd", (DI, 1))
    bcn_t, bcn_tok = lb.load_w("c_bcn", (16, 128), BF16, rowsplit=16)
    BcN, bcn_tok = bcn_t[0], bcn_tok[0]

    xc = xi
    xc_toks = [None] * (DI // 128)
    lb.push_scope()
    cacc = lb.sb("cacc", (128, T), F32)
    cprev = None
    for i in range(DI // 128):
        a1 = P.vec("tensor_scalar", cacc[:], xi[i][:],
                   cw_t[i][:, 3:4], None, op0=ALU.mult,
                   waits=[xi_toks[i], cw_tok[i], cprev])
        a2 = P.vec("scalar_tensor_tensor", cacc[:, 1:T], xi[i][:, 0:T - 1],
                   cw_t[i][:, 2:3], cacc[:, 1:T], op0=ALU.mult, op1=ALU.add,
                   waits=[a1])
        a3 = P.vec("scalar_tensor_tensor", cacc[:, 2:T], xi[i][:, 0:T - 2],
                   cw_t[i][:, 1:2], cacc[:, 2:T], op0=ALU.mult, op1=ALU.add,
                   waits=[a2])
        a4 = P.vec("scalar_tensor_tensor", cacc[:, 3:T], xi[i][:, 0:T - 3],
                   cw_t[i][:, 0:1], cacc[:, 3:T], op0=ALU.mult, op1=ALU.add,
                   waits=[a3])
        a5 = P.act(xc[i][:], cacc[:], AF.Silu, bias=cb_t[i][:],
                   waits=[a4, cb_tok[i]])
        xc_toks[i] = a5
        cprev = a5
        P.dma("sync", xc_d[i * 128:(i + 1) * 128, :], xc[i][:],
              waits=[a5])
    lb.pop_scope()

    BpT = lb.sb("BpT", (DS, T), BF16)
    CpT = lb.sb("CpT", (DS, T), BF16)
    bp_tok = cp_tok = x0_tok = None
    for s in range(NSLAB):
        s0, s1 = s * SLAB, (s + 1) * SLAB
        for (wt, wtok, dst, prev_t, which) in (
                (xpB_t, xpB_tok, BpT, bp_tok, "b"),
                (xpC_t, xpC_tok, CpT, cp_tok, "c"),
                (xpd_t, xpd_tok, xp0, x0_tok, "d")):
            ps, war, idx = lb.pg.acquire()
            mmx = None
            nk = DI // 128
            rows = DS if which != "d" else 1
            for kc in range(nk):
                mmx = P.mm(ps[0:rows, 0:SLAB], wt[kc][:],
                           xc[kc][:, s0:s1], kc == 0, kc == nk - 1,
                           waits=[wtok[kc], xc_toks[kc],
                                  war if kc == 0 else None],
                           inc=(kc == nk - 1))
            cv = P.vec("tensor_copy", dst[0:rows, s0:s1],
                       ps[0:rows, 0:SLAB], waits=[mmx, prev_t])
            lb.pg.set_last_use(idx, cv)
            if which == "b":
                bp_tok = cv
            elif which == "c":
                cp_tok = cv
            else:
                x0_tok = cv

    bb_tok = cb2_tok = None
    for s in range(NSLAB):
        s0, s1 = s * SLAB, (s + 1) * SLAB
        ps, war, idx = lb.pg.acquire()
        m1 = P.mm(ps[0:128, 0:SLAB], BcN[:], BpT[:, s0:s1], True, True,
                  waits=[bcn_tok, bp_tok, war], inc=True)
        bb_tok = P.vec("tensor_copy", B_b[:, s0:s1], ps[0:128, 0:SLAB],
                       waits=[m1, bb_tok])
        lb.pg.set_last_use(idx, bb_tok)
        ps, war, idx = lb.pg.acquire()
        m2 = P.mm(ps[0:128, 0:SLAB], BcN[:], CpT[:, s0:s1], True, True,
                  waits=[bcn_tok, cp_tok, war], inc=True)
        cb2_tok = P.vec("tensor_copy", C_b[:, s0:s1], ps[0:128, 0:SLAB],
                        waits=[m2, cb2_tok])
        lb.pg.set_last_use(idx, cb2_tok)
    lb.pop_scope()

    NTILE = DI // 128
    xcb = [lb.sb("xcb", (128, T), BF16) for _ in range(2)]
    xcb_toks = [None, None]
    xc_lu = [None, None]
    xcb_toks[0] = P.dma("sync", xcb[0][:], xc_d[0:128, :])
    final = [lb.sb("final_c", (128, T), BF16) for _ in range(2)]
    final_toks = [None, None]
    delta_i = [lb.sb("delta_i", (128, T), BF16) for _ in range(2)]
    dx_i = [lb.sb("dx_i", (128, T), BF16) for _ in range(2)]
    sp1 = [lb.sb("sp1", (128, SLAB), F32) for _ in range(2)]
    sp2 = [lb.sb("sp2", (128, SLAB), F32) for _ in range(2)]
    dA_buf = [lb.sb("dA_buf", (128, T // 2), F32) for _ in range(2)]
    dB_buf = [lb.sb("dB_buf", (128, T // 2), F32) for _ in range(2)]
    h_buf = [lb.sb("h_buf", (128, T), BF16) for _ in range(2)]
    hC_buf = [lb.sb("hC_buf", (128, T), BF16) for _ in range(2)]
    zb = [lb.sb("zb", (128, T), BF16) for _ in range(2)]
    zw = [[lb.sb("zw", (128, 128), BF16) for _ in range(NDT)]
          for _ in range(2)]
    inp_ap = lb.dram_in("l3_inpT", (D, 2 * DI))
    P.barrier()
    poolY = PsumPool(None, None, "yY", 0, tiles=lb.pg.tiles[0:4])
    poolB = PsumPool(None, None, "yB", 0, tiles=lb.pg.tiles[4:7])
    scan_done = [None, None]
    hct_war = [[None, None], [None, None]]
    sel_war = [None, None]
    sel_fin = [None] * NSLAB
    y_war = [None, None]
    delta_war = [[], []]
    dx_war = [[], []]
    zw_war = [None, None]
    dslab_toks = [[None] * NSLAB for _ in range(NTILE)]
    dxt_toks = [[None, None] for _ in range(NTILE)]
    zw_toks_d = [None] * NTILE
    dprev_d = [None] * NTILE

    def emit_zw_load(i):
        par = i % 2
        toks = []
        for kc in range(NDT):
            toks.append(P.dma(
                "gpsimd", zw[par][kc][:],
                inp_ap[kc * 128:(kc + 1) * 128,
                       DI + i * 128:DI + (i + 1) * 128],
                waits=[zw_war[par]]))
        zw_toks_d[i] = toks

    def emit_delta_slab(i, s):
        par = i % 2
        dprev = dprev_d[i]
        s0, s1 = s * SLAB, (s + 1) * SLAB
        ps, war, idx = poolB.acquire()
        P.mm(ps[0:128, 0:SLAB], dtw_r[0:1, i * 128:(i + 1) * 128],
             xp0[0:1, s0:s1], True, False,
             waits=[dtw_tok, x0_tok, war])
        mmd = P.mm(ps[0:128, 0:SLAB], dtb_r[0:1, i * 128:(i + 1) * 128],
                   ones512[0:1, :], False, True,
                   waits=[dtb_tok, tok_o512], inc=True)
        s1t = P.act(sp1[par][:], ps[0:128, 0:SLAB], AF.Abs,
                    waits=[mmd, dprev, y_war[par]])
        s2t = P.act(sp1[par][:], sp1[par][:], AF.Exp, scale=-1.0,
                    waits=[s1t])
        s3t = P.act(sp1[par][:], sp1[par][:], AF.Ln, bias=onecol[:],
                    waits=[s2t, tok_onecol])
        s4t = P.vec("tensor_single_scalar", sp2[par][:], ps[0:128, 0:SLAB],
                    0.0, op=ALU.max, waits=[mmd, s1t])
        s5t = P.vec("tensor_add", delta_i[par][:, s0:s1], sp1[par][:],
                    sp2[par][:],
                    waits=[s3t, s4t, dprev] + list(delta_war[par]))
        poolB.set_last_use(idx, s4t)
        dslab_toks[i][s] = s5t
        dprev_d[i] = s5t

    def emit_dxt_half(i, half):
        par = i % 2
        h0, h1 = half * (T // 2), (half + 1) * (T // 2)
        dxt_toks[i][half] = P.gp(
            "tensor_mul", dx_i[par][:, h0:h1], delta_i[par][:, h0:h1],
            xcb[par][:, h0:h1],
            waits=[dslab_toks[i][2 * half + 1], xcb_toks[par]]
            + list(dx_war[par]))

    emit_zw_load(0)
    for s in range(NSLAB):
        emit_delta_slab(0, s)
    emit_dxt_half(0, 0)
    emit_dxt_half(0, 1)

    for i in range(NTILE):
        par = i % 2
        if i + 1 < NTILE:
            xcb_toks[(i + 1) % 2] = P.dma(
                "sync", xcb[(i + 1) % 2][:],
                xc_d[(i + 1) * 128:(i + 2) * 128, :],
                waits=[xc_lu[(i + 1) % 2]])
        yps = []
        for s in range(NSLAB):
            ps, war, idx = poolY.acquire()
            yps.append((ps, war, idx))

        def emit_sel(jj, pp, h0t, h1t, stop, yps=yps):
            for s in range(NSLAB):
                s0, s1 = s * SLAB, (s + 1) * SLAB
                ps, war, idx = yps[s]
                smt = P.mm(ps[0:128, 0:SLAB],
                           SelAll[:, jj * 128:(jj + 1) * 128],
                           hC_buf[pp][:, s0:s1], jj == 0, stop,
                           waits=[sel_tok, h0t if s < 2 else h1t,
                                  war if jj == 0 else None],
                           inc=True)
                sel_fin[s] = smt
            sel_war[pp] = sel_fin[NSLAB - 1]

        sel_pend = None
        last_mm1 = last_mm2 = None
        for j in range(16):
            p = j % 2
            dA, dB, hb_ = dA_buf[p], dB_buf[p], h_buf[p]
            hC = hC_buf[p]
            da_tok = db_tok = None
            sc2 = None
            hts = []
            war_scan = scan_done[p]
            for half in range(2):
                h0 = half * (T // 2)
                for sl in range(2):
                    s0 = h0 + sl * SLAB
                    s1 = s0 + SLAB
                    b0, b1 = sl * SLAB, (sl + 1) * SLAB
                    ps1, war1, idx1 = poolB.acquire()
                    mm1 = P.mm(ps1[0:128, 0:SLAB],
                               BcAll[:, j * 128:(j + 1) * 128],
                               delta_i[par][:, s0:s1], True, True,
                               waits=[bca_tok,
                                      dslab_toks[i][half * 2 + sl],
                                      war1], inc=True)
                    last_mm1 = mm1
                    da_tok = P.act(dA[:, b0:b1], ps1[0:128, 0:SLAB],
                                   AF.Exp,
                                   scale=acolT[:,
                                               i * 16 + j:i * 16 + j + 1],
                                   waits=[mm1, acolT_tok, war_scan,
                                          da_tok])
                    poolB.set_last_use(idx1, da_tok)
                    ps2, war2, idx2 = poolB.acquire()
                    mm2 = P.mm(ps2[0:128, 0:SLAB],
                               BcAll[:, j * 128:(j + 1) * 128],
                               dx_i[par][:, s0:s1], True, True,
                               waits=[bca_tok, dxt_toks[i][half], war2],
                               inc=True)
                    last_mm2 = mm2
                    db_tok = P.vec("tensor_mul", dB[:, b0:b1],
                                   ps2[0:128, 0:SLAB], B_b[:, s0:s1],
                                   waits=[mm2, bb_tok, war_scan,
                                          db_tok])
                    poolB.set_last_use(idx2, db_tok)
                scan_f = P.vec
                hct_f = P.gp
                if half == 0:
                    sc2 = scan_f("tensor_tensor_scan", hb_[:, 0:T // 2],
                                 dA[:], dB[:], 0.0,
                                 op0=ALU.mult, op1=ALU.add,
                                 waits=[da_tok, db_tok, hct_war[p][0]])
                else:
                    sc2 = scan_f("tensor_tensor_scan", hb_[:, T // 2:T],
                                 dA[:], dB[:],
                                 hb_[:, T // 2 - 1:T // 2],
                                 op0=ALU.mult, op1=ALU.add,
                                 waits=[da_tok, db_tok, sc2,
                                        hct_war[p][1]])
                war_scan = sc2
                hw0, hw1 = half * (T // 2), (half + 1) * (T // 2)
                ht = hct_f("tensor_mul", hC[:, hw0:hw1], hb_[:, hw0:hw1],
                           C_b[:, hw0:hw1],
                           waits=[sc2, cb2_tok, sel_war[p]])
                hct_war[p][half] = ht
                hts.append(ht)
            scan_done[p] = sc2
            if i + 1 < NTILE:
                if j == 3:
                    emit_zw_load(i + 1)
                elif j in (5, 7, 9, 11):
                    emit_delta_slab(i + 1, (j - 5) // 2)
                elif j == 10:
                    emit_dxt_half(i + 1, 0)
                elif j == 12:
                    emit_dxt_half(i + 1, 1)
            if sel_pend is not None:
                emit_sel(*sel_pend, stop=False)
            sel_pend = (j, p, hts[0], hts[1])
        delta_war[par] = [last_mm1] + dxt_toks[i]
        dx_war[par] = [last_mm2]
        zt_toks = []
        mmz = None
        for s in range(NSLAB):
            s0, s1 = s * SLAB, (s + 1) * SLAB
            psz, warz, idxz = poolB.acquire()
            for kc in range(NDT):
                mmz = P.mm(psz[0:128, 0:SLAB], zw[par][kc][:],
                           xn2b[kc][:, s0:s1], kc == 0, kc == NDT - 1,
                           waits=[zw_toks_d[i][kc], xn2_toks[kc],
                                  warz if kc == 0 else None],
                           inc=(kc == NDT - 1))
            zt = P.act(zb[par][:, s0:s1], psz[0:128, 0:SLAB], AF.Silu,
                       waits=[mmz, y_war[par]])
            poolB.set_last_use(idxz, zt)
            zt_toks.append(zt)
        zw_war[par] = mmz
        emit_sel(*sel_pend, stop=True)
        sel_pend = None
        fprev = None
        for s in range(NSLAB):
            s0, s1 = s * SLAB, (s + 1) * SLAB
            ps, war, idx = yps[s]
            ytok = P.vec("scalar_tensor_tensor", sp1[par][:],
                         xcb[par][:, s0:s1],
                         dsk_t[i][:], ps[0:128, 0:SLAB],
                         op0=ALU.mult, op1=ALU.add,
                         waits=[sel_fin[s], dsk_tok[i],
                                xcb_toks[par], fprev])
            poolY.set_last_use(idx, ytok)
            ft = P.vec("tensor_mul", final[par][:, s0:s1], sp1[par][:],
                       zb[par][:, s0:s1],
                       waits=[ytok, zt_toks[s], final_toks[par]])
            fprev = ft
        final_toks[par] = P.dma(
            "sync", final_d[i * 128:(i + 1) * 128, :], final[par][:],
            waits=[fprev])
        y_war[par] = fprev
        xc_lu[par] = fprev
    P.barrier()
    lb.pop_scope()

    lb.push_scope()
    acc = [lb.sb("acc_l3", (128, T), F32) for _ in range(NDT)]
    acc_toks = lb.load_x_fm(x3, acc)
    for i in range(NDT):
        acc_toks[i] = P.vec("tensor_add", acc[i][:], acc[i][:], xnb[i][:],
                            waits=[acc_toks[i], xn_toks[i]])
    lb.push_scope()
    finalf = [lb.sb("finalf", (128, T), BF16) for _ in range(DI // 128)]
    finalf_toks = lb.load_x_fm(final_d, finalf)
    outT, outT_tok = lb.load_w("l3_outT", (DI, D))

    def evac_out3(oc, s0, s1, ps, mm_tok):
        tok = P.vec("tensor_add", acc[oc][:, s0:s1], acc[oc][:, s0:s1],
                    ps[0:128, 0:s1 - s0], waits=[mm_tok, acc_toks[oc]])
        acc_toks[oc] = tok
        return tok

    lb.proj(outT, outT_tok, finalf, finalf_toks, D, evac_out3)
    lb.pop_scope()

    emit_l3_moe(lb, xnb, xn_toks, acc, acc_toks)

    lb.push_scope()
    obuf = [lb.sb("obuf_o", (128, D), F32) for _ in range(2)]
    ob_tok = [None, None]
    for tt in range(T // 128):
        buf = obuf[tt % 2]
        ctoks = []
        for i in range(NDT):
            ps, war, idx = lb.pg.acquire()
            mt = P.tp(ps[:, 0:128], acc[i][:, tt * 128:(tt + 1) * 128],
                      lb.identf[:],
                      waits=[acc_toks[i], lb.tok_identf, war], inc=True)
            ct = P.vec("tensor_copy", buf[:, i * 128:(i + 1) * 128],
                       ps[:, 0:128], waits=[mt, ob_tok[tt % 2]])
            lb.pg.set_last_use(idx, ct)
            ctoks.append(ct)
        st = P.dma("sync", xout[tt * 128:(tt + 1) * 128, :], buf[:],
                   waits=ctoks)
        ob_tok[tt % 2] = st
        lb.out_dma_toks.append(st)
    lb.pop_scope()


def emit_l3_moe(lb, xnb, xn_toks, acc, acc_toks):
    P = lb.P
    E = 8
    lb.push_scope()
    wrow = [lb.sb("wrow3", (1, T), BF16) for _ in range(E + 1)]
    wrow_toks = [None] * (E + 1)
    lb.push_scope()
    wr_t, wr_tok = lb.load_w("l3_router", (D, E))
    sgw_t, sgw_tok = lb.load_w("l3_sgw", (D, 1))
    logits_T = lb.sb("logits3", (E, T), F32)
    lg_toks = []
    for s in range(NSLAB):
        s0, s1 = s * SLAB, (s + 1) * SLAB
        ps, war, idx = lb.pg.acquire()
        mmr = None
        for kc in range(NDT):
            mmr = P.mm(ps[0:E, 0:SLAB], wr_t[kc][:], xnb[kc][:, s0:s1],
                       kc == 0, kc == NDT - 1,
                       waits=[wr_tok[kc], xn_toks[kc],
                              war if kc == 0 else None],
                       inc=(kc == NDT - 1))
        cv = P.vec("tensor_copy", logits_T[0:E, s0:s1], ps[0:E, 0:SLAB],
                   waits=[mmr] + lg_toks[-1:])
        lb.pg.set_last_use(idx, cv)
        lg_toks.append(cv)
        ps2, war2, idx2 = lb.pg.acquire()
        mms = None
        for kc in range(NDT):
            mms = P.mm(ps2[0:1, 0:SLAB], sgw_t[kc][:], xnb[kc][:, s0:s1],
                       kc == 0, kc == NDT - 1,
                       waits=[sgw_tok[kc], xn_toks[kc],
                              war2 if kc == 0 else None],
                       inc=(kc == NDT - 1))
        sgt = P.act(wrow[E][0:1, s0:s1], ps2[0:1, 0:SLAB], AF.Sigmoid,
                    waits=[mms, wrow_toks[E]])
        lb.pg.set_last_use(idx2, sgt)
        wrow_toks[E] = sgt
    ptile = lb.sb("ptile3", (128, E), F32)
    junk = lb.sb("junk3", (128, E), F32)
    rmax = lb.sb("rmax3", (128, 1), F32)
    accgt = lb.sb("accgt3", (128, 1), F32)
    acceq = lb.sb("acceq3", (128, 1), F32)
    sel = lb.sb("sel3", (128, E), F32)
    for tt in range(T // 128):
        t0 = tt * 128
        ps, war, idx = lb.pg.acquire()
        mt = P.tp(ps[0:128, 0:E], logits_T[0:E, t0:t0 + 128],
                  lb.identf[0:E, 0:E],
                  waits=lg_toks + [lb.tok_identf, war], inc=True)
        cp = P.vec("tensor_copy", ptile[:], ps[0:128, 0:E], waits=[mt])
        lb.pg.set_last_use(idx, cp)
        r1 = P.vec("reduce_max", rmax[:, :, None], ptile[:, None, :],
                   axis=AX.X, waits=[cp])
        r2 = P.vec("tensor_scalar_mul", rmax[:], rmax[:], -1.0, waits=[r1])
        r3 = P.act(ptile[:], ptile[:], AF.Exp, bias=rmax[:], waits=[r2])
        r4 = P.vec("reduce_sum", rmax[:, :, None], ptile[:, None, :],
                   axis=AX.X, waits=[r3])
        r5 = P.vec("reciprocal", rmax[:], rmax[:], waits=[r4])
        r6 = P.vec("tensor_scalar", ptile[:], ptile[:], rmax[:], None,
                   op0=ALU.mult, waits=[r5])
        prev = r6
        for e in range(E):
            t1 = P.vec("tensor_scalar", junk[:], ptile[:],
                       ptile[:, e:e + 1], 0.0, op0=ALU.is_gt,
                       op1=ALU.add, accum_out=accgt[:], waits=[prev])
            if e > 0:
                t2 = P.vec("tensor_scalar", junk[:, 0:e], ptile[:, 0:e],
                           ptile[:, e:e + 1], 0.0, op0=ALU.is_equal,
                           op1=ALU.add, accum_out=acceq[:], waits=[t1])
                t3 = P.vec("tensor_add", accgt[:], accgt[:], acceq[:],
                           waits=[t2])
            else:
                t3 = t1
            t4 = P.vec("tensor_single_scalar", sel[:, e:e + 1], accgt[:],
                       2.0, op=ALU.is_lt, waits=[t3])
            prev = t4
        t5 = P.vec("tensor_mul", sel[:], sel[:], ptile[:], waits=[prev])
        prev = t5
        for e in range(E):
            ps2, war2, idx2 = lb.pg.acquire()
            mt2 = P.tp(ps2[0:1, 0:128], sel[:, e:e + 1], lb.identf[:],
                       waits=[prev, lb.tok_identf, war2], inc=True)
            cv2 = P.vec("tensor_copy", wrow[e][0:1, t0:t0 + 128],
                        ps2[0:1, 0:128], waits=[mt2, wrow_toks[e]])
            lb.pg.set_last_use(idx2, cv2)
            wrow_toks[e] = cv2
            prev = cv2
    lb.pop_scope()

    w1_ap = lb.dram_in("l3_w1", (E * D, D))
    w2_ap = lb.dram_in("l3_w2", (E * D, D))
    sg_ap = lb.dram_in("l3_sgate", (D, D))
    su_ap = lb.dram_in("l3_sup", (D, D))
    sd_ap = lb.dram_in("l3_sdown", (D, D))
    gbuf = [lb.sb("gw3", (128, D), BF16) for _ in range(NDT)]
    ubuf = [lb.sb("uw3", (128, D), BF16) for _ in range(NDT)]
    dbuf = [lb.sb("dw3", (128, D), BF16) for _ in range(NDT)]
    hmid = [lb.sb("hmid3", (128, SLAB), BF16) for _ in range(NDT)]
    sgb = lb.sb("sgb3", (128, SLAB), F32)
    P.barrier()
    poolGU = PsumPool(None, None, "3GU", 0, tiles=lb.pg.tiles[0:4])
    poolW = PsumPool(None, None, "3W", 0, tiles=lb.pg.tiles[4:5])
    poolD = PsumPool(None, None, "3D", 0, tiles=lb.pg.tiles[5:7])

    def issue_gu(ei, wait):
        gt, ut = [], []
        for kc in range(NDT):
            if ei == E:
                gt.append(P.dma("gpsimd", gbuf[kc][:],
                                sg_ap[kc * 128:(kc + 1) * 128, :],
                                waits=wait))
                ut.append(P.dma("gpsimd", ubuf[kc][:],
                                su_ap[kc * 128:(kc + 1) * 128, :],
                                waits=wait))
            else:
                r0 = ei * D + kc * 128
                gt.append(P.dma("gpsimd", gbuf[kc][:],
                                w1_ap[r0:r0 + 128, :], waits=wait))
        return gt, ut

    def issue_d(ei, wait):
        if ei == E:
            return [P.dma("gpsimd", dbuf[kc][:],
                          sd_ap[kc * 128:(kc + 1) * 128, :], waits=wait)
                    for kc in range(NDT)]
        return [P.dma("gpsimd", dbuf[kc][:],
                      w2_ap[ei * D + kc * 128:ei * D + (kc + 1) * 128, :],
                      waits=wait) for kc in range(NDT)]

    gu_next = issue_gu(0, [])
    d_next = issue_d(0, [])
    for ei in range(E + 1):
        shared = ei == E
        g_toks, u_toks = gu_next
        d_toks = d_next
        wre = ei
        for s in range(NSLAB):
            s0, s1_ = s * SLAB, (s + 1) * SLAB
            psw, warw, idxw = poolW.acquire()
            bw = P.mm(psw[0:128, 0:SLAB], lb.onesrow_b[:],
                      wrow[wre][0:1, s0:s1_], True, True,
                      waits=[wrow_toks[wre], lb.tok_ones, warw], inc=True)
            hm_toks = []
            for ic in range(NDT):
                psg, warg, idxg = poolGU.acquire()
                mmg = None
                for kc in range(NDT):
                    mmg = P.mm(psg[0:128, 0:SLAB],
                               gbuf[kc][:, ic * 128:(ic + 1) * 128],
                               xnb[kc][:, s0:s1_], kc == 0, kc == NDT - 1,
                               waits=[g_toks[kc], xn_toks[kc],
                                      warg if kc == 0 else None],
                               inc=(kc == NDT - 1))
                st = P.act(sgb[:], psg[0:128, 0:SLAB], AF.Silu,
                           waits=[mmg, hm_toks[-1] if hm_toks else None])
                poolGU.set_last_use(idxg, st)
                if shared:
                    psu, waru, idxu = poolGU.acquire()
                    mmu = None
                    for kc in range(NDT):
                        mmu = P.mm(psu[0:128, 0:SLAB],
                                   ubuf[kc][:, ic * 128:(ic + 1) * 128],
                                   xnb[kc][:, s0:s1_], kc == 0,
                                   kc == NDT - 1,
                                   waits=[u_toks[kc], xn_toks[kc],
                                          waru if kc == 0 else None],
                                   inc=(kc == NDT - 1))
                    m1 = P.vec("tensor_mul", sgb[:], sgb[:],
                               psu[0:128, 0:SLAB], waits=[st, mmu])
                    poolGU.set_last_use(idxu, m1)
                    st = m1
                ht = P.vec("tensor_mul", hmid[ic][:], sgb[:],
                           psw[0:128, 0:SLAB], waits=[st, bw])
                hm_toks.append(ht)
            poolW.set_last_use(idxw, hm_toks[-1])
            if s == NSLAB - 1 and ei < E:
                gu_next = issue_gu(ei + 1, [mmu if shared else mmg])
            for oc in range(NDT):
                psd, ward, idxd = poolD.acquire()
                mmd = None
                for ic in range(NDT):
                    mmd = P.mm(psd[0:128, 0:SLAB],
                               dbuf[ic][:, oc * 128:(oc + 1) * 128],
                               hmid[ic][:], ic == 0, ic == NDT - 1,
                               waits=[d_toks[ic], hm_toks[ic],
                                      ward if ic == 0 else None],
                               inc=(ic == NDT - 1))
                at = P.vec("tensor_add", acc[oc][:, s0:s1_],
                           acc[oc][:, s0:s1_], psd[0:128, 0:SLAB],
                           waits=[mmd, acc_toks[oc]])
                poolD.set_last_use(idxd, at)
                acc_toks[oc] = at
        if ei < E:
            d_next = issue_d(ei + 1, [P.now("tensor")])
    P.barrier()
    lb.pop_scope()


def prep_l3(inputs):
    f = lambda a: np.ascontiguousarray(np.asarray(a, dtype=np.float32))
    d = {}
    d["c_ident"] = np.eye(128, dtype=np.float32)
    bc = np.zeros((128, 16 * 128), np.float32)
    sl = np.zeros((128, 16 * 128), np.float32)
    for j in range(16):
        for lane in range(128):
            dd = j * 8 + lane // 16
            bc[dd, j * 128 + lane] = 1.0
            sl[lane, j * 128 + dd] = 1.0
    d["c_bcall"] = bc
    d["c_selall"] = sl
    bcn = np.zeros((16, 128), np.float32)
    for lane in range(128):
        bcn[lane % 16, lane] = 1.0
    d["c_bcn"] = bcn
    d["l3_normw"] = f(inputs["L3_attn_norm_w"])
    d["l3_normb"] = f(inputs["L3_attn_norm_b"])
    d["l3_inpT"] = f(np.asarray(inputs["L3_attn_in_proj_w"]).T)
    d["l3_convw"] = f(inputs["L3_attn_conv_w"]).reshape(DI, 4)
    d["l3_convb"] = f(inputs["L3_attn_conv_b"])
    xp = f(inputs["L3_attn_x_proj_w"])
    d["l3_xpd"] = np.ascontiguousarray(xp[0:1].T)
    d["l3_xpB"] = np.ascontiguousarray(xp[1:DS + 1].T)
    d["l3_xpC"] = np.ascontiguousarray(xp[DS + 1:].T)
    d["l3_dtw"] = f(inputs["L3_attn_dt_w"]).reshape(DI)
    d["l3_dtb"] = f(inputs["L3_attn_dt_b"])
    A = -np.exp(f(inputs["L3_attn_A_log"]))
    acolT = np.zeros((128, 256), np.float32)
    for i in range(16):
        for j in range(16):
            for lane in range(128):
                acolT[lane, i * 16 + j] = A[i * 128 + j * 8 + lane // 16,
                                            lane % 16]
    d["l3_acolT"] = acolT
    d["l3_dsk"] = f(inputs["L3_attn_D_skip"])
    d["l3_outT"] = f(np.asarray(inputs["L3_attn_out_proj_w"]).T)
    d["l3_router"] = f(inputs["L3_ffn_router_w"])
    d["l3_w1"] = f(inputs["L3_ffn_w1"]).reshape(8 * D, D)
    d["l3_w2"] = f(inputs["L3_ffn_w2"]).reshape(8 * D, D)
    sgu = f(inputs["L3_ffn_shared_gate_up"])
    d["l3_sgate"] = np.ascontiguousarray(sgu[:, 0:D])
    d["l3_sup"] = np.ascontiguousarray(sgu[:, D:2 * D])
    d["l3_sdown"] = f(inputs["L3_ffn_shared_down"])
    d["l3_sgw"] = f(inputs["L3_ffn_shared_gate_w"])
    return d



# version 13 (final)
# speedup vs baseline: 1.1362x; 1.1362x over previous
import numpy as np
from contextlib import ExitStack
import concourse.bass as bass
import concourse.mybir as mybir
from concourse.bass_utils import run_bass_kernel_spmd

F32 = mybir.dt.float32
BF16 = mybir.dt.bfloat16
I32 = mybir.dt.int32
AF = mybir.ActivationFunctionType
ALU = mybir.AluOpType
AX = mybir.AxisListType

T = 2048
D = 1024
SLAB = 512
NSLAB = T // SLAB
NDT = D // 128

ENGS = ("sync", "gpsimd", "scalar", "vector", "tensor")
DMA_Q = {"sync": "qsync", "gpsimd": "qgpsimd", "scalar": "qscalar"}
NLANES = 8

WARMTAB = {
    0:48,12:13,21:6,30:15,39:14,48:8,57:10,66:30,84:14,93:17,
    120:21,129:12,153:38,164:38,175:38,186:38,189:96,192:45,194:17,196:10,
    197:6,199:18,201:26,203:21,205:22,207:22,235:8,243:51,252:47,261:44,
    270:44,272:9,274:7,315:61,317:17,320:10,325:8,342:22,345:10,348:10,
    351:10,357:10,360:18,363:12,366:12,369:10,372:10,375:10,381:18,384:18,
    387:18,390:17,393:10,396:10,399:10,405:18,408:18,411:18,414:10,417:10,
    420:10,425:10,428:10,431:8,440:52,442:26,443:7,444:7,445:7,446:25,
    448:7,449:7,450:7,451:19,452:7,453:7,454:7,455:7,456:25,458:7,
    459:7,460:7,461:19,463:7,464:7,465:7,466:25,468:7,469:7,470:7,
    471:19,472:7,473:7,474:7,475:7,476:25,478:7,479:7,480:7,481:19,
    483:7,484:7,485:7,486:25,488:7,489:7,490:7,491:19,492:7,493:7,
    494:7,495:7,496:25,498:7,499:7,500:7,501:19,503:7,504:7,505:7,
    506:25,508:7,509:7,510:7,511:19,512:7,513:7,514:7,515:7,516:25,
    518:7,519:7,520:7,521:19,533:8,534:6,538:8,539:7,543:8,544:6,
    548:8,549:7,553:8,554:6,558:8,559:7,563:8,564:6,568:8,586:8,
    595:8,604:8,613:8,622:8,631:8,640:8,649:8,671:8,684:8,697:8,
    710:8,723:8,736:8,749:8,762:8,788:8,805:8,822:8,839:8,856:8,
    873:8,890:8,907:8,915:49,917:27,919:8,942:22,945:10,948:10,951:10,
    957:18,960:12,963:18,966:18,969:10,972:10,975:10,981:17,984:18,987:12,
    990:12,993:10,996:10,999:10,1005:10,1008:18,1011:18,1014:10,1017:10,1020:10,
    1025:10,1028:10,1031:8,1040:52,1042:21,1043:7,1044:7,1045:7,1046:25,1048:7,
    1049:7,1050:7,1051:19,1052:7,1053:7,1054:7,1055:7,1056:25,1058:7,1059:7,
    1060:8,1061:19,1063:7,1064:7,1065:7,1066:25,1068:7,1069:7,1070:7,1071:19,
    1072:7,1073:7,1074:7,1075:7,1076:25,1078:7,1079:7,1080:7,1081:19,1083:7,
    1084:7,1085:7,1086:25,1088:7,1089:7,1090:7,1091:19,1092:7,1093:7,1094:7,
    1095:7,1096:25,1098:7,1099:7,1100:7,1101:19,1103:7,1104:7,1105:7,1106:25,
    1108:7,1109:7,1110:7,1111:22,1113:7,1114:7,1115:7,1116:25,1118:7,1119:7,
    1120:7,1121:19,1133:8,1134:6,1138:8,1139:7,1143:8,1144:6,1148:8,1149:7,
    1153:8,1154:6,1158:8,1159:7,1163:8,1164:6,1168:8,1186:8,1195:8,1204:8,
    1213:8,1222:8,1231:8,1240:8,1249:8,1271:8,1284:8,1297:8,1310:8,1323:8,
    1336:8,1349:8,1362:8,1388:8,1405:8,1422:8,1439:8,1456:8,1473:8,1490:8,
    1507:8,1523:51,1532:47,1541:44,1550:45,1552:89,1562:14,1567:12,1574:97,1576:93,
    1578:85,1580:90,1582:87,1584:91,1586:86,1588:94,1590:87,1592:92,1594:88,1596:93,
    1598:87,1600:94,1602:88,1604:93,1608:36,1609:29,1610:23,1611:10,1612:9,1613:13,
    1614:21,1615:17,1723:10,1724:21,1725:29,1726:10,1839:10,1840:19,1841:23,1842:19,
    1955:10,1956:19,1957:19,1958:22,2071:16,2072:16,2073:18,2074:16,2187:11,2188:21,
    2189:18,2190:19,2303:24,2304:18,2305:18,2306:10,2419:14,2420:19,2421:18,2422:20,
    2533:48,2577:24,2595:10,2613:19,2622:10,2631:16,2640:12,2649:10,2658:9,2667:14,
    2676:6,2685:19,2694:9,2713:51,2724:38,2735:38,2746:38,2749:112,2752:250,2757:94,
    2816:25,2817:7,2818:7,2819:7,2820:25,2822:7,2823:7,2824:7,2825:22,2827:7,
    2828:7,2829:7,2830:25,2832:7,2833:7,2834:7,2835:22,2837:7,2838:7,2839:7,
    2840:25,2842:7,2843:7,2844:7,2845:22,2847:7,2848:7,2849:7,2850:25,2852:7,
    2853:7,2854:7,2855:22,2857:7,2858:7,2859:7,2860:25,2862:7,2863:7,2864:7,
    2865:22,2867:7,2868:7,2869:7,2870:25,2872:7,2873:7,2874:7,2875:22,2877:7,
    2878:7,2879:7,2880:25,2882:7,2883:7,2884:7,2885:22,2887:7,2888:7,2889:7,
    2890:25,2892:7,2893:7,2894:7,2895:22,2903:22,2904:8,2911:8,2920:8,2931:8,
    2942:8,2953:8,2964:8,2975:8,2986:8,3005:10,3014:8,3022:9,3028:14,3031:8,
    3050:8,3069:8,3088:8,3102:8,3107:15,3126:8,3145:8,3166:10,3181:8,3206:8,
    3216:11,3218:8,3220:6,3228:8,3233:8,3260:8,3270:11,3280:13,3287:8,3314:8,
    3341:8,3368:8,3389:6,3412:8,3426:27,3445:8,3463:6,3473:8,3480:8,3515:8,
    3550:8,3585:8,3620:8,3655:8,3665:130,3724:25,3725:7,3726:7,3727:7,3728:25,
    3730:7,3731:7,3732:7,3733:22,3735:7,3736:7,3737:7,3738:25,3740:7,3741:7,
    3742:7,3743:22,3745:7,3746:7,3747:7,3748:25,3750:7,3751:7,3752:7,3753:22,
    3755:8,3756:7,3757:7,3758:25,3760:7,3761:7,3762:7,3763:22,3765:7,3766:7,
    3767:7,3768:25,3770:7,3771:7,3772:7,3773:22,3775:7,3776:7,3777:7,3778:25,
    3780:7,3781:7,3782:7,3783:22,3785:7,3786:7,3787:7,3788:25,3790:7,3791:7,
    3792:7,3793:22,3795:7,3796:7,3797:7,3798:25,3800:7,3801:7,3802:7,3803:21,
    3811:22,3819:8,3828:8,3839:8,3850:8,3861:8,3872:8,3880:7,3883:8,3894:8,
    3913:7,3915:6,3922:8,3939:8,3949:8,3951:7,3958:8,3968:9,3970:6,3977:8,
    3987:8,3996:8,4015:8,4034:8,4044:9,4048:9,4053:8,4072:13,4074:12,4089:8,
    4101:8,4114:8,4141:8,4163:9,4168:15,4188:6,4195:8,4222:8,4242:7,4249:8,
    4273:6,4276:8,4295:15,4297:7,4320:8,4328:10,4330:7,4332:6,4353:8,4363:7,
    4367:6,4375:11,4379:7,4381:8,4388:8,4398:8,4400:8,4402:6,4406:8,4412:12,
    4423:8,4433:7,4435:7,4439:14,4447:12,4458:8,4488:10,4493:8,4528:8,4563:8,
    4573:94,4574:20,4575:19,4576:19,4577:20,4578:20,4579:24,4596:6,4652:6,4708:6,
    4764:6,4820:6,4876:6,4932:6,4988:6,5024:250,5041:38,5052:38,5063:38,5074:42,
    5077:113,5540:60,5541:181,5543:36,5545:35,5547:44,5549:37,5551:35,5553:35,5555:44,
    5592:33,5593:14,5594:7,5627:79,5635:67,5643:64,5651:64,5659:77,5667:62,5675:69,
    5683:66,5691:64,5699:81,5707:72,5715:76,5723:63,5731:73,5739:68,5747:69,5765:35,
    5766:13,5767:10,5768:19,5769:11,5770:18,5771:10,5772:20,5787:7,5789:7,5791:7,
    5801:44,5803:39,5817:7,5819:7,5821:7,5824:44,5825:41,5838:7,5840:7,5842:9,
    5845:44,5846:40,5859:7,5861:7,5863:7,5866:44,5867:39,5881:18,5896:7,5898:7,
    5900:7,5910:44,5912:43,5926:7,5928:7,5930:7,5933:44,5934:41,5947:7,5949:7,
    5951:7,5954:44,5955:42,5968:7,5970:7,5972:7,5975:44,5976:41,5990:20,6005:7,
    6007:7,6009:7,6019:44,6021:50,6035:7,6037:7,6039:7,6042:44,6043:44,6056:7,
    6058:7,6060:7,6063:44,6064:42,6084:44,6085:40,6098:11,6099:21,6114:7,6116:7,
    6118:7,6128:44,6130:42,6144:7,6146:7,6148:7,6151:44,6152:41,6165:7,6167:7,
    6169:7,6172:44,6173:44,6190:6,6193:44,6194:42,6207:10,6208:13,6227:7,6237:44,
    6239:46,6260:43,6261:43,6281:38,6282:41,6302:43,6303:44,6316:14,6317:9,6346:36,
    6348:47,6366:7,6369:41,6370:43,6385:7,6387:7,6390:44,6391:42,6411:41,6412:45,
    6455:36,6457:45,6478:38,6479:47,6499:36,6500:44,6520:36,6521:40,6535:10,6564:38,
    6566:48,6587:36,6588:41,6608:36,6609:40,6629:37,6630:42,6648:37,6649:12,6650:18,
    6651:34,6652:19,6653:13,6654:23,6655:19,6656:49,6668:12,6672:49,6683:37,6694:37,
    6705:38,6708:114,6718:37,6729:105,6740:85,6751:74,6754:207,6764:119,6765:14,6766:9,
    6767:9,6768:14,6769:15,6770:18,6771:9,6839:250,6841:22,6866:35,6868:28,6870:9,
    6871:8,6872:29,6874:18,6877:6,6880:9,6885:10,6888:26,6898:34,6908:32,6918:33,
    6925:33,6928:30,6936:9,6939:22,6946:10,6949:26,6957:11,6960:28,6967:6,6970:26,
    6976:8,6978:9,6981:21,6991:32,6997:12,6999:18,7002:22,7012:26,7019:18,7022:28,
    7029:10,7032:26,7052:15,7053:38,7056:22,7061:17,7064:33,7074:34,7081:9,7084:26,
    7091:10,7094:27,7101:25,7104:27,7110:10,7112:9,7115:21,7125:33,7133:19,7136:22,
    7146:34,7154:11,7157:28,7164:10,7167:26,7173:9,7175:9,7178:21,7188:34,7195:12,
    7198:31,7208:34,7228:17,7229:42,7232:21,7237:22,7240:30,7250:34,7260:35,7267:8,
    7270:26,7277:33,7280:26,7288:9,7291:21,7298:10,7301:28,7309:9,7312:28,7319:10,
    7322:26,7328:13,7330:20,7333:21,7343:34,7349:17,7351:9,7354:21,7361:10,7364:28,
    7371:10,7374:26,7384:33,7404:17,7405:43,7408:21,7413:10,7416:26,7423:10,7426:28,
    7436:34,7446:34,7453:27,7456:28,7462:7,7464:9,7467:21,7477:33,7483:11,7485:9,
    7488:21,7498:32,7506:20,7509:28,7519:34,7525:10,7527:9,7530:22,7540:34,7547:18,
    7550:27,7557:10,7560:26,7580:17,7581:43,7584:22,7589:18,7592:26,7602:34,7612:34,
    7622:34,7629:33,7632:26,7640:9,7643:22,7650:10,7653:26,7661:15,7664:28,7671:7,
    7674:26,7680:11,7682:10,7685:22,7695:34,7701:14,7703:11,7706:22,7713:10,7716:26,
    7723:10,7726:26,7733:10,7736:28,7756:17,7757:43,7760:21,7765:10,7768:26,7775:10,
    7778:27,7788:34,7798:34,7805:26,7808:26,7816:9,7819:28,7826:8,7829:28,7835:11,
    7837:9,7840:21,7850:34,7856:6,7858:10,7861:21,7871:34,7877:10,7879:9,7882:21,
    7892:33,7899:18,7902:27,7909:10,7912:26,7932:17,7933:41,7936:22,7941:17,7944:26,
    7954:34,7964:34,7974:34,7981:33,7984:26,7992:9,7995:22,8002:10,8005:26,8013:15,
    8016:26,8026:26,8032:9,8034:10,8037:22,8047:35,8053:13,8055:20,8058:21,8068:26,
    8075:18,8078:26,8085:10,8088:26,8108:17,8109:46,8112:27,8117:10,8120:26,8127:10,
    8130:26,8140:26,8150:34,8157:26,8160:26,8166:13,8168:19,8171:21,8181:33,8187:9,
    8189:9,8192:21,8202:34,8210:20,8213:22,8220:10,8223:30,8229:9,8231:9,8234:21,
    8244:33,8251:18,8254:26,8264:33,8284:17,8285:43,8288:21,8293:10,8296:26,8303:10,
    8306:30,8316:34,8326:34,8336:29,8342:19,8344:9,8347:21,8354:10,8357:30,8365:9,
    8368:22,8375:10,8378:26,8386:15,8389:28,8396:7,8399:26,8405:19,8407:9,8410:21,
    8417:10,8420:30,8427:15,8430:34,8440:34,8460:15,8461:40,8464:22,8469:17,8472:26,
    8482:33,8492:34,8502:34,8509:33,8512:26,8520:9,8523:22,8530:10,8533:26,8541:10,
    8544:28,8551:6,8554:26,8560:11,8562:11,8565:26,8575:33,8581:15,8583:9,8586:22,
    8593:10,8596:26,8603:10,8606:27,8613:10,8616:29,8636:17,8637:43,8640:21,8645:10,
    8648:26,8655:10,8658:27,8668:34,8678:34,8685:26,8688:27,8696:9,8699:28,8706:9,
    8709:26,8715:13,8717:19,8720:21,8730:34,8736:8,8738:11,8741:28,8751:34,8757:12,
    8759:19,8762:21,8772:34,8779:18,8782:26,8789:10,8792:26,8812:17,8813:43,8816:22,
    8821:10,8824:27,8834:34,8844:35,8851:8,8854:26,8861:33,8864:26,8872:9,8875:21,
    8882:10,8885:29,8893:9,8896:28,8903:10,8906:26,8914:17,8917:28,8927:27,8933:18,
    8935:9,8938:22,8945:10,8948:29,8955:10,8958:26,8968:33,8988:15,8989:41,8992:24,
    8997:10,9000:26,9007:10,9010:26,9020:34,9030:34,9037:26,9040:26,9048:9,9051:28,
    9058:7,9061:26,9067:13,9069:17,9072:21,9082:32,9088:7,9090:11,9093:28,9103:34,
    9109:10,9111:9,9114:21,9124:34,9131:18,9134:27,9141:10,9144:26,9164:15,9165:39,
    9168:21,9173:18,9176:26,9186:34,9196:34,9206:26,9213:33,9216:26,9224:9,9227:21,
    9234:10,9237:26,9245:9,9248:26,9258:26,9264:13,9266:19,9269:22,9279:34,9285:15,
    9287:11,9290:21,9297:10,9300:26,9307:10,9310:26,9320:26,9340:17,9341:43,9344:21,
    9349:10,9352:26,9359:10,9362:26,9372:34,9382:33,9389:25,9392:27,9400:15,9403:28,
    9413:26,9419:10,9421:9,9424:21,9434:32,9442:20,9445:22,9455:34,9461:9,9463:9,
    9466:21,9476:34,9483:18,9486:26,9496:33,9516:17,9517:43,9520:21,9525:10,9528:26,
    9535:10,9538:28,9548:34,9558:34,9565:8,9568:26,9575:10,9578:26,9588:34,9598:34,
    9605:6,9608:26,9615:10,9618:26,9628:33,9638:34,9648:34,9655:10,9658:26,9665:10,
    9668:27,9686:64,9688:30,9696:73,9697:250,9699:14,9701:10,9703:9,9705:11,9707:17,
    9709:10,9711:12,9713:12,9715:9,9723:7,9755:33,9776:134,9787:109,9795:107,9803:103,
    9811:110,9819:108,9827:105,9835:104,9843:110,9851:109,9859:104,9867:112,9875:112,9883:107,
    9891:105,9899:111,9910:31,9911:13,9912:9,10018:8,10744:20,10745:19,10746:9,10875:11,
    10891:9,10907:7,10923:11,10939:8,10955:8,10971:7,10982:23,
}


class Prog:

    def __init__(self, nc):
        self.nc = nc
        self.es = ExitStack()
        self.stream = {e: [] for e in ENGS}
        self.sem = {}
        self.cnt = {}
        for k in ENGS:
            self.sem[k] = self.es.enter_context(nc.semaphore("sem_" + k))
            self.cnt[k] = 0
        for q in DMA_Q.values():
            for ln in range(NLANES):
                k = f"{q}{ln}"
                self.sem[k] = self.es.enter_context(nc.semaphore("sem_" + k))
                self.cnt[k] = 0
        self.lane_idx = {q: 0 for q in DMA_Q.values()}
        self.waited = {e: {} for e in ENGS}
        self.wait_idx = 0
        self.nodummy = False
        self.warm_dummy = None

    def _wait(self, eng, tok):
        if tok is None:
            return
        k, v = tok
        if v <= 0 or v <= self.waited[eng].get(k, 0):
            return
        if eng == "tensor":
            idx = self.wait_idx
            self.wait_idx += 1
            n = WARMTAB.get(idx, 0)
            if n and not self.nodummy and self.warm_dummy is not None:
                dst, src = self.warm_dummy
                for _ in range(n):
                    self.stream[eng].append(
                        ("op", "transpose",
                         (dst[0:128, 0:128], src[:], src[:]), {},
                         None, 0))
        self.waited[eng][k] = v
        self.stream[eng].append(("wait", self.sem[k], v))

    def emit(self, eng, opname, args, kwargs=None, waits=(), inc=True,
             inc_amt=1, sem_key=None):
        for t in waits:
            self._wait(eng, t)
        key = sem_key or eng
        if inc:
            self.cnt[key] += inc_amt
            v = self.cnt[key]
            self.stream[eng].append(
                ("op", opname, args, kwargs or {}, self.sem[key], inc_amt))
            return (key, v)
        self.stream[eng].append(("op", opname, args, kwargs or {}, None, 0))
        return None

    def mm(self, out, lhsT, rhs, start, stop, waits=(), inc=False):
        return self.emit("tensor", "matmul", (out, lhsT, rhs),
                         {"start": start, "stop": stop}, waits, inc)

    def tp(self, out, in_, ident, waits=(), inc=True):
        return self.emit("tensor", "transpose", (out, in_, ident), {},
                         waits, inc)

    def act(self, out, in_, func, waits=(), bias=0.0, scale=1.0,
            accum_out=None, inc=True):
        kw = {"bias": bias, "scale": scale}
        if accum_out is not None:
            kw["accum_out"] = accum_out
        return self.emit("scalar", "activation", (out, in_, func), kw,
                         waits, inc)

    def vec(self, opname, *args, waits=(), inc=True, **kwargs):
        return self.emit("vector", opname, args, kwargs, waits, inc)

    def gp(self, opname, *args, waits=(), inc=True, **kwargs):
        return self.emit("gpsimd", opname, args, kwargs, waits, inc)

    def dma(self, q, out, in_, waits=(), **kw):
        qk = DMA_Q[q]
        ln = self.lane_idx[qk]
        self.lane_idx[qk] = (ln + 1) % NLANES
        lane = f"{qk}{ln}"
        prev = (lane, self.cnt[lane])
        return self.emit(q, "dma_start", (), {"out": out, "in_": in_, **kw},
                         list(waits) + [prev], inc=True, inc_amt=16,
                         sem_key=lane)

    def now(self, eng):
        return (eng, self.cnt[eng])

    def barrier(self):
        toks = [(k, self.cnt[k]) for k in self.cnt]
        for e in ENGS:
            for t in toks:
                self._wait(e, t)

    def finalize(self, final_waits=()):
        for t in final_waits:
            self._wait("sync", t)
        with self.nc.Block() as block:
            for e in ENGS:
                stream = self.stream[e]

                def run(E, stream=stream):
                    for rec in stream:
                        if rec[0] == "wait":
                            E.wait_ge(rec[1], rec[2])
                        else:
                            _, opname, args, kwargs, sem, amt = rec
                            r = getattr(E, opname)(*args, **kwargs)
                            if sem is not None:
                                r.then_inc(sem, amt)
                getattr(block, e)(run)
        self.es.close()


class PsumPool:

    def __init__(self, es, nc, name, n, shape=(128, 512), dtype=F32,
                 tiles=None):
        if tiles is None:
            tiles = [es.enter_context(
                nc.psum_tensor(f"{name}{i}", list(shape), dtype))
                for i in range(n)]
        self.tiles = tiles
        self.last_use = [None] * len(tiles)
        self.i = 0

    def acquire(self):
        i = self.i
        self.i = (self.i + 1) % len(self.tiles)
        return self.tiles[i], self.last_use[i], i

    def set_last_use(self, idx, tok):
        self.last_use[idx] = tok


def cdiv(a, b):
    return (a + b - 1) // b


class LB:

    def __init__(self, name):
        self.nc = bass.Bass("TRN2", target_bir_lowering=False, name=name)
        self.P = Prog(self.nc)
        self.es = self.P.es
        self._scopes = [self.es]
        self._uid = 0
        self.out_dma_toks = []
        self.pg = PsumPool(self.es, self.nc, "pg", 7, (128, 512))
        self.psbf = PsumPool(self.es, self.nc, "psbf", 1, (128, 512), BF16)

    def push_scope(self):
        es = ExitStack()
        self._scopes.append(es)
        return es

    def pop_scope(self):
        es = self._scopes.pop()
        es.close()
        self.P.barrier()

    def finalize(self):
        while len(self._scopes) > 1:
            self._scopes.pop().close()
        self.P.finalize(self.out_dma_toks)

    def sb(self, name, shape, dtype):
        self._uid += 1
        return self._scopes[-1].enter_context(
            self.nc.sbuf_tensor(f"{name}_{self._uid}", list(shape), dtype))

    def dram_in(self, name, shape, dtype=F32):
        if not hasattr(self, "_din"):
            self._din = {}
        if name not in self._din:
            self._din[name] = self.nc.dram_tensor(
                name, list(shape), dtype, kind="ExternalInput").ap()
        return self._din[name]

    def dram_out(self, name, shape, dtype=F32):
        return self.nc.dram_tensor(name, list(shape), dtype,
                                   kind="ExternalOutput").ap()

    def dram_tmp(self, name, shape, dtype=F32):
        return self.nc.dram_tensor(name, list(shape), dtype,
                                   kind="Internal").ap()


    def load_w(self, name, shape, out_dtype=BF16, q="gpsimd", col0=None,
               col1=None, rowsplit=128, declared=None, row0=0, row1=None):
        ap = declared if declared is not None else self.dram_in(
            name, shape, F32)
        rows, cols = shape
        apx = ap
        if col0 is not None:
            apx = apx[:, col0:col1]
            cols = col1 - col0
        if row1 is not None:
            apx = apx[row0:row1, :]
            rows = row1 - row0
        ntile = cdiv(rows, rowsplit)
        tiles, toks = [], []
        queue = q if out_dtype != F32 else "sync"
        for i in range(ntile):
            r0, r1 = i * rowsplit, min(rows, (i + 1) * rowsplit)
            t = self.sb(f"w_{name}", (r1 - r0, cols), out_dtype)
            toks.append(self.P.dma(queue, t[:], apx[r0:r1, :]))
            tiles.append(t)
        return tiles, toks

    def load_row(self, name, n, out_dtype=F32, src=None):
        ap = src if src is not None else self.dram_in(name, (n,), F32)
        t = self.sb(f"r_{name}", (1, n), out_dtype)
        q = "gpsimd" if out_dtype != F32 else "sync"
        tok = self.P.dma(q, t[:], ap[None, :])
        return t, tok

    def load_col(self, name, n, out_dtype=F32, src=None):
        ap = src if src is not None else self.dram_in(name, (n,), F32)
        ntile = cdiv(n, 128)
        tiles, toks = [], []
        q = "gpsimd" if out_dtype != F32 else "sync"
        for i in range(ntile):
            r0, r1 = i * 128, min(n, (i + 1) * 128)
            t = self.sb(f"c_{name}", (r1 - r0, 1), out_dtype)
            toks.append(self.P.dma(q, t[:], ap[r0:r1, None]))
            tiles.append(t)
        return tiles, toks

    def consts(self):
        P = self.P
        ident = self.dram_in("c_ident", (128, 128), F32)
        self.identf = self.sb("identf", (128, 128), F32)
        self.identb = self.sb("identb", (128, 128), BF16)
        self.tok_identf = P.dma("sync", self.identf[:], ident)
        self.tok_identb = P.dma("gpsimd", self.identb[:], ident)
        self.onescol_f = self.sb("onescol_f", (128, 1), F32)
        self.onesrow_f = self.sb("onesrow_f", (1, 128), F32)
        self.onesrow_b = self.sb("onesrow_b", (1, 128), BF16)
        self.onescol_b = self.sb("onescol_b", (128, 1), BF16)
        self.tok_onescol = P.vec("memset", self.onescol_f[:], 1.0)
        self.tok_onesf = P.vec("memset", self.onesrow_f[:], 1.0)
        self.tok_ones = P.vec("memset", self.onesrow_b[:], 1.0)
        self.tok_onescol_b = P.vec("memset", self.onescol_b[:], 1.0)
        self.c_eps = self.sb("c_eps", (1, 1), F32)
        self.tok_eps = P.vec("memset", self.c_eps[:], 1e-5)
        self.c_one = self.sb("c_one", (1, 1), F32)
        self.tok_one = P.vec("memset", self.c_one[:], 1.0)
        self.warm_src = self.sb("warm_src", (128, 128), BF16)
        P.vec("memset", self.warm_src[:], 1.0)
        P.warm_dummy = (self.psbf.tiles[0], self.warm_src)


    def load_x_tokmajor(self, ap, xT):
        P = self.P
        tmps = [self.sb("ldx_tm", (128, D), F32) for _ in range(2)]
        cons = [None, None]
        x_toks = [None] * NDT
        for tt in range(T // 128):
            buf = tmps[tt % 2]
            ld = P.dma("sync", buf[:], ap[tt * 128:(tt + 1) * 128, :],
                       waits=[cons[tt % 2]])
            last_mt = None
            for i in range(NDT):
                ps, war, idx = self.pg.acquire()
                mt = P.tp(ps[:, 0:128], buf[:, i * 128:(i + 1) * 128],
                          self.identf[:],
                          waits=[ld, self.tok_identf, war], inc=True)
                ct = P.vec("tensor_copy", xT[i][:, tt * 128:(tt + 1) * 128],
                           ps[:, 0:128], waits=[mt])
                self.pg.set_last_use(idx, ct)
                x_toks[i] = ct
                last_mt = mt
            cons[tt % 2] = last_mt
        return x_toks

    def load_x_fm(self, ap, xT):
        return [self.P.dma("sync", xT[i][:], ap[i * 128:(i + 1) * 128, :])
                for i in range(len(xT))]

    def store_x_fm(self, xT, x_toks, ap):
        for i in range(NDT):
            tok = self.P.dma("sync", ap[i * 128:(i + 1) * 128, :], xT[i][:],
                             waits=[x_toks[i]])
            self.out_dma_toks.append(tok)


    def proj(self, W, Wtoks, X, Xtoks, out_rows, evac, t_width=T,
             slab=SLAB, w_col0=0, extra_waits=(), pool=None):
        P = self.P
        pool = pool or self.pg
        nkc = len(W)
        noc = cdiv(out_rows, 128)
        for s in range(cdiv(t_width, slab)):
            s0, s1 = s * slab, min(t_width, (s + 1) * slab)
            for oc in range(noc):
                o0, o1 = oc * 128, min(out_rows, (oc + 1) * 128)
                ps, war, idx = pool.acquire()
                mm_tok = None
                for kc in range(nkc):
                    first, last = kc == 0, kc == nkc - 1
                    waits = [Wtoks[kc], Xtoks[kc]] + list(extra_waits)
                    if first:
                        waits.append(war)
                    mm_tok = P.mm(ps[0:o1 - o0, 0:s1 - s0],
                                  W[kc][:, w_col0 + o0:w_col0 + o1],
                                  X[kc][:, s0:s1], first, last,
                                  waits=waits, inc=last)
                use = evac(oc, s0, s1, ps, mm_tok)
                pool.set_last_use(idx, use)


    def norm_fm(self, X, Xtoks, kind, writeback, eps=1e-5):
        P = self.P
        sqb = [self.sb("nrm_sq", (128, SLAB), F32)]
        sq_prev = [None]
        out_toks = [None] * NDT
        r_mu = self.sb("r_mu", (1, SLAB), F32)
        r_var = self.sb("r_var", (1, SLAB), F32)
        r_scale = self.sb("r_scale", (1, SLAB), F32)
        r_shift = self.sb("r_shift", (1, SLAB), F32)
        prev_bt = None
        for s in range(NSLAB):
            s0, s1 = s * SLAB, (s + 1) * SLAB
            psA, warA, idxA = self.pg.acquire()
            psB, warB, idxB = self.pg.acquire()
            mm_tok = None
            for i in range(NDT):
                b = sqb[0]
                sq_tok = P.act(b[:], X[i][:, s0:s1], AF.Square,
                               waits=[Xtoks[i], sq_prev[0]])
                sq_prev[0] = sq_tok
                first, last = i == 0, i == NDT - 1
                xdt = X[i][:].dtype
                ocol = self.onescol_f if xdt == F32 else self.onescol_b
                ocol_tok = (self.tok_onescol if xdt == F32
                            else self.tok_onescol_b)
                if kind == "ln":
                    P.mm(psA[0:1, 0:SLAB], ocol[:], X[i][:, s0:s1],
                         first, last,
                         waits=[Xtoks[i], ocol_tok,
                                warA if first else None])
                mm_tok = P.mm(psB[0:1, 0:SLAB], self.onescol_f[:],
                              b[:], first, last,
                              waits=[sq_tok, self.tok_onescol,
                                     warB if first else None],
                              inc=True)
                sq_prev[0] = mm_tok
            if kind == "ln":
                tmu = P.act(r_mu[:], psA[0:1, 0:SLAB], AF.Copy,
                            scale=1.0 / D, waits=[mm_tok, prev_bt])
                tmsq = P.act(r_var[:], psB[0:1, 0:SLAB], AF.Copy,
                             scale=1.0 / D, waits=[mm_tok, tmu])
                tmm = P.vec("scalar_tensor_tensor", r_scale[:], r_mu[:], -1.0,
                            r_mu[:], op0=ALU.mult, op1=ALU.mult, waits=[tmu])
                tv = P.vec("tensor_add", r_var[:], r_var[:], r_scale[:],
                           waits=[tmsq, tmm])
                tl = P.act(r_var[:], r_var[:], AF.Ln, bias=self.c_eps[:],
                           waits=[tv, self.tok_eps])
                ts = P.act(r_scale[:], r_var[:], AF.Exp, scale=-0.5,
                           waits=[tl])
                tsh = P.vec("scalar_tensor_tensor", r_shift[:], r_mu[:], -1.0,
                            r_scale[:], op0=ALU.mult, op1=ALU.mult,
                            waits=[tmu, ts])
                row_toks = [ts, tsh]
            else:
                tl = P.act(r_var[:], psB[0:1, 0:SLAB], AF.Ln,
                           scale=1.0 / D, waits=[mm_tok, prev_bt])
                tq = P.act(r_mu[:], r_var[:], AF.Exp, scale=0.5, waits=[tl])
                ta = P.vec("tensor_scalar_add", r_mu[:], r_mu[:], eps,
                           waits=[tq])
                tr = P.vec("reciprocal", r_scale[:], r_mu[:], waits=[ta])
                row_toks = [tr]
            bt0 = P.mm(psA[:, 0:SLAB], self.onesrow_f[:], r_scale[:],
                       True, True,
                       waits=row_toks + [self.tok_onesf], inc=True)
            bt = bt0
            if kind == "ln":
                bt = P.mm(psB[:, 0:SLAB], self.onesrow_f[:],
                          r_shift[:], True, True,
                          waits=row_toks + [bt0], inc=True)
            prev_bt = bt
            last_use = None
            for i in range(NDT):
                tok = writeback(i, s0, s1, psA, psB, bt)
                out_toks[i] = tok
                last_use = tok
            self.pg.set_last_use(idxA, last_use)
            self.pg.set_last_use(idxB, last_use)
        return out_toks



def emit_l0(lb, x0, xnf):
    P = lb.P
    yblk_b = lb.sb("yblk", (8, D), BF16)

    lb.push_scope()
    xT = [lb.sb("xT", (128, T), F32) for _ in range(NDT)]
    x_toks = lb.load_x_tokmajor(x0, xT)
    ln_tmp = [lb.sb("ln_tmp", (128, SLAB), F32) for _ in range(2)]
    ln_prev = [None, None]

    def ln_writeback(i, s0, s1, psA, psB, bt):
        tmp = ln_tmp[i % 2]
        ta = P.vec("tensor_mul", tmp[:], xT[i][:, s0:s1], psA[:, 0:SLAB],
                   waits=[x_toks[i], bt, ln_prev[i % 2]])
        tb = P.vec("tensor_add", xnf[i][:, s0:s1], tmp[:],
                   psB[:, 0:SLAB], waits=[ta])
        ln_prev[i % 2] = tb
        return tb

    xn_toks = lb.norm_fm(xT, x_toks, "ln", ln_writeback)
    lb.pop_scope()

    NBLK = T // 256
    ain = float(np.exp(0.125) / (256 * np.exp(0.125) + 1792))
    aout = float(1.0 / (256 * np.exp(0.125) + 1792))

    lb.push_scope()
    wv, wv_tok = lb.load_w("l0_wv", (D, D))
    wout, wout_tok = lb.load_w("l0_wout", (D, D))
    bv_col, bv_tok = lb.load_col("l0_bv", D)
    bout_b, bout_btok = lb.load_row("l0_bout", D, out_dtype=BF16)
    ab_b = [lb.sb("ab", (128, NBLK), BF16) for _ in range(NDT)]
    ab_toks = []
    for i in range(NDT):
        sxn = lb.sb("sxn", (128, NBLK), F32)
        stot = lb.sb("stot", (128, 1), F32)
        x3 = xnf[i].rearrange("p (b c) -> p b c", b=NBLK)
        t1 = P.vec("reduce_sum", sxn[:, :, None], x3, axis=AX.X,
                   waits=[xn_toks[i]])
        t2 = P.vec("reduce_sum", stot[:, :, None], sxn[:, None, :],
                   axis=AX.X, waits=[t1])
        t3 = P.vec("tensor_scalar_mul", sxn[:], sxn[:], ain - aout,
                   waits=[t1, t2])
        t4 = P.vec("tensor_scalar_mul", stot[:], stot[:], aout, waits=[t2])
        t5 = P.vec("tensor_scalar", ab_b[i][:], sxn[:], stot[:], None,
                   op0=ALU.add, waits=[t3, t4])
        ab_toks.append(t5)
    r_b = [lb.sb("r0a", (128, NBLK), BF16) for _ in range(NDT)]
    r_toks = [None] * NDT

    def evac_r(oc, s0, s1, ps, mm_tok):
        tok = P.vec("tensor_scalar", r_b[oc][:, s0:s1], ps[0:128, 0:NBLK],
                    bv_col[oc][:], None, op0=ALU.add,
                    waits=[mm_tok, bv_tok[oc]])
        r_toks[oc] = tok
        return tok

    lb.proj(wv, wv_tok, ab_b, ab_toks, D, evac_r, t_width=NBLK, slab=NBLK)

    yb_tok = None
    for half in range(2):
        c0, c1 = half * 512, (half + 1) * 512
        psy, war_y, psy_i = lb.pg.acquire()
        ytok = None
        for kc in range(NDT):
            P.mm(psy[0:NBLK, 0:512], r_b[kc][:], wout[kc][:, c0:c1],
                 kc == 0, False,
                 waits=[r_toks[kc], wout_tok[kc],
                        war_y if kc == 0 else None])
        ytok = P.mm(psy[0:NBLK, 0:512], lb.onesrow_b[0:1, 0:NBLK],
                    bout_b[0:1, c0:c1], False, True,
                    waits=[bout_btok, lb.tok_ones], inc=True)
        yb_tok = P.vec("tensor_copy", yblk_b[0:NBLK, c0:c1],
                       psy[0:NBLK, 0:512], waits=[ytok, yb_tok])
        lb.pg.set_last_use(psy_i, yb_tok)
    lb.pop_scope()

    cur_toks = list(xn_toks)
    cur_toks = emit_l0_mla(lb, xnf, cur_toks, stop=None)
    cur_toks = emit_l0_moe(lb, xnf, cur_toks)

    lb.push_scope()
    eblk_t, tok_eblk_l = lb.load_w("c_eblk", (8, T), BF16, rowsplit=8)
    eblk, tok_eblk = eblk_t[0], tok_eblk_l[0]
    for oc in range(NDT):
        for s in range(NSLAB):
            s0, s1 = s * SLAB, (s + 1) * SLAB
            ps, war, idx = lb.pg.acquire()
            et = P.mm(ps[:], yblk_b[0:NBLK, oc * 128:(oc + 1) * 128],
                      eblk[0:NBLK, s0:s1], True, True,
                      waits=[yb_tok, tok_eblk, war], inc=True)
            at = P.vec("tensor_add", xnf[oc][:, s0:s1], xnf[oc][:, s0:s1],
                       ps[:], waits=[et, cur_toks[oc]])
            lb.pg.set_last_use(idx, at)
            cur_toks[oc] = at
    xin_buf = [lb.sb("xin", (128, D), F32) for _ in range(2)]
    xin_tok = [None, None]
    for tt in range(T // 128):
        buf = xin_buf[tt % 2]
        ld = P.dma("sync", buf[:], x0[tt * 128:(tt + 1) * 128, :],
                   waits=[xin_tok[tt % 2]])
        last = None
        for oc in range(NDT):
            ps, war, idx = lb.pg.acquire()
            mt = P.tp(ps[:, 0:128], buf[:, oc * 128:(oc + 1) * 128],
                      lb.identf[:], waits=[ld, lb.tok_identf, war], inc=True)
            at = P.vec("tensor_add", xnf[oc][:, tt * 128:(tt + 1) * 128],
                       xnf[oc][:, tt * 128:(tt + 1) * 128], ps[:, 0:128],
                       waits=[mt, cur_toks[oc]])
            lb.pg.set_last_use(idx, at)
            cur_toks[oc] = at
            last = at
        xin_tok[tt % 2] = last
    lb.pop_scope()
    return cur_toks


def emit_l0_mla(lb, xnf, cur_toks, stop=None):
    P = lb.P
    NH, LAT = 16, 256
    cur_toks = list(cur_toks)

    lb.push_scope()
    cmask_t, cmask_tokl = lb.load_w("c_cmask01", (128, 4 * SLAB), BF16)
    cmask, tok_cmask = cmask_t[0], cmask_tokl[0]

    wln1_col, wln1_tok = lb.load_col("l0_wln1", D)
    xr_b = [lb.sb("xr", (128, T), BF16) for _ in range(NDT)]
    rms_tmp = [lb.sb("rms_tmp", (128, SLAB), F32)]
    rms_prev = [None]

    def rms_wb(i, s0, s1, psA, psB, bt):
        tmp = rms_tmp[0]
        ta = P.vec("tensor_mul", tmp[:], xnf[i][:, s0:s1], psA[:, 0:SLAB],
                   waits=[cur_toks[i], bt, rms_prev[0]])
        tb = P.vec("tensor_scalar", xr_b[i][:, s0:s1], tmp[:],
                   wln1_col[i][:], None, op0=ALU.mult,
                   waits=[ta, wln1_tok[i]])
        rms_prev[0] = tb
        return tb

    xr_toks = lb.norm_fm(xnf, cur_toks, "rms", rms_wb)
    if stop == "rms":
        lb.pop_scope()
        return cur_toks

    kvl_b = [lb.sb("kvl", (128, T), BF16) for _ in range(2)]
    ql_b = [lb.sb("ql", (128, T), BF16) for _ in range(2)]
    kvl_toks = [None] * 2
    ql_toks = [None] * 2
    lb.push_scope()
    wkvd, wkvd_tok = lb.load_w("l0_wkvd", (D, LAT))
    wqd, wqd_tok = lb.load_w("l0_wqd", (D, LAT))

    def mk_evac(dst, dtoks):
        def evac(oc, s0, s1, ps, mm_tok):
            tok = P.vec("tensor_copy", dst[oc][:, s0:s1],
                        ps[0:128, 0:s1 - s0], waits=[mm_tok])
            dtoks[oc] = tok
            return tok
        return evac

    lb.proj(wkvd, wkvd_tok, xr_b, xr_toks, LAT, mk_evac(kvl_b, kvl_toks))
    lb.proj(wqd, wqd_tok, xr_b, xr_toks, LAT, mk_evac(ql_b, ql_toks))
    lb.pop_scope()
    if stop == "latents":
        lb.pop_scope()
        return cur_toks

    for hg in range(2):
        lb.push_scope()
        KK = [lb.sb("KK", (128, T), BF16) for _ in range(4)]
        QQ = [lb.sb("QQ", (128, T), BF16) for _ in range(4)]
        V = [lb.sb("V", (128, 520), BF16) for _ in range(T // 128)]
        KK_toks = [None] * 4
        QQ_toks = [None] * 4
        V_toks = [None] * (T // 128)
        co = hg * 256
        lb.push_scope()
        rop_tmp = [lb.sb("rop_tmp", (64, SLAB), F32) for _ in range(2)]
        cos1_t, cos1_tok = lb.load_w("c_cos", (32, T), BF16, rowsplit=32)
        sinn1_t, sinn1_tok = lb.load_w("c_sinn", (32, T), BF16, rowsplit=32)
        cos1, sinn1 = cos1_t[0], sinn1_t[0]
        tok_cos1, tok_sinn1 = cos1_tok[0], sinn1_tok[0]
        wku, wku_tok = lb.load_w("l0_wku", (LAT, 512), col0=co, col1=co + 256)
        wqu, wqu_tok = lb.load_w("l0_wqu", (LAT, 512), col0=co, col1=co + 256)
        wrq, wrq_tok = lb.load_w("l0_wropeq", (LAT, 512), col0=co,
                                 col1=co + 256)
        wrqp, wrqp_tok = lb.load_w("l0_wropeqp", (LAT, 512), col0=co,
                                   col1=co + 256)
        wrk, wrk_tok = lb.load_w("l0_wropek", (D, 512), col0=co,
                                 col1=co + 256)
        wrkp, wrkp_tok = lb.load_w("l0_wropekp", (D, 512), col0=co,
                                   col1=co + 256)
        wvu, wvu_tok = lb.load_w("l0_wvu", (LAT, D), col0=hg * 512,
                                 col1=hg * 512 + 512)

        rop_war = [None]

        def emit_pair(dst, toks_arr, g4, wk, wk_tok, xin, xin_toks,
                      wr1, wr1_tok, wr2, wr2_tok, rin, rin_toks):
            last = rop_war[0]
            for s in range(NSLAB):
                s0, s1 = s * SLAB, (s + 1) * SLAB
                ps, war, idx = lb.pg.acquire()
                mmk = None
                for kc in range(len(wk)):
                    mmk = P.mm(ps[0:64, 0:SLAB],
                               wk[kc][:, g4 * 64:(g4 + 1) * 64],
                               xin[kc][:, s0:s1], kc == 0,
                               kc == len(wk) - 1,
                               waits=[wk_tok[kc], xin_toks[kc],
                                      war if kc == 0 else None],
                               inc=(kc == len(wk) - 1))
                c1 = P.vec("tensor_copy", dst[g4][0:32, s0:s1],
                           ps[0:32, 0:SLAB], waits=[mmk, last])
                c2 = P.vec("tensor_copy", dst[g4][64:96, s0:s1],
                           ps[32:64, 0:SLAB], waits=[mmk])
                lb.pg.set_last_use(idx, c2)
                ps1, war1, idx1 = lb.pg.acquire()
                mm1 = None
                for kc in range(len(wr1)):
                    mm1 = P.mm(ps1[0:64, 0:SLAB],
                               wr1[kc][:, g4 * 64:(g4 + 1) * 64],
                               rin[kc][:, s0:s1], kc == 0,
                               kc == len(wr1) - 1,
                               waits=[wr1_tok[kc], rin_toks[kc],
                                      war1 if kc == 0 else None],
                               inc=(kc == len(wr1) - 1))
                ps2, war2, idx2 = lb.pg.acquire()
                mm2 = None
                for kc in range(len(wr2)):
                    mm2 = P.mm(ps2[0:64, 0:SLAB],
                               wr2[kc][:, g4 * 64:(g4 + 1) * 64],
                               rin[kc][:, s0:s1], kc == 0,
                               kc == len(wr2) - 1,
                               waits=[wr2_tok[kc], rin_toks[kc],
                                      war2 if kc == 0 else None],
                               inc=(kc == len(wr2) - 1))
                tm1, tm2 = rop_tmp[0], rop_tmp[1]
                v1a = P.vec("tensor_mul", tm1[0:32, :], ps1[0:32, 0:SLAB],
                            cos1[0:32, s0:s1], waits=[mm1, tok_cos1, last])
                v1 = P.vec("tensor_mul", tm1[32:64, :], ps1[32:64, 0:SLAB],
                           cos1[0:32, s0:s1], waits=[mm1, v1a])
                lb.pg.set_last_use(idx1, v1)
                v2a = P.vec("tensor_mul", tm2[0:32, :], ps2[0:32, 0:SLAB],
                            sinn1[0:32, s0:s1], waits=[mm2, tok_sinn1, last])
                v2 = P.vec("tensor_mul", tm2[32:64, :], ps2[32:64, 0:SLAB],
                           sinn1[0:32, s0:s1], waits=[mm2, v2a])
                lb.pg.set_last_use(idx2, v2)
                a1 = P.vec("tensor_add", dst[g4][32:64, s0:s1], tm1[0:32, :],
                           tm2[0:32, :], waits=[v1a, v2a, v1, v2])
                a2 = P.vec("tensor_add", dst[g4][96:128, s0:s1],
                           tm1[32:64, :], tm2[32:64, :], waits=[v1, v2])
                last = a2
            rop_war[0] = a2
            toks_arr[g4] = last

        for g4 in range(4):
            emit_pair(KK, KK_toks, g4, wku, wku_tok, kvl_b, kvl_toks,
                      wrk, wrk_tok, wrkp, wrkp_tok, xr_b, xr_toks)
            emit_pair(QQ, QQ_toks, g4, wqu, wqu_tok, ql_b, ql_toks,
                      wrq, wrq_tok, wrqp, wrqp_tok, ql_b, ql_toks)
        if stop == "kkqq":
            lb.pop_scope()
            lb.pop_scope()
            lb.pop_scope()
            return cur_toks

        for tt in range(T // 128):
            ms = P.vec("memset",
                       V[tt].rearrange("p (h c) -> p h c", c=65)[:, :, 64:65],
                       1.0, waits=[V_toks[tt]])
            ps, war, idx = lb.pg.acquire()
            mmv = None
            for kc in range(2):
                mmv = P.mm(ps[0:128, 0:SLAB],
                           kvl_b[kc][:, tt * 128:(tt + 1) * 128],
                           wvu[kc][:], kc == 0, kc == 1,
                           waits=[kvl_toks[kc], wvu_tok[kc],
                                  war if kc == 0 else None], inc=(kc == 1))
            dview = V[tt].rearrange("p (h c) -> p h c", c=65)[:, 0:8, 0:64]
            sview = ps[0:128, 0:SLAB].rearrange("p (h c) -> p h c", c=64)
            cv = P.vec("tensor_copy", dview, sview, waits=[mmv, ms])
            lb.pg.set_last_use(idx, cv)
            V_toks[tt] = cv
        lb.pop_scope()
        if stop == "v":
            lb.pop_scope()
            lb.pop_scope()
            return cur_toks
        wo4, wo4_tok = lb.load_w("l0_wo", (D, D), row0=hg * 512,
                                 row1=hg * 512 + 512)

        negU = lb.sb("negU", (128, 8), F32)
        negU_toks = [None] * 8
        lb.push_scope()
        sqt = lb.sb("sqt", (128, T), BF16)
        m12 = [lb.sb("mkq", (1, 1), F32) for _ in range(2)]
        mtmp = lb.sb("mtmp", (1, 1), F32)
        uval = lb.sb("uval", (1, 1), F32)
        sq_war = {0: None, 64: None}
        for h in range(8):
            g4, hb = h // 2, (h % 2) * 64
            toks_nrm = []
            for j, (MAT, MT) in enumerate(((KK, KK_toks), (QQ, QQ_toks))):
                sq_tok = P.act(sqt[hb:hb + 64, :], MAT[g4][hb:hb + 64, :],
                               AF.Square,
                               waits=[MT[g4], sq_war[hb]] + toks_nrm)
                mj = P.vec("memset", m12[j][:], 0.0, waits=toks_nrm)
                ps, war, idx = lb.pg.acquire()
                for s in range(NSLAB):
                    mmn = P.mm(ps[0:1, 0:SLAB],
                               lb.onescol_b[hb:hb + 64, :],
                               sqt[hb:hb + 64, s * SLAB:(s + 1) * SLAB],
                               True, True,
                               waits=[sq_tok, lb.tok_onescol_b,
                                      war if s == 0 else None,
                                      toks_nrm[-1] if toks_nrm else None],
                               inc=True)
                    c1 = P.vec("reduce_max", mtmp[:, :, None],
                               ps[0:1, None, 0:SLAB], axis=AX.X,
                               waits=[mmn, mj] + toks_nrm[-1:])
                    c2 = P.vec("tensor_max", m12[j][:],
                               m12[j][:], mtmp[:], waits=[c1])
                    toks_nrm.append(c2)
                lb.pg.set_last_use(idx, toks_nrm[-1])
                sq_war[hb] = toks_nrm[-1]
            t2 = P.vec("tensor_mul", m12[0][:], m12[0][:], m12[1][:],
                       waits=toks_nrm)
            t3 = P.act(uval[:], m12[0][:], AF.Ln, waits=[t2])
            t4 = P.act(uval[:], uval[:], AF.Exp, scale=0.5, waits=[t3])
            t5 = P.vec("tensor_scalar_mul", uval[:], uval[:], -0.125,
                       waits=[t4])
            ps, war, idx = lb.pg.acquire()
            t6 = P.mm(ps[0:128, 0:1], lb.onesrow_f[:], uval[:], True, True,
                      waits=[t5, lb.tok_onesf, war], inc=True)
            t7 = P.vec("tensor_copy", negU[:, h:h + 1], ps[0:128, 0:1],
                       waits=[t6])
            lb.pg.set_last_use(idx, t7)
            negU_toks[h] = t7
        lb.pop_scope()

        if stop == "u":
            lb.pop_scope()
            lb.pop_scope()
            return cur_toks
        P.barrier()
        poolS = PsumPool(None, None, "sS", 0,
                         tiles=lb.pg.tiles[0:3] + lb.pg.tiles[6:7])
        poolPV = PsumPool(None, None, "sPV", 0, tiles=lb.pg.tiles[3:5])
        poolN = PsumPool(None, None, "sN", 0, tiles=lb.pg.tiles[5:6])
        NPB = 4
        Pb = [lb.sb("Pb", (128, SLAB), BF16) for _ in range(NPB)]
        pb_war = [None] * NPB
        yn4 = [lb.sb("yn4", (128, SLAB), BF16) for _ in range(4)]
        yn_toks = [None] * 8
        yn_war = [None] * 8
        bcn = [lb.sb("bcn", (64, SLAB), F32) for _ in range(2)]
        bcn_war = [None, None]
        rrow = [lb.sb("rrow", (1, SLAB), F32) for _ in range(2)]
        rrow_war = [None, None]
        MOFF = {0: 0, 128: 1, 256: 2, 384: 3}
        qs_max = {"attn1": 1, "attnq1": 1, "attnq2": 2, "attnh8": 1,
                  "attnwo1": 1}.get(stop, NSLAB)
        h_max = {"attn1": 1, "attnwo1": 1}.get(stop, 8)
        do_wo = stop not in ("attn1", "attnh8")
        for qs in range(qs_max):
            q0, q1 = qs * SLAB, (qs + 1) * SLAB
            for h in range(h_max):
                g4, hb = h // 2, (h % 2) * 64
                pspv, warpv, ipv = poolPV.acquire()
                nkc = 4 * (qs + 1)
                pvmm = None
                pend = None
                for kc in range(nkc):
                    psS, warS, iS = poolS.acquire()
                    smm = P.mm(psS[0:128, 0:SLAB],
                               KK[g4][hb:hb + 64, kc * 128:(kc + 1) * 128],
                               QQ[g4][hb:hb + 64, q0:q1], True, True,
                               waits=[KK_toks[g4], QQ_toks[g4], warS],
                               inc=True)
                    pb = Pb[kc % NPB]
                    e = P.act(pb[:], psS[0:128, 0:SLAB], AF.Exp, scale=0.125,
                              bias=negU[:, h:h + 1],
                              waits=[smm, negU_toks[h], pb_war[kc % NPB]])
                    poolS.set_last_use(iS, e)
                    ptok = e
                    if kc >= qs * 4:
                        mi = MOFF[kc * 128 - qs * 512]
                        ptok = P.vec("tensor_mul", pb[:], pb[:],
                                     cmask[:, mi * 512:(mi + 1) * 512],
                                     waits=[e, tok_cmask])
                    if pend is not None:
                        pkc, pptok = pend
                        pvmm = P.mm(pspv[0:65, 0:SLAB],
                                    V[pkc][:, h * 65:(h + 1) * 65],
                                    Pb[pkc % NPB][:], pkc == 0, False,
                                    waits=[V_toks[pkc], pptok,
                                           warpv if pkc == 0 else None],
                                    inc=True)
                        pb_war[pkc % NPB] = pvmm
                    pend = (kc, ptok)
                pkc, pptok = pend
                pvmm = P.mm(pspv[0:65, 0:SLAB],
                            V[pkc][:, h * 65:(h + 1) * 65],
                            Pb[pkc % NPB][:], pkc == 0, True,
                            waits=[V_toks[pkc], pptok,
                                   warpv if pkc == 0 else None],
                            inc=True)
                pb_war[pkc % NPB] = pvmm
                rr = rrow[h % 2]
                tl = P.act(rr[:], pspv[64:65, 0:SLAB], AF.Ln,
                           waits=[pvmm, rrow_war[h % 2]])
                t1 = P.act(rr[:], rr[:], AF.Exp, scale=-1.0, waits=[tl])
                psn, warn, inn = poolN.acquire()
                t2 = P.mm(psn[0:64, 0:SLAB], lb.onesrow_f[0:1, 0:64], rr[:],
                          True, True, waits=[t1, warn, lb.tok_onesf],
                          inc=True)
                rrow_war[h % 2] = t2
                bc = bcn[h % 2]
                t3 = P.vec("tensor_copy", bc[:], psn[0:64, 0:SLAB],
                           waits=[t2, bcn_war[h % 2]])
                poolN.set_last_use(inn, t3)
                t4 = P.vec("tensor_mul", yn4[g4][hb:hb + 64, :],
                           pspv[0:64, 0:SLAB], bc[:],
                           waits=[t3, yn_war[h]])
                poolPV.set_last_use(ipv, t4)
                bcn_war[h % 2] = t4
                yn_toks[h] = t4
            last_wo = None
            for oc in range(NDT if do_wo else 0):
                psO, warO, iO = poolS.acquire()
                mmo = None
                for g4 in range(4):
                    mmo = P.mm(psO[0:128, 0:SLAB],
                               wo4[g4][:, oc * 128:(oc + 1) * 128],
                               yn4[g4][:], g4 == 0, g4 == 3,
                               waits=[wo4_tok[g4], yn_toks[2 * g4],
                                      yn_toks[2 * g4 + 1],
                                      warO if g4 == 0 else None],
                               inc=(g4 == 3))
                at = P.vec("tensor_add", xnf[oc][:, q0:q1],
                           xnf[oc][:, q0:q1], psO[0:128, 0:SLAB],
                           waits=[mmo, cur_toks[oc]])
                poolS.set_last_use(iO, at)
                cur_toks[oc] = at
                last_wo = at
            for h in range(8):
                yn_war[h] = last_wo
        P.barrier()
        lb.pop_scope()
    lb.pop_scope()
    return cur_toks


def emit_l0_moe(lb, xnf, cur_toks):
    P = lb.P
    E = 7
    cur_toks = list(cur_toks)
    lb.push_scope()
    xr = [lb.sb("xr2", (128, T), BF16) for _ in range(NDT)]
    wrow = [lb.sb("wrow", (1, T), BF16) for _ in range(E)]
    wrow_toks = [None] * E
    lb.push_scope()
    wln2_col, wln2_tok = lb.load_col("l0_wln2", D)
    rms_tmp = [lb.sb("rms2_tmp", (128, SLAB), F32) for _ in range(2)]
    rms_prev = [None, None]

    def rms_wb(i, s0, s1, psA, psB, bt):
        tmp = rms_tmp[i % 2]
        ta = P.vec("tensor_mul", tmp[:], xnf[i][:, s0:s1], psA[:, 0:SLAB],
                   waits=[cur_toks[i], bt, rms_prev[i % 2]])
        tb = P.vec("tensor_scalar", xr[i][:, s0:s1], tmp[:],
                   wln2_col[i][:], None, op0=ALU.mult,
                   waits=[ta, wln2_tok[i]])
        rms_prev[i % 2] = tb
        return tb

    xr_toks = lb.norm_fm(xnf, cur_toks, "rms", rms_wb)
    lb.pop_scope()

    lb.push_scope()
    wr_t, wr_tok = lb.load_w("l0_wrouter", (D, E))
    rb_col, rb_tok = lb.load_col("l0_rbias", E)
    probs_T = lb.sb("probsT", (E, T), F32)
    probs_toks = []
    for s in range(NSLAB):
        s0, s1 = s * SLAB, (s + 1) * SLAB
        ps, war, idx = lb.pg.acquire()
        mmr = None
        for kc in range(NDT):
            mmr = P.mm(ps[0:E, 0:SLAB], wr_t[kc][:], xr[kc][:, s0:s1],
                       kc == 0, kc == NDT - 1,
                       waits=[wr_tok[kc], xr_toks[kc],
                              war if kc == 0 else None],
                       inc=(kc == NDT - 1))
        sg = P.act(probs_T[0:E, s0:s1], ps[0:E, 0:SLAB], AF.Sigmoid,
                   bias=rb_col[0][0:E, :], waits=[mmr, rb_tok[0]])
        lb.pg.set_last_use(idx, sg)
        probs_toks.append(sg)
    ptile = lb.sb("ptile", (128, E), F32)
    junk = lb.sb("junk", (128, E), F32)
    accgt = lb.sb("accgt", (128, 1), F32)
    acceq = lb.sb("acceq", (128, 1), F32)
    zcol = lb.sb("zcol", (128, 1), F32)
    ztok = P.vec("memset", zcol[:], 0.0)
    sel = lb.sb("sel", (128, E), F32)
    den = lb.sb("den", (128, 1), F32)
    for tt in range(T // 128):
        t0 = tt * 128
        ps, war, idx = lb.pg.acquire()
        mt = P.tp(ps[0:128, 0:E], probs_T[0:E, t0:t0 + 128],
                  lb.identf[0:E, 0:E],
                  waits=probs_toks + [lb.tok_identf, war], inc=True)
        cp = P.vec("tensor_copy", ptile[:], ps[0:128, 0:E], waits=[mt])
        lb.pg.set_last_use(idx, cp)
        prev = cp
        sel_toks = []
        for e in range(E):
            t1 = P.vec("tensor_scalar", junk[:], ptile[:],
                       ptile[:, e:e + 1], 0.0, op0=ALU.is_gt,
                       op1=ALU.add, accum_out=accgt[:], waits=[cp, prev])
            if e > 0:
                t2 = P.vec("tensor_scalar", junk[:, 0:e], ptile[:, 0:e],
                           ptile[:, e:e + 1], 0.0, op0=ALU.is_equal,
                           op1=ALU.add, accum_out=acceq[:], waits=[t1])
                t3 = P.vec("tensor_add", accgt[:], accgt[:], acceq[:],
                           waits=[t2])
            else:
                t3 = t1
            t4 = P.vec("tensor_single_scalar", sel[:, e:e + 1], accgt[:],
                       2.0, op=ALU.is_lt, waits=[t3])
            sel_toks.append(t4)
            prev = t4
        t5 = P.vec("tensor_mul", sel[:], sel[:], ptile[:], waits=sel_toks)
        t6 = P.vec("reduce_sum", den[:, :, None], sel[:, None, :],
                   axis=AX.X, waits=[t5])
        t7 = P.vec("reciprocal", den[:], den[:], waits=[t6])
        t8 = P.vec("tensor_scalar", sel[:], sel[:], den[:], None,
                   op0=ALU.mult, waits=[t7])
        for e in range(E):
            ps2, war2, idx2 = lb.pg.acquire()
            mt2 = P.tp(ps2[0:1, 0:128], sel[:, e:e + 1], lb.identf[:],
                       waits=[t8, lb.tok_identf, war2], inc=True)
            cv = P.vec("tensor_copy", wrow[e][0:1, t0:t0 + 128],
                       ps2[0:1, 0:128], waits=[mt2, wrow_toks[e]])
            lb.pg.set_last_use(idx2, cv)
            wrow_toks[e] = cv

    lb.pop_scope()

    wgs_ap = lb.dram_in("l0_wgs", (D, D))
    wus_ap = lb.dram_in("l0_wus", (D, D))
    wds_ap = lb.dram_in("l0_wds", (D, D))
    wgr_ap = lb.dram_in("l0_wgr", (E * D, D))
    wur_ap = lb.dram_in("l0_wur", (E * D, D))
    wdr_ap = lb.dram_in("l0_wdr", (E * D, D))
    gbuf = [lb.sb("gw", (128, D), BF16) for _ in range(NDT)]
    ubuf = [lb.sb("uw", (128, D), BF16) for _ in range(NDT)]
    dbuf = [lb.sb("dw", (128, D), BF16) for _ in range(NDT)]
    hmid = [lb.sb("hmid", (128, SLAB), BF16) for _ in range(NDT)]
    sgb = lb.sb("sgb", (128, SLAB), F32)
    hmt = lb.sb("hmt", (128, SLAB), F32)
    P.barrier()
    poolGU = PsumPool(None, None, "mGU", 0, tiles=lb.pg.tiles[0:4])
    poolW = PsumPool(None, None, "mW", 0, tiles=lb.pg.tiles[4:5])
    poolD = PsumPool(None, None, "mD", 0, tiles=lb.pg.tiles[5:7])

    def _srcs(ei, kc):
        e = ei - 1
        if e < 0:
            return (wgs_ap[kc * 128:(kc + 1) * 128, :],
                    wus_ap[kc * 128:(kc + 1) * 128, :],
                    wds_ap[kc * 128:(kc + 1) * 128, :])
        r0 = e * D + kc * 128
        return (wgr_ap[r0:r0 + 128, :], wur_ap[r0:r0 + 128, :],
                wdr_ap[r0:r0 + 128, :])

    def issue_gu(ei, wait):
        gt, ut = [], []
        for kc in range(NDT):
            gsrc, usrc, _ = _srcs(ei, kc)
            gt.append(P.dma("gpsimd", gbuf[kc][:], gsrc, waits=[wait]))
            ut.append(P.dma("gpsimd", ubuf[kc][:], usrc, waits=[wait]))
        return gt, ut

    def issue_d(ei, wait):
        return [P.dma("gpsimd", dbuf[kc][:], _srcs(ei, kc)[2],
                      waits=[wait]) for kc in range(NDT)]

    gu_next = issue_gu(0, None)
    d_next = issue_d(0, None)
    for ei in range(E + 1):
        e = ei - 1
        g_toks, u_toks = gu_next
        d_toks = d_next
        for s in range(NSLAB):
            s0, s1 = s * SLAB, (s + 1) * SLAB
            if e >= 0:
                psw, warw, idxw = poolW.acquire()
                bw = P.mm(psw[0:128, 0:SLAB], lb.onesrow_b[:],
                          wrow[e][0:1, s0:s1], True, True,
                          waits=[wrow_toks[e], lb.tok_ones, warw], inc=True)
            hm_toks = []
            for ic in range(NDT):
                psg, warg, idxg = poolGU.acquire()
                mmg = None
                for kc in range(NDT):
                    mmg = P.mm(psg[0:128, 0:SLAB],
                               gbuf[kc][:, ic * 128:(ic + 1) * 128],
                               xr[kc][:, s0:s1], kc == 0, kc == NDT - 1,
                               waits=[g_toks[kc], xr_toks[kc],
                                      warg if kc == 0 else None],
                               inc=(kc == NDT - 1))
                psu, waru, idxu = poolGU.acquire()
                mmu = None
                for kc in range(NDT):
                    mmu = P.mm(psu[0:128, 0:SLAB],
                               ubuf[kc][:, ic * 128:(ic + 1) * 128],
                               xr[kc][:, s0:s1], kc == 0, kc == NDT - 1,
                               waits=[u_toks[kc], xr_toks[kc],
                                      waru if kc == 0 else None],
                               inc=(kc == NDT - 1))
                st = P.act(sgb[:], psg[0:128, 0:SLAB], AF.Silu,
                           waits=[mmg, hm_toks[-1] if hm_toks else None])
                poolGU.set_last_use(idxg, st)
                if e >= 0:
                    mt = P.vec("tensor_mul", hmt[:], sgb[:],
                               psu[0:128, 0:SLAB], waits=[st, mmu])
                    poolGU.set_last_use(idxu, mt)
                    ht = P.vec("tensor_mul", hmid[ic][:], hmt[:],
                               psw[0:128, 0:SLAB], waits=[mt, bw])
                else:
                    ht = P.vec("tensor_mul", hmid[ic][:], sgb[:],
                               psu[0:128, 0:SLAB], waits=[st, mmu])
                    poolGU.set_last_use(idxu, ht)
                hm_toks.append(ht)
            if e >= 0:
                poolW.set_last_use(idxw, hm_toks[-1])
            if s == NSLAB - 1 and ei < E:
                gu_next = issue_gu(ei + 1, mmu)
            for oc in range(NDT):
                psd, ward, idxd = poolD.acquire()
                mmd = None
                for ic in range(NDT):
                    mmd = P.mm(psd[0:128, 0:SLAB],
                               dbuf[ic][:, oc * 128:(oc + 1) * 128],
                               hmid[ic][:], ic == 0, ic == NDT - 1,
                               waits=[d_toks[ic], hm_toks[ic],
                                      ward if ic == 0 else None],
                               inc=(ic == NDT - 1))
                at = P.vec("tensor_add", xnf[oc][:, s0:s1],
                           xnf[oc][:, s0:s1], psd[0:128, 0:SLAB],
                           waits=[mmd, cur_toks[oc]])
                poolD.set_last_use(idxd, at)
                cur_toks[oc] = at
        if ei < E:
            d_next = issue_d(ei + 1, P.now("tensor"))
    P.barrier()
    lb.pop_scope()
    return cur_toks




def _rope_tables():
    HH = 32
    inv = 1.0 / (10000.0 ** (np.arange(0, HH, 2, dtype=np.float64) / HH))
    t = np.arange(T, dtype=np.float64)
    freqs = np.outer(t, inv)
    emb = np.concatenate([freqs, freqs], axis=1)
    cos = np.cos(emb).astype(np.float32).T
    sin = np.sin(emb).astype(np.float32).T
    sinn = sin.copy()
    sinn[0:16] = -sinn[0:16]
    return np.ascontiguousarray(cos), np.ascontiguousarray(sinn)


def _consts_common():
    c = {}
    c["c_ident"] = np.eye(128, dtype=np.float32)
    k = np.arange(128)[:, None]
    q = np.arange(SLAB)[None, :]
    cm = [(q >= k + o).astype(np.float32) for o in (0, 128, 256, 384)]
    c["c_cmask01"] = np.concatenate(cm, axis=1)
    tt = np.arange(T)
    c["c_eblk"] = (tt[None, :] // 256 == np.arange(8)[:, None]).astype(
        np.float32)
    return c


def _perm_rope_cols(W):
    W = W.reshape(W.shape[0], 16, 32)
    Wp = np.concatenate([W[:, :, 16:32], W[:, :, 0:16]], axis=2)
    return np.ascontiguousarray(Wp.reshape(W.shape[0], 512))


def prep_l0(inputs):
    f = lambda a: np.ascontiguousarray(np.asarray(a, dtype=np.float32))
    d = dict(_consts_common())
    cos1, sinn1 = _rope_tables()
    d["c_cos"] = cos1
    d["c_sinn"] = sinn1
    d["l0_wv"] = f(inputs["L0_attn_W_qkv"][:, 2048:3072])
    d["l0_bv"] = f(inputs["L0_attn_b_qkv"][2048:3072])
    d["l0_wout"] = f(inputs["L0_attn_W_out"])
    d["l0_bout"] = f(inputs["L0_attn_b_out"])
    d["l0_wln1"] = f(inputs["L0_ffn_w_ln1"])
    d["l0_wln2"] = f(inputs["L0_ffn_w_ln2"])
    d["l0_wkvd"] = f(inputs["L0_ffn_W_kvd"])
    d["l0_wqd"] = f(inputs["L0_ffn_W_qd"])
    d["l0_wku"] = f(inputs["L0_ffn_W_ku"])
    d["l0_wqu"] = f(inputs["L0_ffn_W_qu"])
    d["l0_wvu"] = f(inputs["L0_ffn_W_vu"])
    d["l0_wropeq"] = f(inputs["L0_ffn_W_ropeq"])
    d["l0_wropeqp"] = _perm_rope_cols(f(inputs["L0_ffn_W_ropeq"]))
    d["l0_wropek"] = f(inputs["L0_ffn_W_ropek"])
    d["l0_wropekp"] = _perm_rope_cols(f(inputs["L0_ffn_W_ropek"]))
    d["l0_wo"] = f(inputs["L0_ffn_W_o"])
    d["l0_wgs"] = f(inputs["L0_ffn_Wg_s"])
    d["l0_wus"] = f(inputs["L0_ffn_Wu_s"])
    d["l0_wds"] = f(inputs["L0_ffn_Wd_s"])
    d["l0_wgr"] = f(inputs["L0_ffn_Wg_r"]).reshape(7 * 1024, 1024)
    d["l0_wur"] = f(inputs["L0_ffn_Wu_r"]).reshape(7 * 1024, 1024)
    d["l0_wdr"] = f(inputs["L0_ffn_Wd_r"]).reshape(7 * 1024, 1024)
    d["l0_wrouter"] = f(inputs["L0_ffn_W_router"])
    d["l0_rbias"] = f(inputs["L0_ffn_routing_bias"])
    return d


def _want_inputs(nc):
    want = set()
    for a in nc.m.functions[0].allocations:
        if isinstance(a, mybir.MemoryLocationSet) and a.kind == "ExternalInput":
            want.add(a.memorylocations[0].name)
    want.discard("partition_id")
    return want


_BUILD_CACHE = {}


def _get_nc(name, builder):
    if name not in _BUILD_CACHE:
        nc = builder()
        _BUILD_CACHE[name] = (nc, _want_inputs(nc))
    return _BUILD_CACHE[name]


def build_all():
    lb = LB("mk")
    P = lb.P
    x0 = lb.dram_in("x0", (T, D), F32)
    xout = lb.dram_out("xout", (T, D), F32)
    x3spill = lb.dram_tmp("x3spill", (D, T), F32)
    lb.consts()

    lb.push_scope()
    stream = [lb.sb("stream", (128, T), F32) for _ in range(NDT)]
    lb.push_scope()
    cur = emit_l0(lb, x0, stream)
    lb.pop_scope()
    lb.push_scope()
    cur = emit_l1(lb, stream, cur)
    lb.pop_scope()
    lb.push_scope()
    cur = emit_l2(lb, stream, cur)
    lb.pop_scope()
    for i in range(NDT):
        lb.P.dma("sync", x3spill[i * 128:(i + 1) * 128, :], stream[i][:],
                 waits=[cur[i]])
    lb.pop_scope()

    lb.push_scope()
    emit_l3(lb, x3spill, xout)
    lb.pop_scope()
    lb.finalize()
    return lb.nc


def kernel(**inputs):
    NC = 8
    x = np.ascontiguousarray(np.asarray(inputs["L0_attn_x"],
                                        dtype=np.float32))
    nc, want = _get_nc("all", build_all)
    base = {}
    base.update(prep_l0(inputs))
    base.update(prep_l1(inputs))
    base.update(prep_l3(inputs))
    maps = []
    for c in range(NC):
        m = dict(base)
        m.update(prep_l2(inputs, core=c))
        m = {k: v for k, v in m.items() if k in want}
        m["x0"] = np.ascontiguousarray(x[c])
        assert set(m) == want, want - set(m)
        maps.append(m)
    r = run_bass_kernel_spmd(nc, maps, core_ids=list(range(NC)))
    out = np.stack([r.results[c]["xout"] for c in range(NC)], axis=0)
    return np.ascontiguousarray(out.astype(np.float32))



def emit_l1(lb, acc, acc_toks):
    P = lb.P
    xspill = lb.dram_tmp("l1_xspill", (D, T))
    cumpad_d = lb.dram_tmp("l1_cumpad", (16, 2 * T), BF16)
    jm_t, jm_tok_l = lb.load_w("c_jmat", (128, 128), BF16)
    jmat, jm_tok = jm_t[0], jm_tok_l[0]

    acc_toks = list(acc_toks)
    xnb = [lb.sb("xnb", (128, T), BF16) for _ in range(NDT)]

    lb.push_scope()
    ln_tmp = lb.sb("ln_tmp", (128, SLAB), F32)
    ln_prev = [None]

    def ln_wb(i, s0, s1, psA, psB, bt):
        ta = P.vec("tensor_mul", ln_tmp[:], acc[i][:, s0:s1],
                   psA[:, 0:SLAB], waits=[acc_toks[i], bt, ln_prev[0]])
        tb = P.vec("tensor_add", xnb[i][:, s0:s1], ln_tmp[:],
                   psB[:, 0:SLAB], waits=[ta])
        ln_prev[0] = tb
        return tb

    xn_toks = lb.norm_fm(acc, acc_toks, "ln", ln_wb)
    lb.pop_scope()

    bmax_row = lb.sb("bmax_row", (1, 16), F32)
    lb.push_scope()
    rpe_t, rpe_tokl = lb.load_w("l1_rpe", (16, T), F32, rowsplit=16)
    cum0, cum1 = rpe_t[0], lb.sb("cum1", (16, T), F32)
    ct = P.vec("tensor_scalar_mul", cum0[:], cum0[:], 8.0,
               waits=[rpe_tokl[0]])
    src, dst = cum0, cum1
    k = 1
    while k < T:
        t1 = P.vec("tensor_copy", dst[:, 0:k], src[:, 0:k], waits=[ct])
        ct = P.vec("tensor_add", dst[:, k:T], src[:, k:T], src[:, 0:T - k],
                   waits=[t1])
        src, dst = dst, src
        k *= 2
    cum = src
    cumpad = lb.sb("cumpad", (16, 2 * T), BF16)
    t1 = P.vec("memset", cumpad[:], -240000.0)
    t2 = P.vec("tensor_copy", cumpad[:, T - 1:2 * T - 1], cum[:],
               waits=[t1, ct])
    cp_tok = P.dma("sync", cumpad_d, cumpad[:], waits=[t2])
    bmax_col = lb.sb("bmax_col", (16, 1), F32)
    t3 = P.vec("reduce_max", bmax_col[:, :, None], cum[:, None, :],
               axis=AX.X, waits=[ct])
    ps, war, idx = lb.pg.acquire()
    t4 = P.tp(ps[0:1, 0:16], bmax_col[:], lb.identf[0:16, 0:16],
              waits=[t3, lb.tok_identf, war], inc=True)
    t5 = P.vec("tensor_copy", bmax_row[:], ps[0:1, 0:16], waits=[t4])
    lb.pg.set_last_use(idx, t5)
    bmax_tok = t5
    lb.pop_scope()

    cur_toks = list(acc_toks)

    for hg in range(2):
        lb.push_scope()
        KT = [lb.sb("KT", (64, 2 * T), BF16) for _ in range(4)]
        QT = [lb.sb("QT", (64, 2 * T), BF16) for _ in range(4)]
        V = [lb.sb("V", (128, 520), BF16) for _ in range(T // 128)]
        KT_toks = [None] * 4
        QT_toks = [None] * 4
        V_toks = [None] * (T // 128)
        lb.push_scope()
        wq, wq_tok = lb.load_w("l1_wq", (D, D), col0=hg * 512,
                               col1=hg * 512 + 512)
        wk, wk_tok = lb.load_w("l1_wk", (D, D), col0=hg * 512,
                               col1=hg * 512 + 512)
        wv, wv_tok = lb.load_w("l1_wv", (D, D), col0=hg * 512,
                               col1=hg * 512 + 512)

        def evac_kq(dst, dtoks):
            def evac(oc, s0, s1, ps, mm_tok):
                c1 = P.vec("tensor_copy", dst[oc][0:64, s0:s1],
                           ps[0:64, 0:s1 - s0], waits=[mm_tok])
                c2 = P.vec("tensor_copy", dst[oc][0:64, T + s0:T + s1],
                           ps[64:128, 0:s1 - s0], waits=[mm_tok])
                dtoks[oc] = c2
                return c2
            return evac

        lb.proj(wk, wk_tok, xnb, xn_toks, 512, evac_kq(KT, KT_toks))
        lb.proj(wq, wq_tok, xnb, xn_toks, 512, evac_kq(QT, QT_toks))
        for tt in range(T // 128):
            ms = P.vec("memset",
                       V[tt].rearrange("p (h c) -> p h c", c=65)[:, :, 64:65],
                       1.0, waits=[V_toks[tt]])
            ps, war, idx = lb.pg.acquire()
            mmv = None
            for kc in range(NDT):
                mmv = P.mm(ps[0:128, 0:SLAB],
                           xnb[kc][:, tt * 128:(tt + 1) * 128],
                           wv[kc][:], kc == 0, kc == NDT - 1,
                           waits=[xn_toks[kc], wv_tok[kc],
                                  war if kc == 0 else None],
                           inc=(kc == NDT - 1))
            dview = V[tt].rearrange("p (h c) -> p h c", c=65)[:, 0:8, 0:64]
            sview = ps[0:128, 0:SLAB].rearrange("p (h c) -> p h c", c=64)
            cv = P.vec("tensor_copy", dview, sview, waits=[mmv, ms])
            lb.pg.set_last_use(idx, cv)
            V_toks[tt] = cv
        lb.pop_scope()

        wo4, wo4_tok = lb.load_w("l1_wout", (D, D), row0=hg * 512,
                                 row1=hg * 512 + 512)
        negU = lb.sb("negU", (128, 8), F32)
        negU_toks = [None] * 8
        lb.push_scope()
        sqt = lb.sb("sqt", (64, 2 * T), BF16)
        m12 = [lb.sb("mkq", (1, 1), F32) for _ in range(2)]
        mtmp = lb.sb("mtmp", (1, 1), F32)
        uval = lb.sb("uval", (1, 1), F32)
        sq_war = [None]
        for h in range(8):
            g4, off = h // 2, (h % 2) * T
            toks_nrm = []
            for j, (MAT, MT) in enumerate(((KT, KT_toks), (QT, QT_toks))):
                sq_tok = P.act(sqt[0:64, 0:T], MAT[g4][0:64, off:off + T],
                               AF.Square,
                               waits=[MT[g4], sq_war[0]] + toks_nrm)
                mj = P.vec("memset", m12[j][:], 0.0, waits=toks_nrm)
                ps, war, idx = lb.pg.acquire()
                for s in range(NSLAB):
                    mmn = P.mm(ps[0:1, 0:SLAB], lb.onescol_b[0:64, :],
                               sqt[0:64, s * SLAB:(s + 1) * SLAB],
                               True, True,
                               waits=[sq_tok, lb.tok_onescol_b,
                                      war if s == 0 else None,
                                      toks_nrm[-1] if toks_nrm else None],
                               inc=True)
                    c1 = P.vec("reduce_max", mtmp[:, :, None],
                               ps[0:1, None, 0:SLAB], axis=AX.X,
                               waits=[mmn, mj] + toks_nrm[-1:])
                    c2 = P.vec("tensor_max", m12[j][:], m12[j][:], mtmp[:],
                               waits=[c1])
                    toks_nrm.append(c2)
                lb.pg.set_last_use(idx, toks_nrm[-1])
                sq_war[0] = toks_nrm[-1]
            t2 = P.vec("tensor_mul", m12[0][:], m12[0][:], m12[1][:],
                       waits=toks_nrm)
            t3 = P.act(uval[:], m12[0][:], AF.Ln, waits=[t2])
            t4 = P.act(uval[:], uval[:], AF.Exp, scale=0.5, waits=[t3])
            t4b = P.vec("tensor_add", uval[:], uval[:],
                        bmax_row[0:1, hg * 8 + h:hg * 8 + h + 1],
                        waits=[t4, bmax_tok])
            t5 = P.vec("tensor_scalar_mul", uval[:], uval[:], -0.125,
                       waits=[t4b])
            ps, war, idx = lb.pg.acquire()
            t6 = P.mm(ps[0:128, 0:1], lb.onesrow_f[:], uval[:], True, True,
                      waits=[t5, lb.tok_onesf, war], inc=True)
            t7 = P.vec("tensor_copy", negU[:, h:h + 1], ps[0:128, 0:1],
                       waits=[t6])
            lb.pg.set_last_use(idx, t7)
            negU_toks[h] = t7
        lb.pop_scope()

        P.barrier()
        poolS = PsumPool(None, None, "sS", 0,
                         tiles=lb.pg.tiles[0:3] + lb.pg.tiles[6:7])
        poolPV = PsumPool(None, None, "sPV", 0, tiles=lb.pg.tiles[3:5])
        poolN = PsumPool(None, None, "sN", 0, tiles=lb.pg.tiles[5:6])
        NPB = 4
        Pb = [lb.sb("Pb", (128, SLAB), BF16) for _ in range(NPB)]
        pb_war = [None] * NPB
        Hb = [lb.sb("Hb", (128, SLAB), BF16) for _ in range(4)]
        hb_war = [None] * 4
        hb_i = [0]
        yn4 = [lb.sb("yn4", (128, SLAB), BF16) for _ in range(4)]
        yn_toks = [None] * 8
        yn_war = [None] * 8
        bcn = [lb.sb("bcn", (64, SLAB), F32) for _ in range(2)]
        bcn_war = [None, None]
        rrow = [lb.sb("rrow", (1, SLAB), F32) for _ in range(2)]
        rrow_war = [None, None]
        for qs in range(NSLAB):
            q0, q1 = qs * SLAB, (qs + 1) * SLAB
            for h in range(8):
                g4, off = h // 2, (h % 2) * T
                gh = hg * 8 + h
                pspv, warpv, ipv = poolPV.acquire()
                nkc = 4 * (qs + 1)
                pvmm = None
                pend = None
                for kc in range(nkc):
                    hbuf = Hb[hb_i[0] % 4]
                    base = q0 - (kc * 128) - 127 + T - 1
                    hsrc = bass.AP(cumpad_d.tensor, gh * 2 * T + base,
                                   [[1, 128], [1, SLAB]])
                    hb_tok = P.dma("sync" if kc % 2 == 0 else "gpsimd",
                                   hbuf[:], hsrc,
                                   waits=[cp_tok, hb_war[hb_i[0] % 4]])
                    psS, warS, iS = poolS.acquire()
                    smm = P.mm(psS[0:128, 0:SLAB],
                               KT[g4][0:64, off + kc * 128:
                                      off + (kc + 1) * 128],
                               QT[g4][0:64, off + q0:off + q1], True, False,
                               waits=[KT_toks[g4], QT_toks[g4], warS])
                    smm = P.mm(psS[0:128, 0:SLAB], jmat[:], hbuf[:],
                               False, True,
                               waits=[jm_tok, hb_tok], inc=True)
                    pb = Pb[kc % NPB]
                    e = P.act(pb[:], psS[0:128, 0:SLAB], AF.Exp,
                              scale=0.125, bias=negU[:, h:h + 1],
                              waits=[smm, negU_toks[h], pb_war[kc % NPB]])
                    poolS.set_last_use(iS, e)
                    hb_war[hb_i[0] % 4] = e
                    hb_i[0] += 1
                    if pend is not None:
                        pkc, pptok = pend
                        pvmm = P.mm(pspv[0:65, 0:SLAB],
                                    V[pkc][:, h * 65:(h + 1) * 65],
                                    Pb[pkc % NPB][:], pkc == 0, False,
                                    waits=[V_toks[pkc], pptok,
                                           warpv if pkc == 0 else None],
                                    inc=True)
                        pb_war[pkc % NPB] = pvmm
                    pend = (kc, e)
                pkc, pptok = pend
                pvmm = P.mm(pspv[0:65, 0:SLAB],
                            V[pkc][:, h * 65:(h + 1) * 65],
                            Pb[pkc % NPB][:], pkc == 0, True,
                            waits=[V_toks[pkc], pptok,
                                   warpv if pkc == 0 else None],
                            inc=True)
                pb_war[pkc % NPB] = pvmm
                rr = rrow[h % 2]
                tl = P.act(rr[:], pspv[64:65, 0:SLAB], AF.Ln,
                           waits=[pvmm, rrow_war[h % 2]])
                t1 = P.act(rr[:], rr[:], AF.Exp, scale=-1.0, waits=[tl])
                psn, warn, inn = poolN.acquire()
                t2 = P.mm(psn[0:64, 0:SLAB], lb.onesrow_f[0:1, 0:64], rr[:],
                          True, True, waits=[t1, warn, lb.tok_onesf],
                          inc=True)
                rrow_war[h % 2] = t2
                bc = bcn[h % 2]
                t3 = P.vec("tensor_copy", bc[:], psn[0:64, 0:SLAB],
                           waits=[t2, bcn_war[h % 2]])
                poolN.set_last_use(inn, t3)
                hb2 = (h % 2) * 64
                t4 = P.vec("tensor_mul", yn4[g4][hb2:hb2 + 64, :],
                           pspv[0:64, 0:SLAB], bc[:],
                           waits=[t3, yn_war[h]])
                poolPV.set_last_use(ipv, t4)
                bcn_war[h % 2] = t4
                yn_toks[h] = t4
            last_wo = None
            for oc in range(NDT):
                psO, warO, iO = poolS.acquire()
                mmo = None
                for g4 in range(4):
                    mmo = P.mm(psO[0:128, 0:SLAB],
                               wo4[g4][:, oc * 128:(oc + 1) * 128],
                               yn4[g4][:], g4 == 0, g4 == 3,
                               waits=[wo4_tok[g4], yn_toks[2 * g4],
                                      yn_toks[2 * g4 + 1],
                                      warO if g4 == 0 else None],
                               inc=(g4 == 3))
                at = P.vec("tensor_add", acc[oc][:, q0:q1],
                           acc[oc][:, q0:q1], psO[0:128, 0:SLAB],
                           waits=[mmo, cur_toks[oc]])
                poolS.set_last_use(iO, at)
                cur_toks[oc] = at
                last_wo = at
            for h in range(8):
                yn_war[h] = last_wo
        P.barrier()
        lb.pop_scope()

    spill_toks = []
    for i in range(NDT):
        spill_toks.append(P.dma("sync", xspill[i * 128:(i + 1) * 128, :],
                                acc[i][:], waits=[cur_toks[i]]))

    lb.push_scope()
    ysb = [lb.sb("ysb", (128, T), F32) for _ in range(NDT)]
    ysb_toks = [None] * NDT
    lb.push_scope()
    wfc_ap = lb.dram_in("l1_wfcT", (D, 4096))
    wpr_ap = lb.dram_in("l1_wprojT", (4096, D))
    wfcb = [lb.sb("wfcb", (128, SLAB), BF16) for _ in range(NDT)]
    wprb = [lb.sb("wprb", (128, D), BF16) for _ in range(4)]
    hrelu = lb.sb("hrelu", (128, SLAB), F32)
    hfc = [lb.sb("hfc", (128, T), BF16) for _ in range(4)]
    hl_tok = None
    first_y = [[True] * NSLAB for _ in range(NDT)]

    def issue_wfc(fg, wait):
        return [P.dma(
            "gpsimd", wfcb[kc][:],
            wfc_ap[kc * 128:(kc + 1) * 128,
                   fg * SLAB:(fg + 1) * SLAB], waits=wait)
            for kc in range(NDT)]

    def issue_wpr(fg, wait):
        return [P.dma("gpsimd", wprb[fl][:],
                      wpr_ap[fg * SLAB + fl * 128:fg * SLAB + fl * 128 + 128,
                             :], waits=wait) for fl in range(4)]

    wfc_next = issue_wfc(0, [])
    wpr_next = issue_wpr(0, [])
    for fg in range(8):
        wfc_toks = wfc_next
        wpr_toks = wpr_next
        hfc_toks = []
        for fl in range(4):
            for s in range(NSLAB):
                s0, s1 = s * SLAB, (s + 1) * SLAB
                ps, war, idx = lb.pg.acquire()
                mmh = None
                for kc in range(NDT):
                    mmh = P.mm(ps[0:128, 0:SLAB],
                               wfcb[kc][:, fl * 128:(fl + 1) * 128],
                               xnb[kc][:, s0:s1], kc == 0, kc == NDT - 1,
                               waits=[wfc_toks[kc], xn_toks[kc],
                                      war if kc == 0 else None],
                               inc=(kc == NDT - 1))
                rt = P.act(hrelu[:], ps[0:128, 0:SLAB], AF.Relu,
                           waits=[mmh, hl_tok])
                st = P.act(hfc[fl][:, s0:s1], hrelu[:], AF.Square,
                           waits=[rt])
                lb.pg.set_last_use(idx, st)
                hl_tok = st
            hfc_toks.append(hl_tok)
        if fg < 7:
            wfc_next = issue_wfc(fg + 1, [mmh])
        for oc in range(NDT):
            for s in range(NSLAB):
                s0, s1 = s * SLAB, (s + 1) * SLAB
                ps, war, idx = lb.pg.acquire()
                mmy = None
                for fl in range(4):
                    mmy = P.mm(ps[0:128, 0:SLAB],
                               wprb[fl][:, oc * 128:(oc + 1) * 128],
                               hfc[fl][:, s0:s1], fl == 0, fl == 3,
                               waits=[wpr_toks[fl], hfc_toks[fl],
                                      war if fl == 0 else None],
                               inc=(fl == 3))
                if first_y[oc][s]:
                    yt = P.vec("tensor_copy", ysb[oc][:, s0:s1],
                               ps[0:128, 0:SLAB], waits=[mmy, ysb_toks[oc]])
                    first_y[oc][s] = False
                else:
                    yt = P.vec("tensor_add", ysb[oc][:, s0:s1],
                               ysb[oc][:, s0:s1], ps[0:128, 0:SLAB],
                               waits=[mmy, ysb_toks[oc]])
                lb.pg.set_last_use(idx, yt)
                ysb_toks[oc] = yt
        if fg < 7:
            wpr_next = issue_wpr(fg + 1, [P.now("tensor")])
    lb.pop_scope()
    mem = [lb.sb("mem", (128, SLAB), F32) for _ in range(2)]
    spk = [lb.sb("spk", (128, SLAB), F32) for _ in range(2)]
    ssum = [lb.sb("ssum", (128, SLAB), F32) for _ in range(2)]
    arel = [lb.sb("arel", (128, SLAB), F32) for _ in range(2)]
    arel_tok = [None, None]
    prev_out = [None, None]
    bi = 0
    for oc in range(NDT):
        for s in range(NSLAB):
            s0, s1 = s * SLAB, (s + 1) * SLAB
            par = bi % 2
            EV = P.vec
            me, sp, su = mem[par], spk[par], ssum[par]
            ab = arel[par]
            ld = P.dma("sync", ab[:], xspill[oc * 128:(oc + 1) * 128, s0:s1],
                       waits=[spill_toks[oc], arel_tok[par]])
            y = ysb[oc]
            t1 = EV("tensor_single_scalar", su[:], y[:, s0:s1], 1.0,
                    op=ALU.is_gt, waits=[ysb_toks[oc], prev_out[par]])
            t2 = EV("tensor_copy", sp[:], su[:], waits=[t1])
            t3 = EV("tensor_copy", me[:], y[:, s0:s1],
                    waits=[ysb_toks[oc], prev_out[par]])
            for step in range(4):
                t4 = EV("scalar_tensor_tensor", me[:], me[:], 0.9,
                        y[:, s0:s1], op0=ALU.mult, op1=ALU.add,
                        waits=[t3, t2])
                t5 = EV("tensor_sub", me[:], me[:], sp[:], waits=[t4])
                t6 = EV("tensor_single_scalar", sp[:], me[:], 1.0,
                        op=ALU.is_gt, waits=[t5])
                t7 = EV("tensor_add", su[:], su[:], sp[:],
                        waits=[t6])
                t3 = t7
                t2 = t7
            t8 = EV("scalar_tensor_tensor", acc[oc][:, s0:s1], su[:],
                    0.2, ab[:], op0=ALU.mult, op1=ALU.add,
                    waits=[t3, ld, spill_toks[oc], cur_toks[oc]])
            arel_tok[par] = t8
            prev_out[par] = t8
            cur_toks[oc] = t8
            bi += 1
    lb.pop_scope()
    return cur_toks


def prep_l1(inputs):
    f = lambda a: np.ascontiguousarray(np.asarray(a, dtype=np.float32))
    d = {}
    d["c_ident"] = np.eye(128, dtype=np.float32)
    d["c_jmat"] = np.eye(128, dtype=np.float32)[::-1].copy()
    qkv = f(inputs["L1_attn_qkv"])
    d["l1_wq"] = f(qkv[:, 0].reshape(D, D))
    d["l1_wk"] = f(qkv[:, 1].reshape(D, D))
    d["l1_wv"] = f(qkv[:, 2].reshape(D, D))
    d["l1_wout"] = f(inputs["L1_attn_out_w"]).reshape(D, D)
    d["l1_rpe"] = f(inputs["L1_attn_rpe_bias"])[:, :T]
    d["l1_wfcT"] = f(np.asarray(inputs["L1_ffn_W_fc"]).T)
    d["l1_wprojT"] = f(np.asarray(inputs["L1_ffn_W_proj"]).T)
    return d



def emit_l2(lb, acc, acc_toks):
    P = lb.P
    rmT_d = lb.dram_tmp("l2_rmT", (D, T), BF16)
    acc_toks = list(acc_toks)

    xnb = [lb.sb("xnb_l2", (128, T), BF16) for _ in range(NDT)]

    lb.push_scope()
    ln_tmp = lb.sb("ln_tmp", (128, SLAB), F32)
    ln_prev = [None]
    xn_toks = [None] * NDT

    def ln_wb(i, s0, s1, psA, psB, bt):
        ta = P.vec("tensor_mul", ln_tmp[:], acc[i][:, s0:s1],
                   psA[:, 0:SLAB], waits=[acc_toks[i], bt, ln_prev[0]])
        tb = P.vec("tensor_add", xnb[i][:, s0:s1], ln_tmp[:],
                   psB[:, 0:SLAB], waits=[ta])
        ln_prev[0] = tb
        return tb

    xn_toks = lb.norm_fm(acc, acc_toks, "ln", ln_wb)
    lb.pop_scope()

    lb.push_scope()
    lb.P.nodummy = True
    triu_t, triu_tokl = lb.load_w("c_triu", (128, 128), BF16)
    triu, triu_tok = triu_t[0], triu_tokl[0]
    ew1 = lb.sb("ew1", (1, D), F32)
    e127 = lb.sb("e127", (1, D), F32)
    e128 = lb.sb("e128", (1, D), F32)
    EWneg = lb.sb("EWneg", (128, D), BF16)
    EWu = lb.sb("EWu", (128, D), BF16)
    lb.push_scope()
    iota_r, iota_tok = lb.load_row("c_iota128", 128)
    iom1_r, iom1_tok = lb.load_row("c_iotam1", 128)
    td_r, td_tok = lb.load_row("l2_td", D)
    tf_r, tf_tok = lb.load_row("l2_tf", D)
    wrow = lb.sb("wrow_l2", (1, D), F32)
    t1 = P.act(wrow[:], td_r[:], AF.Exp, waits=[td_tok])
    t1 = P.vec("tensor_scalar_mul", wrow[:], wrow[:], -1.0, waits=[t1])
    t1b = P.act(ew1[:], wrow[:], AF.Exp, waits=[t1])
    t2 = P.vec("tensor_scalar_mul", e127[:], wrow[:], 127.0, waits=[t1])
    t2 = P.act(e127[:], e127[:], AF.Exp, waits=[t2])
    t3 = P.vec("tensor_scalar_mul", e128[:], wrow[:], 128.0, waits=[t1])
    t3 = P.act(e128[:], e128[:], AF.Exp, waits=[t3])
    ew_toks = []
    for half in range(2):
        c0, c1_ = half * 512, (half + 1) * 512
        ps, war, idx = lb.pg.acquire()
        mm1 = P.mm(ps[0:128, 0:SLAB], iota_r[0:1, :], wrow[0:1, c0:c1_],
                   True, True, waits=[iota_tok, t1, war], inc=True)
        a1 = P.act(EWneg[:, c0:c1_], ps[0:128, 0:SLAB], AF.Exp, scale=-1.0,
                   waits=[mm1])
        ps2, war2, idx2 = lb.pg.acquire()
        P.mm(ps2[0:128, 0:SLAB], iom1_r[0:1, :], wrow[0:1, c0:c1_],
             True, False, waits=[iom1_tok, t1, war2])
        mm2 = P.mm(ps2[0:128, 0:SLAB], lb.onesrow_f[:], tf_r[0:1, c0:c1_],
                   False, True, waits=[tf_tok, lb.tok_onesf], inc=True)
        a2 = P.act(EWu[:, c0:c1_], ps2[0:128, 0:SLAB], AF.Exp, waits=[mm2])
        lb.pg.set_last_use(idx, a1)
        lb.pg.set_last_use(idx2, a2)
        ew_toks += [a1, a2]
    lb.pop_scope()
    aA = lb.sb("aA", (1, D), F32)
    aB = lb.sb("aB", (1, D), F32)
    cwA = lb.sb("cwA", (1, D), BF16)
    cwB = lb.sb("cwB", (1, D), BF16)
    sumA = lb.sb("sumA", (1, D), F32)
    sumB = lb.sb("sumB", (1, D), F32)
    cA_tok = P.vec("memset", aA[:], 0.0)
    cB_tok = P.vec("memset", aB[:], 0.0)
    zA = lb.sb("zA", (128, D), BF16)
    zB = lb.sb("zB", (128, D), BF16)
    numb = lb.sb("numb", (128, SLAB), F32)
    denb = lb.sb("denb", (128, SLAB), F32)
    rm_c = lb.sb("rm_c", (128, D), BF16)
    rm_sp = [lb.sb("rm_sp", (128, 128), BF16) for _ in range(2)]
    z_war = [None, None]
    prev_nd = None
    rm_war = [None]
    st_war = [None, None]
    CHP = 4

    def phase_open(ch0):
        WPH = CHP * 128
        t0_, t1_ = ch0 * 128, ch0 * 128 + WPH
        lb.push_scope()
        ekL = [lb.sb("ek", (128, D), BF16) for _ in range(CHP)]
        ekvL = [lb.sb("ekv", (128, D), BF16) for _ in range(CHP)]
        ekL_t = [None] * CHP
        ekvL_t = [None] * CHP

        def mix_and_proj(tm_name, w_name, out_tiles, out_toks, act):
            lb.push_scope()
            tm_col, tm_tok = lb.load_col(tm_name, D)
            wT, wT_tok = lb.load_w(w_name, (D, D))
            xm = [lb.sb("xm", (128, WPH), BF16) for _ in range(NDT)]
            xm_toks = []
            for i in range(NDT):
                if t0_ == 0:
                    d1 = P.vec("tensor_sub", xm[i][:, 1:WPH],
                               xnb[i][:, 1:WPH], xnb[i][:, 0:WPH - 1],
                               waits=[xn_toks[i]])
                    m1 = P.vec("scalar_tensor_tensor", xm[i][:, 1:WPH],
                               xm[i][:, 1:WPH], tm_col[i][:],
                               xnb[i][:, 0:WPH - 1],
                               op0=ALU.mult, op1=ALU.add,
                               waits=[d1, tm_tok[i]])
                    m2 = P.vec("tensor_scalar", xm[i][:, 0:1],
                               xnb[i][:, 0:1], tm_col[i][:], None,
                               op0=ALU.mult, waits=[m1])
                else:
                    d1 = P.vec("tensor_sub", xm[i][:],
                               xnb[i][:, t0_:t1_],
                               xnb[i][:, t0_ - 1:t1_ - 1],
                               waits=[xn_toks[i]])
                    m2 = P.vec("scalar_tensor_tensor", xm[i][:],
                               xm[i][:], tm_col[i][:],
                               xnb[i][:, t0_ - 1:t1_ - 1],
                               op0=ALU.mult, op1=ALU.add,
                               waits=[d1, tm_tok[i]])
                xm_toks.append(m2)
            for lc in range(CHP):
                for half in range(2):
                    ps, war, idx = lb.pg.acquire()
                    mmv = None
                    for kc in range(NDT):
                        mmv = P.mm(ps[0:128, 0:SLAB],
                                   xm[kc][:, lc * 128:(lc + 1) * 128],
                                   wT[kc][:, half * 512:(half + 1) * 512],
                                   kc == 0, kc == NDT - 1,
                                   waits=[xm_toks[kc], wT_tok[kc],
                                          war if kc == 0 else None],
                                   inc=(kc == NDT - 1))
                    dst = out_tiles[lc][:, half * 512:(half + 1) * 512]
                    if act == "exp":
                        ev = P.act(dst, ps[0:128, 0:SLAB], AF.Exp,
                                   waits=[mmv, out_toks[lc]])
                    else:
                        ev = P.vec("tensor_copy", dst, ps[0:128, 0:SLAB],
                                   waits=[mmv, out_toks[lc]])
                    lb.pg.set_last_use(idx, ev)
                    out_toks[lc] = ev
            lb.pop_scope()

        mix_and_proj("l2_tmk", "l2_wkT", ekL, ekL_t, "exp")
        mix_and_proj("l2_tmv", "l2_wvT", ekvL, ekvL_t, "copy")
        for lc in range(CHP):
            ekvL_t[lc] = P.vec("tensor_mul", ekvL[lc][:], ekL[lc][:],
                               ekvL[lc][:], waits=[ekL_t[lc], ekvL_t[lc]])
        P.barrier()
        for k in range(len(lb.pg.tiles)):
            lb.pg.last_use[k] = None
        pT = PsumPool(None, None, "wT", 0, tiles=lb.pg.tiles[0:4])
        pR = PsumPool(None, None, "wR", 0, tiles=lb.pg.tiles[4:6])
        return ({ch0 + lc: ekL[lc] for lc in range(CHP)},
                {ch0 + lc: ekvL[lc] for lc in range(CHP)},
                {ch0 + lc: ekL_t[lc] for lc in range(CHP)},
                {ch0 + lc: ekvL_t[lc] for lc in range(CHP)}, pT, pR)

    for ch in range(T // 128):
        if ch % CHP == 0:
            ek, ekv, ek_toks, ekv_toks, poolT, poolR = phase_open(ch)
        cw1 = P.vec("scalar_tensor_tensor", cwA[:], aA[:], 1.0, ew1[:],
                    op0=ALU.mult, op1=ALU.mult, waits=[cA_tok, t1b])
        cw2 = P.vec("scalar_tensor_tensor", cwB[:], aB[:], 1.0, ew1[:],
                    op0=ALU.mult, op1=ALU.mult, waits=[cB_tok, t1b])
        zt1 = P.vec("tensor_mul", zA[:], ekv[ch][:], EWneg[:],
                    waits=[ekv_toks[ch], z_war[0]] + ew_toks)
        zt2 = P.vec("tensor_mul", zB[:], ek[ch][:], EWneg[:],
                    waits=[ek_toks[ch], z_war[1]] + ew_toks)
        sum_toks = []
        nd_last = None
        for half in range(2):
            c0, c1_ = half * 512, (half + 1) * 512
            psA_, warA, idxA = poolT.acquire()
            P.mm(psA_[0:128, 0:SLAB], triu[:], zA[:, c0:c1_], True, False,
                 waits=[triu_tok, zt1, warA])
            mmA = P.mm(psA_[0:128, 0:SLAB], lb.onesrow_b[:],
                       cwA[0:1, c0:c1_], False, True,
                       waits=[cw1, lb.tok_ones], inc=True)
            psB_, warB, idxB = poolT.acquire()
            P.mm(psB_[0:128, 0:SLAB], triu[:], zB[:, c0:c1_], True, False,
                 waits=[triu_tok, zt2, warB])
            mmB = P.mm(psB_[0:128, 0:SLAB], lb.onesrow_b[:],
                       cwB[0:1, c0:c1_], False, True,
                       waits=[cw2, lb.tok_ones], inc=True)
            n1 = P.vec("tensor_mul", numb[:], psA_[0:128, 0:SLAB],
                       EWu[:, c0:c1_], waits=[mmA, prev_nd])
            n2 = P.vec("tensor_add", numb[:], numb[:], ekv[ch][:, c0:c1_],
                       waits=[n1])
            d1 = P.vec("tensor_mul", denb[:], psB_[0:128, 0:SLAB],
                       EWu[:, c0:c1_], waits=[mmB, n2])
            d2 = P.vec("tensor_add", denb[:], denb[:], ek[ch][:, c0:c1_],
                       waits=[d1])
            d3 = P.vec("reciprocal", denb[:], denb[:], waits=[d2])
            y1 = P.vec("tensor_mul", numb[:], numb[:], denb[:], waits=[d3])
            y2 = P.vec("tensor_copy", rm_c[:, c0:c1_], numb[:],
                       waits=[y1, rm_war[0]])
            prev_nd = y2
            nd_last = y2
            poolT.set_last_use(idxA, n1)
            poolT.set_last_use(idxB, d1)
            psS_, warS, idxS = poolR.acquire()
            sm1 = P.mm(psS_[0:1, 0:SLAB], lb.onescol_b[:], zA[:, c0:c1_],
                       True, True, waits=[zt1, lb.tok_onescol_b, warS],
                       inc=True)
            sv1 = P.vec("tensor_copy", sumA[0:1, c0:c1_], psS_[0:1, 0:SLAB],
                        waits=[sm1])
            psS2_, warS2, idxS2 = poolR.acquire()
            sm2 = P.mm(psS2_[0:1, 0:SLAB], lb.onescol_b[:], zB[:, c0:c1_],
                       True, True, waits=[zt2, lb.tok_onescol_b, warS2],
                       inc=True)
            sv2 = P.vec("tensor_copy", sumB[0:1, c0:c1_], psS2_[0:1, 0:SLAB],
                        waits=[sm2])
            poolR.set_last_use(idxS, sv1)
            poolR.set_last_use(idxS2, sv2)
            sum_toks += [sv1, sv2]
        z_war[0] = sum_toks[-2]
        z_war[1] = sum_toks[-1]
        tp_last = None
        for i in range(NDT):
            pst, wart, idxt = lb.psbf.acquire()
            mt = P.tp(pst[0:128, 0:128], rm_c[:, i * 128:(i + 1) * 128],
                      lb.identb[:],
                      waits=[nd_last, lb.tok_identb, wart], inc=True)
            bi2 = (ch * NDT + i) % 2
            sbf = rm_sp[bi2]
            cv = P.vec("tensor_copy", sbf[:], pst[0:128, 0:128],
                       waits=[mt, st_war[bi2]])
            lb.psbf.set_last_use(idxt, cv)
            st = P.dma("sync",
                       bass.AP(rmT_d.tensor, (i * 128) * T + ch * 128,
                               [[T, 128], [1, 128]]),
                       sbf[:], waits=[cv])
            st_war[bi2] = st
            tp_last = cv
            lb.out_dma_toks.append(st)
        rm_war[0] = tp_last
        u1 = P.vec("tensor_mul", sumA[:], sumA[:], e127[:],
                   waits=sum_toks + [t2])
        u2 = P.vec("scalar_tensor_tensor", aA[:], aA[:], 1.0, e128[:],
                   op0=ALU.mult, op1=ALU.mult, waits=[t3, cw1, u1])
        cA_tok = P.vec("tensor_add", aA[:], aA[:], sumA[:], waits=[u2])
        u3 = P.vec("tensor_mul", sumB[:], sumB[:], e127[:],
                   waits=sum_toks + [t2])
        u4 = P.vec("scalar_tensor_tensor", aB[:], aB[:], 1.0, e128[:],
                   op0=ALU.mult, op1=ALU.mult, waits=[cw2, u3])
        cB_tok = P.vec("tensor_add", aB[:], aB[:], sumB[:], waits=[u4])
        if ch % CHP == CHP - 1:
            lb.pop_scope()
    lb.pop_scope()
    lb.P.nodummy = False

    lb.push_scope()
    rmT = [lb.sb("rmTs", (128, T), BF16) for _ in range(NDT)]
    rmT_toks = lb.load_x_fm(rmT_d, rmT)
    lb.push_scope()
    tmr_col, tmr_tok = lb.load_col("l2_tmr", D)
    wrT, wrT_tok = lb.load_w("l2_wrT", (D, D))
    xm_r = [lb.sb("xmr", (128, T), BF16) for _ in range(NDT)]
    dbuf_r = lb.sb("dmixr", (128, T), F32)
    xm_toks = []
    dprev = None
    for i in range(NDT):
        d1 = P.vec("tensor_sub", dbuf_r[:, 1:T], xnb[i][:, 1:T],
                   xnb[i][:, 0:T - 1], waits=[xn_toks[i], dprev])
        m1 = P.vec("scalar_tensor_tensor", xm_r[i][:, 1:T], dbuf_r[:, 1:T],
                   tmr_col[i][:], xnb[i][:, 0:T - 1],
                   op0=ALU.mult, op1=ALU.add, waits=[d1, tmr_tok[i]])
        m2 = P.vec("tensor_scalar", xm_r[i][:, 0:1], xnb[i][:, 0:1],
                   tmr_col[i][:], None, op0=ALU.mult, waits=[m1])
        xm_toks.append(m2)
        dprev = m1
    sgtmp = lb.sb("sgtmp", (128, SLAB), BF16)
    sg_war = [None]

    def evac_gate(oc, s0, s1, ps, mm_tok):
        sg = P.act(sgtmp[:], ps[0:128, 0:s1 - s0], AF.Sigmoid,
                   waits=[mm_tok, sg_war[0]])
        tok = P.vec("tensor_mul", rmT[oc][:, s0:s1], rmT[oc][:, s0:s1],
                    sgtmp[:], waits=[sg, rmT_toks[oc]])
        sg_war[0] = tok
        rmT_toks[oc] = tok
        return tok

    lb.proj(wrT, wrT_tok, xm_r, xm_toks, D, evac_gate)
    lb.pop_scope()

    lb.push_scope()
    woT, woT_tok = lb.load_w("l2_woT", (D, D))

    def evac_out(oc, s0, s1, ps, mm_tok):
        tok = P.vec("tensor_add", acc[oc][:, s0:s1], acc[oc][:, s0:s1],
                    ps[0:128, 0:s1 - s0], waits=[mm_tok, acc_toks[oc]])
        acc_toks[oc] = tok
        return tok

    lb.proj(woT, woT_tok, rmT, rmT_toks, D, evac_out)
    lb.pop_scope()
    lb.pop_scope()

    emit_l2_moe(lb, xnb, xn_toks, acc, acc_toks)
    return acc_toks


def emit_l2_moe(lb, xnb, xn_toks, acc, acc_toks):
    P = lb.P
    E = 8
    lb.push_scope()
    ridx_ap = lb.dram_in("l2_ridx", (T, 2), I32)
    rw_ap = lb.dram_in("l2_rw", (T, E))
    mwrow = [lb.sb("mwrow", (1, T), BF16) for _ in range(E)]
    mwrow_toks = [None] * E
    lb.push_scope()
    idx_i = lb.sb("idx_i", (128, 2), I32)
    idx_f = lb.sb("idx_f", (128, 2), F32)
    rw_t = lb.sb("rw_t", (128, E), F32)
    junk2 = lb.sb("junk2", (128, 2), F32)
    cnt = lb.sb("cnt", (128, 1), F32)
    mw_t = lb.sb("mw_t", (128, E), F32)
    for tt in range(T // 128):
        t0 = tt * 128
        ld1 = P.dma("sync", idx_i[:], ridx_ap[t0:t0 + 128, :],
                    waits=[P.now("vector")])
        ld2 = P.dma("sync", rw_t[:], rw_ap[t0:t0 + 128, :],
                    waits=[P.now("vector")])
        cv = P.vec("tensor_copy", idx_f[:], idx_i[:], waits=[ld1])
        prev = cv
        for e in range(E):
            t1 = P.vec("tensor_scalar", junk2[:], idx_f[:], float(e), 0.0,
                       op0=ALU.is_equal, op1=ALU.add, accum_out=cnt[:],
                       waits=[prev])
            t2 = P.vec("tensor_mul", mw_t[:, e:e + 1], rw_t[:, e:e + 1],
                       cnt[:], waits=[t1, ld2])
            prev = t2
        for e in range(E):
            ps2, war2, idx2 = lb.pg.acquire()
            mt2 = P.tp(ps2[0:1, 0:128], mw_t[:, e:e + 1], lb.identf[:],
                       waits=[prev, lb.tok_identf, war2], inc=True)
            cv2 = P.vec("tensor_copy", mwrow[e][0:1, t0:t0 + 128],
                        ps2[0:1, 0:128], waits=[mt2, mwrow_toks[e]])
            lb.pg.set_last_use(idx2, cv2)
            mwrow_toks[e] = cv2
            prev = cv2
    lb.pop_scope()

    gate_ap = lb.dram_in("l2_gate", (E * D, D))
    up_ap = lb.dram_in("l2_up", (E * D, D))
    down_ap = lb.dram_in("l2_down", (E * D, D))
    gb_ap = lb.dram_in("l2_gb", (E, D))
    ub_ap = lb.dram_in("l2_ub", (E, D))
    db_ap = lb.dram_in("l2_db", (E, D))
    gbuf = [lb.sb("gw2", (128, D), BF16) for _ in range(NDT)]
    ubuf = [lb.sb("uw2", (128, D), BF16) for _ in range(NDT)]
    dbuf = [lb.sb("dw2", (128, D), BF16) for _ in range(NDT)]
    gb_c = [lb.sb("gb_c", (128, 1), F32) for _ in range(NDT)]
    ub_c = [lb.sb("ub_c", (128, 1), F32) for _ in range(NDT)]
    db_r = lb.sb("db_r", (1, D), BF16)
    hmid = [lb.sb("hmid2", (128, SLAB), BF16) for _ in range(NDT)]
    g1 = [lb.sb("g1", (128, SLAB), F32) for _ in range(2)]
    s1b = [lb.sb("s1b", (128, SLAB), F32) for _ in range(2)]
    u1 = [lb.sb("u1", (128, SLAB), F32) for _ in range(2)]
    P.barrier()
    poolGU = PsumPool(None, None, "gGU", 0, tiles=lb.pg.tiles[0:4])
    poolW = PsumPool(None, None, "gW", 0, tiles=lb.pg.tiles[4:5])
    poolD = PsumPool(None, None, "gD", 0, tiles=lb.pg.tiles[5:7])

    def issue_gub(e, wait):
        gt, ut, bt = [], [], []
        for kc in range(NDT):
            r0 = e * D + kc * 128
            gt.append(P.dma("gpsimd", gbuf[kc][:],
                            gate_ap[r0:r0 + 128, :], waits=wait))
            ut.append(P.dma("gpsimd", ubuf[kc][:],
                            up_ap[r0:r0 + 128, :], waits=wait))
            bt.append(P.dma(
                "sync", gb_c[kc][:],
                bass.AP(gb_ap.tensor, e * D + kc * 128, [[1, 128], [1, 1]]),
                waits=wait))
            bt.append(P.dma(
                "sync", ub_c[kc][:],
                bass.AP(ub_ap.tensor, e * D + kc * 128, [[1, 128], [1, 1]]),
                waits=wait))
        return gt, ut, bt

    def issue_dn(e, wait):
        dt = [P.dma("gpsimd", dbuf[kc][:],
                    down_ap[e * D + kc * 128:e * D + (kc + 1) * 128, :],
                    waits=wait) for kc in range(NDT)]
        return dt, P.dma("gpsimd", db_r[:], db_ap[e:e + 1, :], waits=wait)

    gub_next = issue_gub(0, [])
    dn_next = issue_dn(0, [])
    chain_war = [None, None]
    for e in range(E):
        g_toks, u_toks, b_toks = gub_next
        d_toks, dbr_tok = dn_next
        for s in range(NSLAB):
            s0, s1_ = s * SLAB, (s + 1) * SLAB
            psw, warw, idxw = poolW.acquire()
            bw = P.mm(psw[0:128, 0:SLAB], lb.onesrow_b[:],
                      mwrow[e][0:1, s0:s1_], True, True,
                      waits=[mwrow_toks[e], lb.tok_ones, warw], inc=True)
            hm_toks = []
            for ic in range(NDT):
                psg, warg, idxg = poolGU.acquire()
                mmg = None
                for kc in range(NDT):
                    mmg = P.mm(psg[0:128, 0:SLAB],
                               gbuf[kc][:, ic * 128:(ic + 1) * 128],
                               xnb[kc][:, s0:s1_], kc == 0, kc == NDT - 1,
                               waits=[g_toks[kc], xn_toks[kc],
                                      warg if kc == 0 else None],
                               inc=(kc == NDT - 1))
                psu, waru, idxu = poolGU.acquire()
                mmu = None
                for kc in range(NDT):
                    mmu = P.mm(psu[0:128, 0:SLAB],
                               ubuf[kc][:, ic * 128:(ic + 1) * 128],
                               xnb[kc][:, s0:s1_], kc == 0, kc == NDT - 1,
                               waits=[u_toks[kc], xn_toks[kc],
                                      waru if kc == 0 else None],
                               inc=(kc == NDT - 1))
                pr = ic % 2
                war_ch = chain_war[pr]
                w1 = P.vec("tensor_scalar", g1[pr][:], psg[0:128, 0:SLAB],
                           gb_c[ic][:], 7.0, op0=ALU.add, op1=ALU.min,
                           waits=[mmg, b_toks[2 * ic], war_ch])
                poolGU.set_last_use(idxg, w1)
                w2 = P.act(s1b[pr][:], g1[pr][:], AF.Sigmoid, scale=1.702,
                           waits=[w1, war_ch])
                w3 = P.vec("tensor_mul", g1[pr][:], g1[pr][:], s1b[pr][:],
                           waits=[w2])
                w4 = P.vec("tensor_scalar", u1[pr][:], psu[0:128, 0:SLAB],
                           ub_c[ic][:], -7.0, op0=ALU.add, op1=ALU.max,
                           waits=[mmu, b_toks[2 * ic + 1], war_ch])
                poolGU.set_last_use(idxu, w4)
                w5 = P.vec("tensor_scalar", u1[pr][:], u1[pr][:], 7.0, 1.0,
                           op0=ALU.min, op1=ALU.add, waits=[w4])
                w6 = P.vec("tensor_mul", u1[pr][:], u1[pr][:], g1[pr][:],
                           waits=[w5, w3])
                w7 = P.vec("tensor_mul", hmid[ic][:], u1[pr][:],
                           psw[0:128, 0:SLAB], waits=[w6, bw])
                hm_toks.append(w7)
                chain_war[pr] = w7
            poolW.set_last_use(idxw, hm_toks[-1])
            if s == NSLAB - 1 and e + 1 < E:
                gub_next = issue_gub(e + 1, [mmu, w4])
            for oc in range(NDT):
                psd, ward, idxd = poolD.acquire()
                mmd = None
                for ic in range(NDT):
                    mmd = P.mm(psd[0:128, 0:SLAB],
                               dbuf[ic][:, oc * 128:(oc + 1) * 128],
                               hmid[ic][:], ic == 0, False,
                               waits=[d_toks[ic], hm_toks[ic],
                                      ward if ic == 0 else None])
                mmd = P.mm(psd[0:128, 0:SLAB],
                           db_r[0:1, oc * 128:(oc + 1) * 128],
                           mwrow[e][0:1, s0:s1_], False, True,
                           waits=[dbr_tok, mwrow_toks[e]], inc=True)
                at = P.vec("tensor_add", acc[oc][:, s0:s1_],
                           acc[oc][:, s0:s1_], psd[0:128, 0:SLAB],
                           waits=[mmd, acc_toks[oc]])
                poolD.set_last_use(idxd, at)
                acc_toks[oc] = at
        if e + 1 < E:
            dn_next = issue_dn(e + 1, [P.now("tensor")])
    P.barrier()
    lb.pop_scope()


def prep_l2(inputs, core):
    f = lambda a: np.ascontiguousarray(np.asarray(a, dtype=np.float32))
    d = {}
    d["c_ident"] = np.eye(128, dtype=np.float32)
    s_ = np.arange(128)
    d["c_triu"] = (s_[:, None] < s_[None, :]).astype(np.float32)
    d["c_iota128"] = np.arange(128, dtype=np.float32)
    d["c_iotam1"] = np.arange(-1, 127, dtype=np.float32)
    d["l2_tmk"] = f(inputs["L2_attn_time_mix_k"]).reshape(D)
    d["l2_tmv"] = f(inputs["L2_attn_time_mix_v"]).reshape(D)
    d["l2_tmr"] = f(inputs["L2_attn_time_mix_r"]).reshape(D)
    d["l2_wkT"] = f(np.asarray(inputs["L2_attn_Wk"]).T)
    d["l2_wvT"] = f(np.asarray(inputs["L2_attn_Wv"]).T)
    d["l2_wrT"] = f(np.asarray(inputs["L2_attn_Wr"]).T)
    d["l2_woT"] = f(np.asarray(inputs["L2_attn_Wo"]).T)
    d["l2_td"] = f(inputs["L2_attn_time_decay"])
    d["l2_tf"] = f(inputs["L2_attn_time_first"])
    gup = np.asarray(inputs["L2_ffn_gate_up_proj"], dtype=np.float32)
    d["l2_gate"] = np.ascontiguousarray(gup[:, :, 0::2]).reshape(8 * D, D)
    d["l2_up"] = np.ascontiguousarray(gup[:, :, 1::2]).reshape(8 * D, D)
    gub = np.asarray(inputs["L2_ffn_gate_up_proj_bias"], dtype=np.float32)
    d["l2_gb"] = np.ascontiguousarray(gub[:, 0::2])
    d["l2_ub"] = np.ascontiguousarray(gub[:, 1::2])
    d["l2_down"] = f(inputs["L2_ffn_down_proj"]).reshape(8 * D, D)
    d["l2_db"] = f(inputs["L2_ffn_down_proj_bias"])
    if core == 0:
        d["l2_ridx"] = np.ascontiguousarray(
            np.asarray(inputs["L2_ffn_router_indices"], dtype=np.int32))
        d["l2_rw"] = f(inputs["L2_ffn_routing_weights"])
    else:
        d["l2_ridx"] = np.zeros((T, 2), np.int32)
        d["l2_rw"] = np.zeros((T, 8), np.float32)
    return d



DS = 16
DI = 2048


def emit_l3(lb, x3, xout):
    P = lb.P
    final_d = lb.dram_tmp("l3_final", (DI, T), BF16)
    ones512 = lb.sb("ones512", (1, SLAB), BF16)
    tok_o512 = P.vec("memset", ones512[:], 1.0)
    onecol = lb.sb("onecol", (128, 1), F32)
    tok_onecol = P.vec("memset", onecol[:], 1.0)

    xnb = [lb.sb("xnb", (128, T), BF16) for _ in range(NDT)]

    lb.push_scope()
    xbuf = [lb.sb("xb_l3", (128, T), F32) for _ in range(NDT)]
    x_toks = lb.load_x_fm(x3, xbuf)
    ln_tmp = lb.sb("ln_tmp", (128, SLAB), F32)
    ln_prev = [None]

    def ln_wb(i, s0, s1, psA, psB, bt):
        ta = P.vec("tensor_mul", ln_tmp[:], xbuf[i][:, s0:s1],
                   psA[:, 0:SLAB], waits=[x_toks[i], bt, ln_prev[0]])
        tb = P.vec("tensor_add", xnb[i][:, s0:s1], ln_tmp[:],
                   psB[:, 0:SLAB], waits=[ta])
        ln_prev[0] = tb
        return tb

    xn_toks = lb.norm_fm(xbuf, x_toks, "ln", ln_wb)
    lb.pop_scope()

    lb.push_scope()
    nw_col, nw_tok = lb.load_col("l3_normw", D)
    nb_col, nb_tok = lb.load_col("l3_normb", D)
    xn2b = [lb.sb("xn2b", (128, T), BF16) for _ in range(NDT)]
    lb.push_scope()
    l2tmp = lb.sb("l2tmp", (128, SLAB), F32)
    l2prev = [None]

    def ln2_wb(i, s0, s1, psA, psB, bt):
        ta = P.vec("tensor_mul", l2tmp[:], xnb[i][:, s0:s1],
                   psA[:, 0:SLAB], waits=[xn_toks[i], bt, l2prev[0]])
        tb = P.vec("tensor_add", l2tmp[:], l2tmp[:], psB[:, 0:SLAB],
                   waits=[ta])
        tc = P.vec("tensor_scalar", l2tmp[:], l2tmp[:], nw_col[i][:], None,
                   op0=ALU.mult, waits=[tb, nw_tok[i]])
        td = P.vec("tensor_scalar", xn2b[i][:, s0:s1], l2tmp[:],
                   nb_col[i][:], None, op0=ALU.add, waits=[tc, nb_tok[i]])
        l2prev[0] = td
        return td

    xn2_toks = lb.norm_fm(xnb, xn_toks, "ln", ln2_wb)
    lb.pop_scope()

    xp0 = lb.sb("xp0", (1, T), BF16)
    B_b = lb.sb("B_b", (128, T), BF16)
    C_b = lb.sb("C_b", (128, T), BF16)
    dtw_r, dtw_tok = lb.load_row("l3_dtw", DI, out_dtype=BF16)
    dtb_r, dtb_tok = lb.load_row("l3_dtb", DI, out_dtype=BF16)
    acolT_t, acolT_tok = lb.load_w("l3_acolT", (128, 256), out_dtype=F32)
    acolT, acolT_tok = acolT_t[0], acolT_tok[0]
    dsk_t, dsk_tok = lb.load_col("l3_dsk", DI)
    bca_t, bca_tok = lb.load_w("c_bcall", (128, 16 * 128), BF16)
    BcAll, bca_tok = bca_t[0], bca_tok[0]
    sel_t, sel_tok = lb.load_w("c_selall", (128, 16 * 128), BF16)
    SelAll, sel_tok = sel_t[0], sel_tok[0]
    xc_d = lb.dram_tmp("l3_xc", (DI, T), BF16)

    lb.push_scope()
    xi = [lb.sb("xi", (128, T), BF16) for _ in range(DI // 128)]
    xi_toks = [None] * (DI // 128)
    lb.push_scope()
    inpT_a, inpT_tok = lb.load_w("l3_inpT", (D, 2 * DI), col0=0, col1=DI)

    def evac_xi(oc, s0, s1, ps, mm_tok):
        tok = P.vec("tensor_copy", xi[oc][:, s0:s1], ps[0:128, 0:s1 - s0],
                    waits=[mm_tok])
        xi_toks[oc] = tok
        return tok

    lb.proj(inpT_a, inpT_tok, xn2b, xn2_toks, DI, evac_xi)
    lb.pop_scope()

    cw_t, cw_tok = lb.load_w("l3_convw", (DI, 4), out_dtype=F32)
    cb_t, cb_tok = lb.load_col("l3_convb", DI)
    xpB_t, xpB_tok = lb.load_w("l3_xpB", (DI, DS))
    xpC_t, xpC_tok = lb.load_w("l3_xpC", (DI, DS))
    xpd_t, xpd_tok = lb.load_w("l3_xpd", (DI, 1))
    bcn_t, bcn_tok = lb.load_w("c_bcn", (16, 128), BF16, rowsplit=16)
    BcN, bcn_tok = bcn_t[0], bcn_tok[0]

    xc = xi
    xc_toks = [None] * (DI // 128)
    lb.push_scope()
    cacc = lb.sb("cacc", (128, T), F32)
    cprev = None
    for i in range(DI // 128):
        a1 = P.vec("tensor_scalar", cacc[:], xi[i][:],
                   cw_t[i][:, 3:4], None, op0=ALU.mult,
                   waits=[xi_toks[i], cw_tok[i], cprev])
        a2 = P.vec("scalar_tensor_tensor", cacc[:, 1:T], xi[i][:, 0:T - 1],
                   cw_t[i][:, 2:3], cacc[:, 1:T], op0=ALU.mult, op1=ALU.add,
                   waits=[a1])
        a3 = P.vec("scalar_tensor_tensor", cacc[:, 2:T], xi[i][:, 0:T - 2],
                   cw_t[i][:, 1:2], cacc[:, 2:T], op0=ALU.mult, op1=ALU.add,
                   waits=[a2])
        a4 = P.vec("scalar_tensor_tensor", cacc[:, 3:T], xi[i][:, 0:T - 3],
                   cw_t[i][:, 0:1], cacc[:, 3:T], op0=ALU.mult, op1=ALU.add,
                   waits=[a3])
        a5 = P.act(xc[i][:], cacc[:], AF.Silu, bias=cb_t[i][:],
                   waits=[a4, cb_tok[i]])
        xc_toks[i] = a5
        cprev = a5
        P.dma("sync", xc_d[i * 128:(i + 1) * 128, :], xc[i][:],
              waits=[a5])
    lb.pop_scope()

    BpT = lb.sb("BpT", (DS, T), BF16)
    CpT = lb.sb("CpT", (DS, T), BF16)
    bp_tok = cp_tok = x0_tok = None
    for s in range(NSLAB):
        s0, s1 = s * SLAB, (s + 1) * SLAB
        for (wt, wtok, dst, prev_t, which) in (
                (xpB_t, xpB_tok, BpT, bp_tok, "b"),
                (xpC_t, xpC_tok, CpT, cp_tok, "c"),
                (xpd_t, xpd_tok, xp0, x0_tok, "d")):
            ps, war, idx = lb.pg.acquire()
            mmx = None
            nk = DI // 128
            rows = DS if which != "d" else 1
            for kc in range(nk):
                mmx = P.mm(ps[0:rows, 0:SLAB], wt[kc][:],
                           xc[kc][:, s0:s1], kc == 0, kc == nk - 1,
                           waits=[wtok[kc], xc_toks[kc],
                                  war if kc == 0 else None],
                           inc=(kc == nk - 1))
            cv = P.vec("tensor_copy", dst[0:rows, s0:s1],
                       ps[0:rows, 0:SLAB], waits=[mmx, prev_t])
            lb.pg.set_last_use(idx, cv)
            if which == "b":
                bp_tok = cv
            elif which == "c":
                cp_tok = cv
            else:
                x0_tok = cv

    bb_tok = cb2_tok = None
    for s in range(NSLAB):
        s0, s1 = s * SLAB, (s + 1) * SLAB
        ps, war, idx = lb.pg.acquire()
        m1 = P.mm(ps[0:128, 0:SLAB], BcN[:], BpT[:, s0:s1], True, True,
                  waits=[bcn_tok, bp_tok, war], inc=True)
        bb_tok = P.vec("tensor_copy", B_b[:, s0:s1], ps[0:128, 0:SLAB],
                       waits=[m1, bb_tok])
        lb.pg.set_last_use(idx, bb_tok)
        ps, war, idx = lb.pg.acquire()
        m2 = P.mm(ps[0:128, 0:SLAB], BcN[:], CpT[:, s0:s1], True, True,
                  waits=[bcn_tok, cp_tok, war], inc=True)
        cb2_tok = P.vec("tensor_copy", C_b[:, s0:s1], ps[0:128, 0:SLAB],
                        waits=[m2, cb2_tok])
        lb.pg.set_last_use(idx, cb2_tok)
    lb.pop_scope()

    NTILE = DI // 128
    xcb = [lb.sb("xcb", (128, T), BF16) for _ in range(2)]
    xcb_toks = [None, None]
    xc_lu = [None, None]
    xcb_toks[0] = P.dma("sync", xcb[0][:], xc_d[0:128, :])
    final = [lb.sb("final_c", (128, T), BF16) for _ in range(2)]
    final_toks = [None, None]
    delta_i = [lb.sb("delta_i", (128, T), BF16) for _ in range(2)]
    dx_i = [lb.sb("dx_i", (128, T), BF16) for _ in range(2)]
    sp1 = [lb.sb("sp1", (128, SLAB), F32) for _ in range(2)]
    sp2 = [lb.sb("sp2", (128, SLAB), F32) for _ in range(2)]
    dA_buf = [lb.sb("dA_buf", (128, T // 2), F32) for _ in range(2)]
    dB_buf = [lb.sb("dB_buf", (128, T // 2), F32) for _ in range(2)]
    h_buf = [lb.sb("h_buf", (128, T), BF16) for _ in range(2)]
    hC_buf = [lb.sb("hC_buf", (128, T), BF16) for _ in range(2)]
    zb = [lb.sb("zb", (128, T), BF16) for _ in range(2)]
    zw = [[lb.sb("zw", (128, 128), BF16) for _ in range(NDT)]
          for _ in range(2)]
    inp_ap = lb.dram_in("l3_inpT", (D, 2 * DI))
    P.barrier()
    poolY = PsumPool(None, None, "yY", 0, tiles=lb.pg.tiles[0:4])
    poolB = PsumPool(None, None, "yB", 0, tiles=lb.pg.tiles[4:7])
    scan_done = [None, None]
    hct_war = [[None, None], [None, None]]
    sel_war = [None, None]
    sel_fin = [None] * NSLAB
    y_war = [None, None]
    delta_war = [[], []]
    dx_war = [[], []]
    zw_war = [None, None]
    dslab_toks = [[None] * NSLAB for _ in range(NTILE)]
    dxt_toks = [[None, None] for _ in range(NTILE)]
    zw_toks_d = [None] * NTILE
    dprev_d = [None] * NTILE

    def emit_zw_load(i):
        par = i % 2
        toks = []
        for kc in range(NDT):
            toks.append(P.dma(
                "gpsimd", zw[par][kc][:],
                inp_ap[kc * 128:(kc + 1) * 128,
                       DI + i * 128:DI + (i + 1) * 128],
                waits=[zw_war[par]]))
        zw_toks_d[i] = toks

    def emit_delta_slab(i, s):
        par = i % 2
        dprev = dprev_d[i]
        s0, s1 = s * SLAB, (s + 1) * SLAB
        ps, war, idx = poolB.acquire()
        P.mm(ps[0:128, 0:SLAB], dtw_r[0:1, i * 128:(i + 1) * 128],
             xp0[0:1, s0:s1], True, False,
             waits=[dtw_tok, x0_tok, war])
        mmd = P.mm(ps[0:128, 0:SLAB], dtb_r[0:1, i * 128:(i + 1) * 128],
                   ones512[0:1, :], False, True,
                   waits=[dtb_tok, tok_o512], inc=True)
        s1t = P.act(sp1[par][:], ps[0:128, 0:SLAB], AF.Abs,
                    waits=[mmd, dprev, y_war[par]])
        s2t = P.act(sp1[par][:], sp1[par][:], AF.Exp, scale=-1.0,
                    waits=[s1t])
        s3t = P.act(sp1[par][:], sp1[par][:], AF.Ln, bias=onecol[:],
                    waits=[s2t, tok_onecol])
        s4t = P.vec("tensor_single_scalar", sp2[par][:], ps[0:128, 0:SLAB],
                    0.0, op=ALU.max, waits=[mmd, s1t])
        s5t = P.vec("tensor_add", delta_i[par][:, s0:s1], sp1[par][:],
                    sp2[par][:],
                    waits=[s3t, s4t, dprev] + list(delta_war[par]))
        poolB.set_last_use(idx, s4t)
        dslab_toks[i][s] = s5t
        dprev_d[i] = s5t

    def emit_dxt_half(i, half):
        par = i % 2
        h0, h1 = half * (T // 2), (half + 1) * (T // 2)
        dxt_toks[i][half] = P.gp(
            "tensor_mul", dx_i[par][:, h0:h1], delta_i[par][:, h0:h1],
            xcb[par][:, h0:h1],
            waits=[dslab_toks[i][2 * half + 1], xcb_toks[par]]
            + list(dx_war[par]))

    emit_zw_load(0)
    for s in range(NSLAB):
        emit_delta_slab(0, s)
    emit_dxt_half(0, 0)
    emit_dxt_half(0, 1)

    for i in range(NTILE):
        par = i % 2
        if i + 1 < NTILE:
            xcb_toks[(i + 1) % 2] = P.dma(
                "sync", xcb[(i + 1) % 2][:],
                xc_d[(i + 1) * 128:(i + 2) * 128, :],
                waits=[xc_lu[(i + 1) % 2]])
        yps = []
        for s in range(NSLAB):
            ps, war, idx = poolY.acquire()
            yps.append((ps, war, idx))

        def emit_sel(jj, pp, h0t, h1t, stop, yps=yps):
            for s in range(NSLAB):
                s0, s1 = s * SLAB, (s + 1) * SLAB
                ps, war, idx = yps[s]
                smt = P.mm(ps[0:128, 0:SLAB],
                           SelAll[:, jj * 128:(jj + 1) * 128],
                           hC_buf[pp][:, s0:s1], jj == 0, stop,
                           waits=[sel_tok, h0t if s < 2 else h1t,
                                  war if jj == 0 else None],
                           inc=True)
                sel_fin[s] = smt
            sel_war[pp] = sel_fin[NSLAB - 1]

        sel_pend = None
        last_mm1 = last_mm2 = None
        for j in range(16):
            p = j % 2
            dA, dB, hb_ = dA_buf[p], dB_buf[p], h_buf[p]
            hC = hC_buf[p]
            da_tok = db_tok = None
            sc2 = None
            hts = []
            war_scan = scan_done[p]
            for half in range(2):
                h0 = half * (T // 2)
                for sl in range(2):
                    s0 = h0 + sl * SLAB
                    s1 = s0 + SLAB
                    b0, b1 = sl * SLAB, (sl + 1) * SLAB
                    ps1, war1, idx1 = poolB.acquire()
                    mm1 = P.mm(ps1[0:128, 0:SLAB],
                               BcAll[:, j * 128:(j + 1) * 128],
                               delta_i[par][:, s0:s1], True, True,
                               waits=[bca_tok,
                                      dslab_toks[i][half * 2 + sl],
                                      war1], inc=True)
                    last_mm1 = mm1
                    da_tok = P.act(dA[:, b0:b1], ps1[0:128, 0:SLAB],
                                   AF.Exp,
                                   scale=acolT[:,
                                               i * 16 + j:i * 16 + j + 1],
                                   waits=[mm1, acolT_tok, war_scan,
                                          da_tok])
                    poolB.set_last_use(idx1, da_tok)
                    ps2, war2, idx2 = poolB.acquire()
                    mm2 = P.mm(ps2[0:128, 0:SLAB],
                               BcAll[:, j * 128:(j + 1) * 128],
                               dx_i[par][:, s0:s1], True, True,
                               waits=[bca_tok, dxt_toks[i][half], war2],
                               inc=True)
                    last_mm2 = mm2
                    db_tok = P.vec("tensor_mul", dB[:, b0:b1],
                                   ps2[0:128, 0:SLAB], B_b[:, s0:s1],
                                   waits=[mm2, bb_tok, war_scan,
                                          db_tok])
                    poolB.set_last_use(idx2, db_tok)
                scan_f = P.vec
                hct_f = P.gp
                if half == 0:
                    sc2 = scan_f("tensor_tensor_scan", hb_[:, 0:T // 2],
                                 dA[:], dB[:], 0.0,
                                 op0=ALU.mult, op1=ALU.add,
                                 waits=[da_tok, db_tok, hct_war[p][0]])
                else:
                    sc2 = scan_f("tensor_tensor_scan", hb_[:, T // 2:T],
                                 dA[:], dB[:],
                                 hb_[:, T // 2 - 1:T // 2],
                                 op0=ALU.mult, op1=ALU.add,
                                 waits=[da_tok, db_tok, sc2,
                                        hct_war[p][1]])
                war_scan = sc2
                hw0, hw1 = half * (T // 2), (half + 1) * (T // 2)
                ht = hct_f("tensor_mul", hC[:, hw0:hw1], hb_[:, hw0:hw1],
                           C_b[:, hw0:hw1],
                           waits=[sc2, cb2_tok, sel_war[p]])
                hct_war[p][half] = ht
                hts.append(ht)
            scan_done[p] = sc2
            if i + 1 < NTILE:
                if j == 3:
                    emit_zw_load(i + 1)
                elif j in (5, 7, 9, 11):
                    emit_delta_slab(i + 1, (j - 5) // 2)
                elif j == 10:
                    emit_dxt_half(i + 1, 0)
                elif j == 12:
                    emit_dxt_half(i + 1, 1)
            if sel_pend is not None:
                emit_sel(*sel_pend, stop=False)
            sel_pend = (j, p, hts[0], hts[1])
        delta_war[par] = [last_mm1] + dxt_toks[i]
        dx_war[par] = [last_mm2]
        zt_toks = []
        mmz = None
        for s in range(NSLAB):
            s0, s1 = s * SLAB, (s + 1) * SLAB
            psz, warz, idxz = poolB.acquire()
            for kc in range(NDT):
                mmz = P.mm(psz[0:128, 0:SLAB], zw[par][kc][:],
                           xn2b[kc][:, s0:s1], kc == 0, kc == NDT - 1,
                           waits=[zw_toks_d[i][kc], xn2_toks[kc],
                                  warz if kc == 0 else None],
                           inc=(kc == NDT - 1))
            zt = P.act(zb[par][:, s0:s1], psz[0:128, 0:SLAB], AF.Silu,
                       waits=[mmz, y_war[par]])
            poolB.set_last_use(idxz, zt)
            zt_toks.append(zt)
        zw_war[par] = mmz
        emit_sel(*sel_pend, stop=True)
        sel_pend = None
        fprev = None
        for s in range(NSLAB):
            s0, s1 = s * SLAB, (s + 1) * SLAB
            ps, war, idx = yps[s]
            ytok = P.vec("scalar_tensor_tensor", sp1[par][:],
                         xcb[par][:, s0:s1],
                         dsk_t[i][:], ps[0:128, 0:SLAB],
                         op0=ALU.mult, op1=ALU.add,
                         waits=[sel_fin[s], dsk_tok[i],
                                xcb_toks[par], fprev])
            poolY.set_last_use(idx, ytok)
            ft = P.vec("tensor_mul", final[par][:, s0:s1], sp1[par][:],
                       zb[par][:, s0:s1],
                       waits=[ytok, zt_toks[s], final_toks[par]])
            fprev = ft
        final_toks[par] = P.dma(
            "sync", final_d[i * 128:(i + 1) * 128, :], final[par][:],
            waits=[fprev])
        y_war[par] = fprev
        xc_lu[par] = fprev
    P.barrier()
    lb.pop_scope()

    lb.push_scope()
    acc = [lb.sb("acc_l3", (128, T), F32) for _ in range(NDT)]
    acc_toks = lb.load_x_fm(x3, acc)
    for i in range(NDT):
        acc_toks[i] = P.vec("tensor_add", acc[i][:], acc[i][:], xnb[i][:],
                            waits=[acc_toks[i], xn_toks[i]])
    lb.push_scope()
    finalf = [lb.sb("finalf", (128, T), BF16) for _ in range(DI // 128)]
    finalf_toks = lb.load_x_fm(final_d, finalf)
    outT, outT_tok = lb.load_w("l3_outT", (DI, D))

    def evac_out3(oc, s0, s1, ps, mm_tok):
        tok = P.vec("tensor_add", acc[oc][:, s0:s1], acc[oc][:, s0:s1],
                    ps[0:128, 0:s1 - s0], waits=[mm_tok, acc_toks[oc]])
        acc_toks[oc] = tok
        return tok

    lb.proj(outT, outT_tok, finalf, finalf_toks, D, evac_out3)
    lb.pop_scope()

    emit_l3_moe(lb, xnb, xn_toks, acc, acc_toks)

    lb.push_scope()
    obuf = [lb.sb("obuf_o", (128, D), F32) for _ in range(2)]
    ob_tok = [None, None]
    for tt in range(T // 128):
        buf = obuf[tt % 2]
        ctoks = []
        for i in range(NDT):
            ps, war, idx = lb.pg.acquire()
            mt = P.tp(ps[:, 0:128], acc[i][:, tt * 128:(tt + 1) * 128],
                      lb.identf[:],
                      waits=[acc_toks[i], lb.tok_identf, war], inc=True)
            ct = P.vec("tensor_copy", buf[:, i * 128:(i + 1) * 128],
                       ps[:, 0:128], waits=[mt, ob_tok[tt % 2]])
            lb.pg.set_last_use(idx, ct)
            ctoks.append(ct)
        st = P.dma("sync", xout[tt * 128:(tt + 1) * 128, :], buf[:],
                   waits=ctoks)
        ob_tok[tt % 2] = st
        lb.out_dma_toks.append(st)
    lb.pop_scope()


def emit_l3_moe(lb, xnb, xn_toks, acc, acc_toks):
    P = lb.P
    E = 8
    lb.push_scope()
    wrow = [lb.sb("wrow3", (1, T), BF16) for _ in range(E + 1)]
    wrow_toks = [None] * (E + 1)
    lb.push_scope()
    wr_t, wr_tok = lb.load_w("l3_router", (D, E))
    sgw_t, sgw_tok = lb.load_w("l3_sgw", (D, 1))
    logits_T = lb.sb("logits3", (E, T), F32)
    lg_toks = []
    for s in range(NSLAB):
        s0, s1 = s * SLAB, (s + 1) * SLAB
        ps, war, idx = lb.pg.acquire()
        mmr = None
        for kc in range(NDT):
            mmr = P.mm(ps[0:E, 0:SLAB], wr_t[kc][:], xnb[kc][:, s0:s1],
                       kc == 0, kc == NDT - 1,
                       waits=[wr_tok[kc], xn_toks[kc],
                              war if kc == 0 else None],
                       inc=(kc == NDT - 1))
        cv = P.vec("tensor_copy", logits_T[0:E, s0:s1], ps[0:E, 0:SLAB],
                   waits=[mmr] + lg_toks[-1:])
        lb.pg.set_last_use(idx, cv)
        lg_toks.append(cv)
        ps2, war2, idx2 = lb.pg.acquire()
        mms = None
        for kc in range(NDT):
            mms = P.mm(ps2[0:1, 0:SLAB], sgw_t[kc][:], xnb[kc][:, s0:s1],
                       kc == 0, kc == NDT - 1,
                       waits=[sgw_tok[kc], xn_toks[kc],
                              war2 if kc == 0 else None],
                       inc=(kc == NDT - 1))
        sgt = P.act(wrow[E][0:1, s0:s1], ps2[0:1, 0:SLAB], AF.Sigmoid,
                    waits=[mms, wrow_toks[E]])
        lb.pg.set_last_use(idx2, sgt)
        wrow_toks[E] = sgt
    ptile = lb.sb("ptile3", (128, E), F32)
    junk = lb.sb("junk3", (128, E), F32)
    rmax = lb.sb("rmax3", (128, 1), F32)
    accgt = lb.sb("accgt3", (128, 1), F32)
    acceq = lb.sb("acceq3", (128, 1), F32)
    sel = lb.sb("sel3", (128, E), F32)
    for tt in range(T // 128):
        t0 = tt * 128
        ps, war, idx = lb.pg.acquire()
        mt = P.tp(ps[0:128, 0:E], logits_T[0:E, t0:t0 + 128],
                  lb.identf[0:E, 0:E],
                  waits=lg_toks + [lb.tok_identf, war], inc=True)
        cp = P.vec("tensor_copy", ptile[:], ps[0:128, 0:E], waits=[mt])
        lb.pg.set_last_use(idx, cp)
        r1 = P.vec("reduce_max", rmax[:, :, None], ptile[:, None, :],
                   axis=AX.X, waits=[cp])
        r2 = P.vec("tensor_scalar_mul", rmax[:], rmax[:], -1.0, waits=[r1])
        r3 = P.act(ptile[:], ptile[:], AF.Exp, bias=rmax[:], waits=[r2])
        r4 = P.vec("reduce_sum", rmax[:, :, None], ptile[:, None, :],
                   axis=AX.X, waits=[r3])
        r5 = P.vec("reciprocal", rmax[:], rmax[:], waits=[r4])
        r6 = P.vec("tensor_scalar", ptile[:], ptile[:], rmax[:], None,
                   op0=ALU.mult, waits=[r5])
        prev = r6
        for e in range(E):
            t1 = P.vec("tensor_scalar", junk[:], ptile[:],
                       ptile[:, e:e + 1], 0.0, op0=ALU.is_gt,
                       op1=ALU.add, accum_out=accgt[:], waits=[prev])
            if e > 0:
                t2 = P.vec("tensor_scalar", junk[:, 0:e], ptile[:, 0:e],
                           ptile[:, e:e + 1], 0.0, op0=ALU.is_equal,
                           op1=ALU.add, accum_out=acceq[:], waits=[t1])
                t3 = P.vec("tensor_add", accgt[:], accgt[:], acceq[:],
                           waits=[t2])
            else:
                t3 = t1
            t4 = P.vec("tensor_single_scalar", sel[:, e:e + 1], accgt[:],
                       2.0, op=ALU.is_lt, waits=[t3])
            prev = t4
        t5 = P.vec("tensor_mul", sel[:], sel[:], ptile[:], waits=[prev])
        prev = t5
        for e in range(E):
            ps2, war2, idx2 = lb.pg.acquire()
            mt2 = P.tp(ps2[0:1, 0:128], sel[:, e:e + 1], lb.identf[:],
                       waits=[prev, lb.tok_identf, war2], inc=True)
            cv2 = P.vec("tensor_copy", wrow[e][0:1, t0:t0 + 128],
                        ps2[0:1, 0:128], waits=[mt2, wrow_toks[e]])
            lb.pg.set_last_use(idx2, cv2)
            wrow_toks[e] = cv2
            prev = cv2
    lb.pop_scope()

    w1_ap = lb.dram_in("l3_w1", (E * D, D))
    w2_ap = lb.dram_in("l3_w2", (E * D, D))
    sg_ap = lb.dram_in("l3_sgate", (D, D))
    su_ap = lb.dram_in("l3_sup", (D, D))
    sd_ap = lb.dram_in("l3_sdown", (D, D))
    gbuf = [lb.sb("gw3", (128, D), BF16) for _ in range(NDT)]
    ubuf = [lb.sb("uw3", (128, D), BF16) for _ in range(NDT)]
    dbuf = [lb.sb("dw3", (128, D), BF16) for _ in range(NDT)]
    hmid = [lb.sb("hmid3", (128, SLAB), BF16) for _ in range(NDT)]
    sgb = lb.sb("sgb3", (128, SLAB), F32)
    P.barrier()
    poolGU = PsumPool(None, None, "3GU", 0, tiles=lb.pg.tiles[0:4])
    poolW = PsumPool(None, None, "3W", 0, tiles=lb.pg.tiles[4:5])
    poolD = PsumPool(None, None, "3D", 0, tiles=lb.pg.tiles[5:7])

    def issue_gu(ei, wait):
        gt, ut = [], []
        for kc in range(NDT):
            if ei == E:
                gt.append(P.dma("gpsimd", gbuf[kc][:],
                                sg_ap[kc * 128:(kc + 1) * 128, :],
                                waits=wait))
                ut.append(P.dma("gpsimd", ubuf[kc][:],
                                su_ap[kc * 128:(kc + 1) * 128, :],
                                waits=wait))
            else:
                r0 = ei * D + kc * 128
                gt.append(P.dma("gpsimd", gbuf[kc][:],
                                w1_ap[r0:r0 + 128, :], waits=wait))
        return gt, ut

    def issue_d(ei, wait):
        if ei == E:
            return [P.dma("gpsimd", dbuf[kc][:],
                          sd_ap[kc * 128:(kc + 1) * 128, :], waits=wait)
                    for kc in range(NDT)]
        return [P.dma("gpsimd", dbuf[kc][:],
                      w2_ap[ei * D + kc * 128:ei * D + (kc + 1) * 128, :],
                      waits=wait) for kc in range(NDT)]

    gu_next = issue_gu(0, [])
    d_next = issue_d(0, [])
    for ei in range(E + 1):
        shared = ei == E
        g_toks, u_toks = gu_next
        d_toks = d_next
        wre = ei
        for s in range(NSLAB):
            s0, s1_ = s * SLAB, (s + 1) * SLAB
            psw, warw, idxw = poolW.acquire()
            bw = P.mm(psw[0:128, 0:SLAB], lb.onesrow_b[:],
                      wrow[wre][0:1, s0:s1_], True, True,
                      waits=[wrow_toks[wre], lb.tok_ones, warw], inc=True)
            hm_toks = []
            for ic in range(NDT):
                psg, warg, idxg = poolGU.acquire()
                mmg = None
                for kc in range(NDT):
                    mmg = P.mm(psg[0:128, 0:SLAB],
                               gbuf[kc][:, ic * 128:(ic + 1) * 128],
                               xnb[kc][:, s0:s1_], kc == 0, kc == NDT - 1,
                               waits=[g_toks[kc], xn_toks[kc],
                                      warg if kc == 0 else None],
                               inc=(kc == NDT - 1))
                st = P.act(sgb[:], psg[0:128, 0:SLAB], AF.Silu,
                           waits=[mmg, hm_toks[-1] if hm_toks else None])
                poolGU.set_last_use(idxg, st)
                if shared:
                    psu, waru, idxu = poolGU.acquire()
                    mmu = None
                    for kc in range(NDT):
                        mmu = P.mm(psu[0:128, 0:SLAB],
                                   ubuf[kc][:, ic * 128:(ic + 1) * 128],
                                   xnb[kc][:, s0:s1_], kc == 0,
                                   kc == NDT - 1,
                                   waits=[u_toks[kc], xn_toks[kc],
                                          waru if kc == 0 else None],
                                   inc=(kc == NDT - 1))
                    m1 = P.vec("tensor_mul", sgb[:], sgb[:],
                               psu[0:128, 0:SLAB], waits=[st, mmu])
                    poolGU.set_last_use(idxu, m1)
                    st = m1
                ht = P.vec("tensor_mul", hmid[ic][:], sgb[:],
                           psw[0:128, 0:SLAB], waits=[st, bw])
                hm_toks.append(ht)
            poolW.set_last_use(idxw, hm_toks[-1])
            if s == NSLAB - 1 and ei < E:
                gu_next = issue_gu(ei + 1, [mmu if shared else mmg])
            for oc in range(NDT):
                psd, ward, idxd = poolD.acquire()
                mmd = None
                for ic in range(NDT):
                    mmd = P.mm(psd[0:128, 0:SLAB],
                               dbuf[ic][:, oc * 128:(oc + 1) * 128],
                               hmid[ic][:], ic == 0, ic == NDT - 1,
                               waits=[d_toks[ic], hm_toks[ic],
                                      ward if ic == 0 else None],
                               inc=(ic == NDT - 1))
                at = P.vec("tensor_add", acc[oc][:, s0:s1_],
                           acc[oc][:, s0:s1_], psd[0:128, 0:SLAB],
                           waits=[mmd, acc_toks[oc]])
                poolD.set_last_use(idxd, at)
                acc_toks[oc] = at
        if ei < E:
            d_next = issue_d(ei + 1, [P.now("tensor")])
    P.barrier()
    lb.pop_scope()


def prep_l3(inputs):
    f = lambda a: np.ascontiguousarray(np.asarray(a, dtype=np.float32))
    d = {}
    d["c_ident"] = np.eye(128, dtype=np.float32)
    bc = np.zeros((128, 16 * 128), np.float32)
    sl = np.zeros((128, 16 * 128), np.float32)
    for j in range(16):
        for lane in range(128):
            dd = j * 8 + lane // 16
            bc[dd, j * 128 + lane] = 1.0
            sl[lane, j * 128 + dd] = 1.0
    d["c_bcall"] = bc
    d["c_selall"] = sl
    bcn = np.zeros((16, 128), np.float32)
    for lane in range(128):
        bcn[lane % 16, lane] = 1.0
    d["c_bcn"] = bcn
    d["l3_normw"] = f(inputs["L3_attn_norm_w"])
    d["l3_normb"] = f(inputs["L3_attn_norm_b"])
    d["l3_inpT"] = f(np.asarray(inputs["L3_attn_in_proj_w"]).T)
    d["l3_convw"] = f(inputs["L3_attn_conv_w"]).reshape(DI, 4)
    d["l3_convb"] = f(inputs["L3_attn_conv_b"])
    xp = f(inputs["L3_attn_x_proj_w"])
    d["l3_xpd"] = np.ascontiguousarray(xp[0:1].T)
    d["l3_xpB"] = np.ascontiguousarray(xp[1:DS + 1].T)
    d["l3_xpC"] = np.ascontiguousarray(xp[DS + 1:].T)
    d["l3_dtw"] = f(inputs["L3_attn_dt_w"]).reshape(DI)
    d["l3_dtb"] = f(inputs["L3_attn_dt_b"])
    A = -np.exp(f(inputs["L3_attn_A_log"]))
    acolT = np.zeros((128, 256), np.float32)
    for i in range(16):
        for j in range(16):
            for lane in range(128):
                acolT[lane, i * 16 + j] = A[i * 128 + j * 8 + lane // 16,
                                            lane % 16]
    d["l3_acolT"] = acolT
    d["l3_dsk"] = f(inputs["L3_attn_D_skip"])
    d["l3_outT"] = f(np.asarray(inputs["L3_attn_out_proj_w"]).T)
    d["l3_router"] = f(inputs["L3_ffn_router_w"])
    d["l3_w1"] = f(inputs["L3_ffn_w1"]).reshape(8 * D, D)
    d["l3_w2"] = f(inputs["L3_ffn_w2"]).reshape(8 * D, D)
    sgu = f(inputs["L3_ffn_shared_gate_up"])
    d["l3_sgate"] = np.ascontiguousarray(sgu[:, 0:D])
    d["l3_sup"] = np.ascontiguousarray(sgu[:, D:2 * D])
    d["l3_sdown"] = f(inputs["L3_ffn_shared_down"])
    d["l3_sgw"] = f(inputs["L3_ffn_shared_gate_w"])
    return d

